# version 16
# baseline (speedup 1.0000x reference)
import numpy as np
import ml_dtypes
import concourse.bass as bass
import concourse.mybir as mybir
from concourse.bass_utils import run_bass_kernel_spmd
from contextlib import ExitStack

F32 = mybir.dt.float32
F32R = mybir.dt.float32r
BF16 = mybir.dt.bfloat16
AF = mybir.ActivationFunctionType
ALU = mybir.AluOpType
AX = mybir.AxisListType
BF = ml_dtypes.bfloat16

D = 1024
SEQ = 2048
TCH = SEQ // 128
DCH = D // 128
CST = 2052
XOFF = 2
NCORES = 8
LN_EPS = 1e-5


class _Rec:
    __slots__ = ("incs",)

    def __init__(self):
        self.incs = []

    def then_inc(self, sem, n):
        self.incs.append((sem, n))
        return self


class EngQ:
    def __init__(self, name):
        self.name = name
        self.ops = []

    def __getattr__(self, meth):
        if meth.startswith("_"):
            raise AttributeError(meth)

        def call(*a, **kw):
            rec = _Rec()
            self.ops.append((meth, a, kw, rec))
            return rec

        return call

    def replay(self, eng):
        for meth, a, kw, rec in self.ops:
            inst = getattr(eng, meth)(*a, **kw)
            for sem, n in rec.incs:
                inst.then_inc(sem, n)


class Sem:
    def __init__(self, handle, name):
        self.h = handle
        self.name = name
        self.n = 0

    def bump(self, k=1):
        self.n += k
        return self.n


class MK:
    def __init__(self, nc):
        self.nc = nc
        self.es = ExitStack()
        self._uid = 0
        self.t = EngQ("tensor")
        self.v = EngQ("vector")
        self.s = EngQ("scalar")
        self.g = EngQ("gpsimd")
        self.y = EngQ("sync")
        for name in ("sd", "sw", "sp", "sv", "sa", "so", "sq"):
            setattr(self, name, Sem(self.es.enter_context(nc.semaphore(name)), name))
        self._sems = {}
        self.epscol = None

    def sem(self, name):
        if name not in self._sems:
            self._sems[name] = Sem(self.es.enter_context(self.nc.semaphore(name)), name)
        return self._sems[name]

    def init_consts(self):
        self.epscol = self.sbuf("epscol", [128, 1], F32)
        self.v.memset(self.epscol[:], LN_EPS).then_inc(self.sv.h, 1)
        self.sv.bump()
        n = self.sv.n
        for q in (self.s, self.t, self.g, self.y):
            q.wait_ge(self.sv.h, n)
        return n

    def uid(self, name):
        self._uid += 1
        return f"{name}_u{self._uid}"

    def sbuf(self, name, shape, dtype, es=None):
        return (es or self.es).enter_context(self.nc.sbuf_tensor(self.uid(name), shape, dtype))

    def psum(self, name, shape, dtype=F32, es=None):
        return (es or self.es).enter_context(self.nc.psum_tensor(self.uid(name), shape, dtype))

    def audit_sems(self):
        totals = {}
        for q in (self.t, self.v, self.s, self.g, self.y):
            for meth, a, kw, rec in q.ops:
                for sem, n in rec.incs:
                    totals[id(sem)] = totals.get(id(sem), 0) + n
        sems = [getattr(self, n) for n in ("sd", "sw", "sp", "sv", "sa", "so", "sq")]
        sems += list(self._sems.values())
        ok = True
        for s in sems:
            emitted = totals.get(id(s.h), 0)
            if emitted != s.n:
                print(f"SEM MISMATCH {s.name}: emitted incs={emitted} python count={s.n}")
                ok = False
        assert ok, "semaphore count audit failed"

    def finish(self):
        self.audit_sems()
        nc = self.nc
        with nc.Block() as block:
            block.tensor(lambda eng: self.t.replay(eng))
            block.vector(lambda eng: self.v.replay(eng))
            block.scalar(lambda eng: self.s.replay(eng))
            block.gpsimd(lambda eng: self.g.replay(eng))
            block.sync(lambda eng: self.y.replay(eng))
        self.es.close()
        return nc


PHASE_MARKS = []


def _mark(mk, name):
    PHASE_MARKS.append((name, mk.sv.n, mk.sp.n, mk.sa.n))


def sb_ap(tensor, offset, dims):
    return bass.AP(tensor, offset, [[tensor.shape[1], 128]] + dims)


def host_constants():
    c = {}
    c["ident"] = np.eye(128, dtype=np.float32).astype(BF)
    inv = 1.0 / (10000.0 ** (np.arange(0, 128, 2, dtype=np.float64) / 128.0))
    ang = np.arange(SEQ, dtype=np.float64)[:, None] * inv[None, :]
    c["cos0"] = np.tile(np.cos(ang), (1, 4)).astype(np.float32)
    c["sin0"] = np.tile(np.sin(ang), (1, 4)).astype(np.float32)
    h = np.arange(8)
    s_h = np.log2(1.0 - 2.0 ** (-5.0 - h))
    t = np.arange(SEQ, dtype=np.float64)
    c["aq0"] = (2.0 ** (t[:, None] * s_h[None, :]) / np.sqrt(128.0)).astype(np.float32)
    c["bk0"] = (2.0 ** (-t[:, None] * s_h[None, :])).astype(np.float32)
    kk = np.arange(128)[:, None]
    qq = np.arange(128)[None, :]
    c["mask01"] = (qq >= kk).astype(np.float32).astype(BF)
    c["maskneg"] = np.where(qq.T < kk.T, 0.0, 0.0).astype(np.float32)
    c["maskneg"] = np.where(np.arange(128)[None, :] > np.arange(128)[:, None],
                            -1e30, 0.0).astype(np.float32).astype(BF)
    inv1 = 1.0 / (10000.0 ** (np.arange(0, 64, 2, dtype=np.float64) / 64.0))
    ang1 = np.arange(SEQ, dtype=np.float64)[:, None] * inv1[None, :]
    c["cos1"] = np.tile(np.cos(ang1), (1, 16)).astype(np.float32)
    c["sin1"] = np.tile(np.sin(ang1), (1, 16)).astype(np.float32)
    return c


def percol(v):
    return np.ascontiguousarray(np.asarray(v, dtype=np.float32).reshape(-1, 128).T)


def row(v):
    return np.ascontiguousarray(np.asarray(v, dtype=np.float32).reshape(1, -1))


def emit_ln_stats(mk, X, sq, st, x_tile=None):
    v, s = mk.v, mk.s
    sv, sa = mk.sv, mk.sa
    s.wait_ge(sa.h, sa.n)
    for tc in range(TCH):
        xt = X[:, tc * 1024:(tc + 1) * 1024]
        if x_tile is not None:
            s.wait_ge(sv.h, x_tile[tc])
        s.activation(sq[:], xt, AF.Copy, scale=1.0 / 1024.0,
                     accum_out=st[:, tc:tc + 1])
    for tc in range(TCH):
        xt = X[:, tc * 1024:(tc + 1) * 1024]
        aop = s.activation(sq[:], xt, AF.Square, scale=1.0 / 32.0,
                           accum_out=st[:, 16 + tc:17 + tc])
    aop.then_inc(sa.h, 1)
    na = sa.bump()
    v.wait_ge(sa.h, na)
    v.tensor_mul(st[:, 32:48], st[:, 0:16], st[:, 0:16]).then_inc(sv.h, 1)
    nv = sv.bump()
    mk.g.wait_ge(sv.h, nv)
    mk.g.tensor_tensor(st[:, 32:48], st[:, 32:48], st[:, 16:32],
                       ALU.subtract).then_inc(mk.sq.h, 1)
    nq = mk.sq.bump()
    s.wait_ge(mk.sq.h, nq)
    s.activation(st[:, 48:64], st[:, 32:48], AF.Ln, scale=-1.0,
                 bias=mk.epscol[:, 0:1]).then_inc(sa.h, 1)
    na = sa.bump()
    v.wait_ge(sa.h, na)
    v.tensor_copy(st[:, 48:64], st[:, 48:64]).then_inc(sv.h, 1)
    nv = sv.bump()
    s.wait_ge(sv.h, nv)
    s.activation(st[:, 32:48], st[:, 48:64], AF.Exp, scale=-0.5).then_inc(sa.h, 1)
    return sa.bump()


def emit_ln_transpose(mk, X, XnT, bufs, gamma_beta=None):
    t, v, s = mk.t, mk.v, mk.s
    sp, sv, sa = mk.sp, mk.sv, mk.sa
    xn = (bufs["xn0"], bufs["xn1"])
    pts = (bufs["ptA"], bufs["ptB"])
    sq, st, ident = bufs["sq"], bufs["st"], bufs["ident"]
    gscr = bufs.get("gscr")
    xnt_free = bufs.pop("xnt_free", None)
    x_ready = bufs.pop("x_ready", None)
    x_tile = bufs.pop("x_tile", None)
    if x_ready is not None and x_tile is None:
        s.wait_ge(x_ready[0], x_ready[1])
        v.wait_ge(x_ready[0], x_ready[1])
    na_rstd = emit_ln_stats(mk, X, sq, st, x_tile=x_tile)
    prev_tr_done = [None, None]
    prev_evac = [None, None]
    v.wait_ge(sa.h, na_rstd)
    for tc in range(TCH):
        xt = X[:, tc * 1024:(tc + 1) * 1024]
        mean_c, rstd_c = st[:, tc:tc + 1], st[:, 32 + tc:33 + tc]
        xbuf = xn[tc % 2]
        if prev_tr_done[0] is not None:
            v.wait_ge(sp.h, prev_tr_done[0])
        if gamma_beta is None:
            v_last = v.tensor_scalar(xbuf[:], xt, mean_c, rstd_c, ALU.subtract, ALU.mult)
        else:
            g_t, b_t = gamma_beta
            v.tensor_scalar(gscr[:], xt, mean_c, rstd_c, ALU.subtract, ALU.mult)
            v.tensor_mul(gscr[:], gscr[:], g_t[:])
            v_last = v.tensor_add(xbuf[:], gscr[:], b_t[:])
        v_last.then_inc(sv.h, 1)
        nv = sv.bump()
        pt = pts[tc % 2]
        t.wait_ge(sv.h, nv)
        if tc == 0 and xnt_free is not None:
            pass
        if prev_evac[tc % 2] is not None:
            t.wait_ge(sv.h, prev_evac[tc % 2])
        for c in range(DCH):
            tr = t.transpose(pt[:, c * 128:(c + 1) * 128],
                             xbuf[:, c * 128:(c + 1) * 128], ident[:])
        tr.then_inc(sp.h, 1)
        np_ = sp.bump()
        prev_tr_done[0] = np_
        v.wait_ge(sp.h, np_)
        if tc == 0 and xnt_free is not None:
            v.wait_ge(xnt_free[0].h, xnt_free[1])
        dst = bass.AP(XnT, XOFF + tc * 128,
                      [[XnT.shape[1], 128], [CST, DCH], [1, 128]])
        v.tensor_copy(dst, pt[:]).then_inc(sv.h, 1)
        nv = sv.bump()
        prev_evac[tc % 2] = nv
    return sv.n


def emit_l1ffn(mk, X, XnT, W, bufs, pes):
    t, v, s = mk.t, mk.v, mk.s
    sp, sv, sa, sw = mk.sp, mk.sv, mk.sa, mk.sw
    l1scope = ExitStack()
    kaT = mk.sbuf("kaT", [128, DCH * SEQ], BF16, l1scope)
    W["mur"] = mk.sbuf("mur", [128, 8], F32, l1scope)
    W["muk"] = mk.sbuf("muk", [128, 8], F32, l1scope)
    W["wr"] = mk.sbuf("wr", [128, DCH * 1024], BF16, l1scope)
    W["wv"] = mk.sbuf("wv", [128, DCH * 1024], BF16, l1scope)
    queue_l1ffn_weights(mk, W, W["dr"], deps=pes.get("sbuf_free"))
    v.wait_ge(W["n_mu"][0].h, W["n_mu"][1])
    with ExitStack() as ps:
        wk = mk.sbuf("wk", [128, DCH * 1024], BF16, ps)
        xkg = (mk.sbuf("xkg", [128, DCH * 512], BF16, ps),
               mk.sbuf("xkg", [128, DCH * 512], BF16, ps))
        tmpm = mk.sbuf("tmpm", [128, 512], BF16, ps)
        rks = (mk.sbuf("rk", [128, 512], BF16, ps),
               mk.sbuf("rk", [128, 512], BF16, ps))
        pks = (mk.psum("pk0", [128, 512], F32, ps), mk.psum("pk1", [128, 512], F32, ps))
        gdep = pes.get("sbuf_free")
        if gdep is not None:
            mk.g.wait_ge(*gdep)
        swk = mk.sem("l1f_wk")
        src2 = W["dr"]["wk_t"].rearrange("(cc p) d -> p cc d", p=128)
        dst2 = wk.ap().rearrange("p (cc d) -> p cc d", d=1024)
        mk.g.dma_start(dst2, src2).then_inc(swk.h, 16)
        n_wk = (swk, swk.bump(16))
        prev_grp = [None, None]
        prev_bank = [None, None]
        prev_rk = [None, None]
        t.wait_ge(n_wk[0].h, n_wk[1])
        if pes.get("psum_free") is not None:
            t.wait_ge(*pes["psum_free"])
        for tq in range(SEQ // 512):
            i = tq % 2
            if prev_grp[i] is not None:
                v.wait_ge(sp.h, prev_grp[i])
            for c in range(DCH):
                base = c * CST + XOFF + tq * 512
                xcur = sb_ap(XnT, base, [[1, 512]])
                xprv = sb_ap(XnT, base - 1, [[1, 512]])
                mu_c = W["muk"][:, c:c + 1]
                v.scalar_tensor_tensor(tmpm[:], xprv, mu_c, xprv, ALU.mult, ALU.subtract)
                v.scalar_tensor_tensor(xkg[i][:, c * 512:(c + 1) * 512], xcur, mu_c,
                                       tmpm[:], ALU.mult, ALU.subtract).then_inc(sv.h, 1)
                sv.bump()
            nv_mix = sv.n
            t.wait_ge(sv.h, nv_mix)
            for hc in range(DCH):
                j = hc % 2
                if prev_bank[j] is not None:
                    t.wait_ge(sa.h, prev_bank[j])
                for c in range(DCH):
                    mm = t.matmul(pks[j][:],
                                  wk[:, c * 1024 + hc * 128: c * 1024 + (hc + 1) * 128],
                                  xkg[i][:, c * 512:(c + 1) * 512],
                                  start=(c == 0), stop=(c == DCH - 1))
                mm.then_inc(sp.h, 1)
                npk = sp.bump()
                if hc == DCH - 1:
                    prev_grp[i] = npk
                s.wait_ge(sp.h, npk)
                s.activation(rks[j][:], pks[j][:], AF.Relu)
                s.activation(sb_ap(kaT, hc * SEQ + tq * 512, [[1, 512]]),
                             rks[j][:], AF.Square).then_inc(sa.h, 1)
                na = sa.bump()
                prev_bank[j] = na
        nv_k = sa.n
        np_k = sp.n
        pes["wk_free"] = (sp.h, np_k)
    xtl_l1 = []
    with ExitStack() as ps:
        xrs = (mk.sbuf("xrg", [128, DCH * 128], BF16, ps),
               mk.sbuf("xrg", [128, DCH * 128], BF16, ps))
        tmpm = mk.sbuf("tmpm", [128, 128], BF16, ps)
        rtoks = (mk.sbuf("rtok", [128, 1024], F32, ps),
                 mk.sbuf("rtok", [128, 1024], F32, ps))
        vr = mk.sbuf("vr", [128, 1024], F32, ps)
        prs = (mk.psum("pr0", [128, 1024], F32, ps), mk.psum("pr1", [128, 1024], F32, ps))
        pvs = (mk.psum("pv0", [128, 1024], F32, ps), mk.psum("pv1", [128, 1024], F32, ps))
        prev_r = [None, None]
        prev_x = [None, None]
        prev_xr = [None, None]
        v.wait_ge(sp.h, np_k)
        t.wait_ge(sa.h, nv_k)
        t.wait_ge(W["n_wv"][0].h, W["n_wv"][1])
        for tc in range(TCH):
            i = tc % 2
            if prev_xr[i] is not None:
                v.wait_ge(sp.h, prev_xr[i])
            for c in range(DCH):
                base = c * CST + XOFF + tc * 128
                xcur = sb_ap(XnT, base, [[1, 128]])
                xprv = sb_ap(XnT, base - 1, [[1, 128]])
                mu_c = W["mur"][:, c:c + 1]
                v.scalar_tensor_tensor(tmpm[:], xprv, mu_c, xprv, ALU.mult, ALU.subtract)
                v.scalar_tensor_tensor(xrs[i][:, c * 128:(c + 1) * 128], xcur, mu_c,
                                       tmpm[:], ALU.mult, ALU.subtract).then_inc(sv.h, 1)
                sv.bump()
            nv_mixr = sv.n
            t.wait_ge(sv.h, nv_mixr)
            if prev_r[i] is not None:
                t.wait_ge(sa.h, prev_r[i])
            if prev_x[i] is not None:
                t.wait_ge(sv.h, prev_x[i])
            for c in range(DCH):
                for dn in range(2):
                    mm = t.matmul(prs[i][:, dn * 512:(dn + 1) * 512],
                                  xrs[i][:, c * 128:(c + 1) * 128],
                                  W["wr"][:, c * 1024 + dn * 512: c * 1024 + (dn + 1) * 512],
                                  start=(c == 0), stop=(c == DCH - 1))
            mm.then_inc(sp.h, 1)
            npr = sp.bump()
            prev_xr[i] = npr
            for c in range(DCH):
                for dn in range(2):
                    mm = t.matmul(pvs[i][:, dn * 512:(dn + 1) * 512],
                                  sb_ap(kaT, c * SEQ + tc * 128, [[1, 128]]),
                                  W["wv"][:, c * 1024 + dn * 512: c * 1024 + (dn + 1) * 512],
                                  start=(c == 0), stop=(c == DCH - 1))
            mm.then_inc(sp.h, 1)
            npv = sp.bump()
            s.wait_ge(sp.h, npr)
            if prev_x[i] is not None:
                s.wait_ge(sv.h, prev_x[i])
            s.activation(rtoks[i][:], prs[i][:], AF.Sigmoid).then_inc(sa.h, 1)
            na = sa.bump()
            prev_r[i] = na
            xt = X[:, tc * 1024:(tc + 1) * 1024]
            v.wait_ge(sa.h, na)
            v.wait_ge(sp.h, npv)
            v.tensor_mul(vr[:], pvs[i][:], rtoks[i][:])
            v.scalar_tensor_tensor(xt, vr[:], 0.5, xt, ALU.mult,
                                   ALU.add).then_inc(sv.h, 1)
            nv = sv.bump()
            prev_x[i] = nv
            xtl_l1.append(nv)
    pes["x_tile"] = xtl_l1
    l1scope.close()
    pes["psum_free"] = (sv.h, sv.n)
    pes["sbuf_free"] = (sv.h, sv.n)
    return sv.n


def alloc_l1ffn(mk, es=None):
    return dict(), dict()


def queue_l1ffn_weights(mk, W, dr, deps=None):
    W["dr"] = dr
    g = mk.g
    sb = mk.sem("l1f_w")
    if deps is not None:
        g.wait_ge(*deps)
    g.dma_start(W["mur"][:], dr["mu_r_pc"]).then_inc(sb.h, 16)
    sb.bump(16)
    g.dma_start(W["muk"][:], dr["mu_k_pc"]).then_inc(sb.h, 16)
    sb.bump(16)
    for nm in ("wr", "wv"):
        src = dr[nm + "_t"].rearrange("(cc p) d -> p cc d", p=128)
        dst = W[nm].ap().rearrange("p (cc d) -> p cc d", d=1024)
        g.dma_start(dst, src).then_inc(sb.h, 16)
        sb.bump(16)
    W["n_mu"] = (sb, sb.n)
    W["n_wr"] = (sb, sb.n)
    W["n_wv"] = (sb, sb.n)


def alloc_ln(mk, es=None, gamma=False):
    xn = mk.sbuf("xn0", [128, 1024], BF16, es)
    bufs = dict(
        xn0=xn,
        xn1=xn,
        sq=mk.sbuf("sq", [128, 1024], F32, es),
        st=mk.sbuf("st", [128, 64], F32, es),
    )
    if gamma:
        bufs["gscr"] = mk.sbuf("gscr", [128, 1024], F32, es)
    return bufs


def emit_moe(mk, X, XnT, W, bufs, pes):
    t, v, s = mk.t, mk.v, mk.s
    sp, sv, sa, sw = mk.sp, mk.sv, mk.sa, mk.sw
    g = mk.g
    smb = mk.sem("moe_b")

    shared_scope = ExitStack()
    moescope = ExitStack()
    cw = mk.sbuf("cw", [128, TCH * 16], F32, moescope)
    bufs["dbg_cw"] = cw
    st2 = mk.sbuf("st2", [128, 8], F32, moescope)
    esb = mk.sbuf("esb", [128, 16], F32, moescope)
    msk = mk.sbuf("mskr", [128, 16], F32, moescope)
    msk2 = mk.sbuf("mskr2", [128, 16], F32, moescope)
    bufs["sgs"] = (mk.sbuf("sg0", [128, 512], BF16, moescope),
                   mk.sbuf("sg1", [128, 512], BF16, moescope))
    ws1 = mk.sbuf("ws1", [128, DCH * 2048], BF16, shared_scope)
    ws2 = mk.sbuf("ws2", [128, DCH * 1024], BF16, shared_scope)
    hsT = mk.sbuf("hsT", [128, DCH * SEQ], BF16, shared_scope)
    gdep = pes.get("sbuf_free")
    if gdep is not None:
        g.wait_ge(*gdep)
    g.dma_start(ws1.ap().rearrange("p (cc h) -> p cc h", h=2048),
                W["dr"]["ws1"].rearrange("(cc p) h -> p cc h", p=128)).then_inc(smb.h, 16)
    smb.bump(16)
    g.dma_start(ws2.ap().rearrange("p (cc d) -> p cc d", d=1024),
                W["dr"]["ws2"].rearrange("(cc p) d -> p cc d", p=128)).then_inc(smb.h, 16)
    smb.bump(16)
    n_bundle = smb.n

    with ExitStack() as ps:
        scp = mk.psum("scp", [128, 16], F32, ps)
        t.wait_ge(smb.h, n_bundle)
        if pes.get("psum_free") is not None:
            t.wait_ge(*pes["psum_free"])
        prev_done = None
        for tc in range(TCH):
            if prev_done is not None:
                t.wait_ge(sa.h, prev_done)
            for c in range(DCH):
                mm = t.matmul(scp[:], sb_ap(XnT, c * CST + XOFF + tc * 128, [[1, 128]]),
                              W["wgt"][:, c * 16:(c + 1) * 16],
                              start=(c == 0), stop=(c == DCH - 1))
            mm.then_inc(sp.h, 1)
            np_ = sp.bump()
            v.wait_ge(sp.h, np_)
            v.reduce_max(st2[:, 0:1], scp[:], AX.X, negate=True).then_inc(sv.h, 1)
            nv = sv.bump()
            s.wait_ge(sv.h, nv)
            s.activation(esb[:], scp[:], AF.Exp, bias=st2[:, 0:1],
                         accum_out=st2[:, 1:2]).then_inc(sa.h, 1)
            na = sa.bump()
            prev_done = na
            v.wait_ge(sa.h, na)
            v.reduce_max(st2[:, 2:3], esb[:], AX.X, negate=True).then_inc(sv.h, 1)
            nv = sv.bump()
            s.wait_ge(sv.h, nv)
            s.activation(st2[:, 3:4], st2[:, 2:3], AF.Copy, scale=-1.0).then_inc(sa.h, 1)
            na = sa.bump()
            v.wait_ge(sa.h, na)
            v.reciprocal(st2[:, 4:5], st2[:, 1:2]).then_inc(sv.h, 1)
            nv = sv.bump()
            s.wait_ge(sv.h, nv)
            s.activation(st2[:, 5:6], st2[:, 4:5], AF.Copy, scale=0.5).then_inc(sa.h, 1)
            na = sa.bump()
            v.wait_ge(sa.h, na)
            v.tensor_single_scalar(msk[:], esb[:], st2[:, 3:4], ALU.is_lt).then_inc(sv.h, 1)
            nv = sv.bump()
            g.wait_ge(sv.h, nv)
            g.tensor_mul(msk2[:], msk[:], esb[:]).then_inc(mk.sq.h, 1)
            nq = mk.sq.bump()
            v.wait_ge(mk.sq.h, nq)
            v.reduce_max(st2[:, 6:7], msk2[:], AX.X).then_inc(sv.h, 1)
            nv = sv.bump()
            s.wait_ge(sv.h, nv)
            s.activation(st2[:, 7:8], st2[:, 6:7], AF.Copy).then_inc(sa.h, 1)
            na = sa.bump()
            v.wait_ge(sa.h, na)
            v.tensor_single_scalar(msk[:], esb[:], st2[:, 7:8], ALU.is_ge).then_inc(sv.h, 1)
            nv = sv.bump()
            g.wait_ge(sv.h, nv)
            g.tensor_mul(msk2[:], msk[:], esb[:]).then_inc(mk.sq.h, 1)
            nq = mk.sq.bump()
            v.wait_ge(mk.sq.h, nq)
            v.tensor_single_scalar(cw[:, tc * 16:(tc + 1) * 16], msk2[:],
                                   st2[:, 5:6], ALU.mult).then_inc(sv.h, 1)
            nv = sv.bump()
        na_router = prev_done

    def gated_pass(w1sb, n_w1, hT, nu, t_free=None, v_free=None):
        W1W = nu * 256
        pus, pgs, sgs = bufs["pus"], bufs["pgs"], bufs["sgs"]
        t.wait_ge(n_w1[0].h, n_w1[1])
        if t_free is not None:
            t.wait_ge(*t_free)
        first_v = [v_free]
        for tq in range(SEQ // 512):
            for hc in range(nu):
                j = (tq * nu + hc) % 2
                pf = pes.get(("gp", j))
                if pf is not None:
                    t.wait_ge(sv.h, pf)
                for c in range(DCH):
                    mm = t.matmul(pus[j][:], w1sb[:, c * W1W + hc * 128: c * W1W + (hc + 1) * 128],
                                  sb_ap(XnT, c * CST + XOFF + tq * 512, [[1, 512]]),
                                  start=(c == 0), stop=(c == DCH - 1))
                mm.then_inc(sp.h, 1)
                npu = sp.bump()
                for c in range(DCH):
                    mm = t.matmul(pgs[j][:], w1sb[:, c * W1W + (nu + hc) * 128: c * W1W + (nu + hc + 1) * 128],
                                  sb_ap(XnT, c * CST + XOFF + tq * 512, [[1, 512]]),
                                  start=(c == 0), stop=(c == DCH - 1))
                mm.then_inc(sp.h, 1)
                npg = sp.bump()
                s.wait_ge(sp.h, npg)
                pf2 = pes.get(("gsg", j))
                if pf2 is not None:
                    s.wait_ge(sv.h, pf2)
                s.activation(sgs[j][:], pgs[j][:], AF.Silu).then_inc(sa.h, 1)
                na = sa.bump()
                v.wait_ge(sa.h, na)
                v.wait_ge(sp.h, npu)
                if first_v[0] is not None:
                    v.wait_ge(*first_v[0])
                    first_v[0] = None
                v.tensor_mul(sb_ap(hT, hc * SEQ + tq * 512, [[1, 512]]),
                             pus[j][:], sgs[j][:]).then_inc(sv.h, 1)
                nv = sv.bump()
                pes[("gp", j)] = nv
                pes[("gsg", j)] = nv
        return sp.n, sv.n

    def down_pass(w2sb, n_w2, hT, nk, cw_col_of_tc, nv_h, record=None):
        pys = bufs["pys"]
        t.wait_ge(sv.h, nv_h)
        t.wait_ge(n_w2[0].h, n_w2[1])
        for tc in range(TCH):
            j = tc % 2
            pf = pes.get(("yp", j))
            if pf is not None:
                t.wait_ge(sv.h, pf)
            for kc in range(nk):
                for dn in range(2):
                    mm = t.matmul(pys[j][:, dn * 512:(dn + 1) * 512],
                                  sb_ap(hT, kc * SEQ + tc * 128, [[1, 128]]),
                                  w2sb[:, kc * 1024 + dn * 512:kc * 1024 + (dn + 1) * 512],
                                  start=(kc == 0), stop=(kc == nk - 1))
            mm.then_inc(sp.h, 1)
            npy = sp.bump()
            xt = X[:, tc * 1024:(tc + 1) * 1024]
            v.wait_ge(sp.h, npy)
            v.scalar_tensor_tensor(xt, pys[j][:], cw_col_of_tc(tc), xt,
                                   ALU.mult, ALU.add).then_inc(sv.h, 1)
            nv = sv.bump()
            pes[("yp", j)] = nv
            if record is not None:
                record.append(nv)
        return sp.n, sv.n

    with ExitStack() as ps:
        bufs["pus"] = (mk.psum("pu0", [128, 512], F32, ps), mk.psum("pu1", [128, 512], F32, ps))
        bufs["pgs"] = (mk.psum("pg0", [128, 512], F32, ps), mk.psum("pg1", [128, 512], F32, ps))
        bufs["pys"] = (mk.psum("py0", [128, 1024], F32, ps), mk.psum("py1", [128, 1024], F32, ps))
        _, nv_h = gated_pass(ws1, (smb, n_bundle), hsT, DCH, t_free=(sa.h, na_router))
        np_ys, nv_ys = down_pass(ws2, (smb, n_bundle), hsT, DCH, lambda tc: 0.5, nv_h)
        shared_scope.close()
        w1s = (mk.sbuf("w1A", [128, DCH * 1024], BF16, ps), mk.sbuf("w1B", [128, DCH * 1024], BF16, ps))
        w2s = (mk.sbuf("w2A", [128, 4 * 1024], BF16, ps), mk.sbuf("w2B", [128, 4 * 1024], BF16, ps))
        hTs = (mk.sbuf("hTA", [128, 4 * SEQ], BF16, ps), mk.sbuf("hTB", [128, 4 * SEQ], BF16, ps))
        n_w1, n_w2 = [None] * 16, [None] * 16
        nv_h_end = [None] * 16
        np_y_end = [None] * 16

        s_w1 = (mk.sem("moe_w1A"), mk.sem("moe_w1B"))
        s_w2 = (mk.sem("moe_w2A"), mk.sem("moe_w2B"))

        def queue_expert_dma(e, dep, w1dep=None):
            sl = e % 2
            if w1dep is None:
                w1dep = dep
            if w1dep is not None:
                g.wait_ge(sp.h, w1dep)
            g.dma_start(w1s[sl].ap().rearrange("p (cc h) -> p cc h", h=1024),
                        W["dr"]["w1"][e].rearrange("(cc p) h -> p cc h", p=128)).then_inc(s_w1[sl].h, 16)
            n_w1[e] = (s_w1[sl], s_w1[sl].bump(16))
            if dep is not None and dep != w1dep:
                g.wait_ge(sp.h, dep)
            g.dma_start(w2s[sl].ap().rearrange("p (cc d) -> p cc d", d=1024),
                        W["dr"]["w2"][e].rearrange("(cc p) d -> p cc d", p=128)).then_inc(s_w2[sl].h, 16)
            n_w2[e] = (s_w2[sl], s_w2[sl].bump(16))

        queue_expert_dma(0, np_ys)
        queue_expert_dma(1, None)
        np_h_end = [None] * 16
        for e in range(16):
            sl = e % 2
            vfree = (sp.h, np_y_end[e - 2]) if e >= 2 else (sp.h, np_ys)
            np_h, nvh = gated_pass(w1s[sl], n_w1[e], hTs[sl], 4, v_free=vfree)
            nv_h_end[e] = nvh
            np_h_end[e] = np_h
            if e >= 1:
                ep = e - 1
                np_y, _ = down_pass(w2s[ep % 2], n_w2[ep], hTs[ep % 2], 4,
                                    lambda tc, ep=ep: cw[:, tc * 16 + ep: tc * 16 + ep + 1],
                                    nv_h_end[ep])
                np_y_end[ep] = np_y
                if ep + 2 < 16:
                    queue_expert_dma(ep + 2, np_y, w1dep=np_h_end[ep])
        ep = 15
        xtl = []
        np_y, _ = down_pass(w2s[ep % 2], n_w2[ep], hTs[ep % 2], 4,
                            lambda tc, ep=ep: cw[:, tc * 16 + ep: tc * 16 + ep + 1],
                            nv_h_end[ep], record=xtl)
        np_y_end[ep] = np_y
        pes["x_tile"] = xtl
    moescope.close()
    pes["psum_free"] = (sv.h, sv.n)
    pes["sbuf_free"] = (sv.h, sv.n)
    return sv.n


def alloc_moe(mk, es=None):
    return dict(), dict(wgt=mk.sbuf("wgt", [128, DCH * 16], BF16, es))


def queue_moe_router_weights(mk, W, dr, deps=None):
    W["dr"] = dr
    g = mk.g
    sb = mk.sem("moe_b")
    if deps is not None:
        g.wait_ge(*deps)
    g.dma_start(W["wgt"].ap().rearrange("p (cc e) -> p cc e", e=16),
                dr["wg_t"].rearrange("(cc p) e -> p cc e", p=128)).then_inc(sb.h, 16)
    sb.bump(16)


def emit_attn0(mk, X, XnT, W, bufs, pes):
    t, v, s, g, y = mk.t, mk.v, mk.s, mk.g, mk.y
    sp, sv, sa, sw, sd = mk.sp, mk.sv, mk.sa, mk.sw, mk.sd
    dr = W["dr"]

    with ExitStack() as ps:
        gnwb = mk.sbuf("gnwb", [128, 256], F32, ps)
        sgn = mk.sem("a0_gnw")
        if pes.get("sbuf_free") is not None:
            y.wait_ge(pes["sbuf_free"][0], pes["sbuf_free"][1])
        y.dma_start(gnwb[:], bass.AP(dr["gnw_row"].tensor, 0, [[0, 128], [1, 256]])).then_inc(sgn.h, 16)
        sgn.bump(16)
        v.wait_ge(sgn.h, sgn.n)
        QT = mk.sbuf("QT", [128, 4 * SEQ], BF16, ps)
        KT = mk.sbuf("KT", [128, 4 * SEQ], BF16, ps)
        Vh0 = mk.sbuf("Vh", [128, TCH * 256], BF16, ps)
        Vh = (Vh0, Vh0)
        pv0 = mk.psum("pv", [128, 256], F32, ps)
        pv = (pv0, pv0)
        ogT = mk.sbuf("ogT", [128, 4 * SEQ], BF16, ps)
        wqg = mk.sbuf("wqg", [128, DCH * 512], BF16, ps)
        wkg = mk.sbuf("wkg", [128, DCH * 512], BF16, ps)
        wvh0 = mk.sbuf("wvh", [128, DCH * 256], BF16, ps)
        wvh = (wvh0, wvh0)
        wgh0 = mk.sbuf("wgh", [128, DCH * 256], BF16, ps)
        wgh = (wgh0, wgh0)
        wop0 = mk.sbuf("wop", [128, 4 * 1024], BF16, ps)
        wop = (wop0, wop0)
        tabs = (mk.sbuf("tab", [128, 4 * 256], F32, ps), mk.sbuf("tab", [128, 4 * 256], F32, ps))
        qn = (mk.sbuf("qn", [128, 512], BF16, ps),
              mk.sbuf("qn", [128, 512], BF16, ps))
        kn = (mk.sbuf("kn", [128, 512], BF16, ps),
              mk.sbuf("kn", [128, 512], BF16, ps))
        rsc = bufs["lnsq"]
        Sb = (mk.sbuf("Sb", [128, 512], BF16, ps),
              mk.sbuf("Sb", [128, 512], BF16, ps))
        ogt = mk.sbuf("ogt", [128, 256], BF16, ps)
        stA = mk.sbuf("stA", [128, 16], F32, ps)

        mask01 = W["mask01"]
        xtl = []

        for G in range(2):
            gdep = pes.get("sbuf_free") if G == 0 else None
            if gdep is not None:
                g.wait_ge(*gdep)
            if G == 1:
                g.wait_ge(sp.h, pes["attn0_qk_done"])
            sqk = mk.sem("a0_qk")
            for wsb, dnm in ((wqg, "wq"), (wkg, "wk0")):
                src2 = dr[dnm][:, G * 512:(G + 1) * 512].rearrange("(cc p) n -> p cc n", p=128)
                g.dma_start(wsb.ap().rearrange("p (cc n) -> p cc n", n=512), src2).then_inc(sqk.h, 16)
            n_wqk = (sqk, sqk.bump(32))
            with ExitStack() as qs:
                pq_ = (mk.psum("pq", [128, 512], F32, qs),
                       mk.psum("pq", [128, 512], F32, qs))
                pk_ = (mk.psum("pk", [128, 512], F32, qs),
                       mk.psum("pk", [128, 512], F32, qs))
                ptr = mk.psum("ptr", [128, 1024], BF16, qs)
                t.wait_ge(n_wqk[0].h, n_wqk[1])
                if pes.get("psum_free") is not None:
                    t.wait_ge(*pes["psum_free"])
                prev_rope = [None, None]
                prev_evac = [None, None]
                pend_tr = None
                for tc in range(TCH + 1):
                  if tc < TCH:
                    j = tc % 2
                    stab = mk.sem("a0_tabA" if j == 0 else "a0_tabB")
                    tgate = pes.get(("a0_tab", j)) or 0
                    if pes.get("sbuf_free") is not None:
                        tgate = max(tgate, pes["sbuf_free"][1])
                    if tgate:
                        y.wait_ge(sv.h, tgate)
                    for ti, tn in enumerate(("cos_aq", "sin_aq", "cos_bk", "sin_bk")):
                        y.dma_start(tabs[j][:, ti * 256:(ti + 1) * 256],
                                    dr[tn][tc * 128:(tc + 1) * 128, G * 256:(G + 1) * 256]).then_inc(stab.h, 16)
                    n_tab = (stab, stab.bump(64))
                    if prev_rope[j] is not None:
                        t.wait_ge(sv.h, prev_rope[j])
                    for c in range(DCH):
                        mm = t.matmul(pq_[j][:], sb_ap(XnT, c * CST + XOFF + tc * 128, [[1, 128]]),
                                      wqg[:, c * 512:(c + 1) * 512],
                                      start=(c == 0), stop=(c == DCH - 1))
                    mm.then_inc(sp.h, 1)
                    npq = sp.bump()
                    for c in range(DCH):
                        mm = t.matmul(pk_[j][:], sb_ap(XnT, c * CST + XOFF + tc * 128, [[1, 128]]),
                                      wkg[:, c * 512:(c + 1) * 512],
                                      start=(c == 0), stop=(c == DCH - 1))
                    mm.then_inc(sp.h, 1)
                    npk = sp.bump()
                    v.wait_ge(n_tab[0].h, n_tab[1])
                    v.wait_ge(sp.h, npq)
                    if prev_evac[j] is not None:
                        v.wait_ge(sp.h, prev_evac[j])
                    for (pp, dst, tb) in ((pq_[j], qn[j], 0), (pk_[j], kn[j], 2)):
                        if tb == 2:
                            v.wait_ge(sp.h, npk)
                        p0 = bass.AP(pp, 0, [[512, 128], [128, 4], [2, 64]])
                        p1 = bass.AP(pp, 1, [[512, 128], [128, 4], [2, 64]])
                        cost = tabs[j][:, tb * 256:(tb + 1) * 256].ap().rearrange(
                            "p (h i) -> p h i", i=64) if False else bass.AP(
                            tabs[j], tb * 256, [[tabs[j].shape[1], 128], [64, 4], [1, 64]])
                        sint = bass.AP(tabs[j], (tb + 1) * 256,
                                       [[tabs[j].shape[1], 128], [64, 4], [1, 64]])
                        t1 = bass.AP(rsc, 0, [[rsc.shape[1], 128], [64, 4], [1, 64]])
                        t2 = bass.AP(rsc, 256, [[rsc.shape[1], 128], [64, 4], [1, 64]])
                        t3 = bass.AP(rsc, 512, [[rsc.shape[1], 128], [64, 4], [1, 64]])
                        t4 = bass.AP(rsc, 768, [[rsc.shape[1], 128], [64, 4], [1, 64]])
                        d0 = bass.AP(dst, 0, [[dst.shape[1], 128], [128, 4], [2, 64]])
                        d1 = bass.AP(dst, 1, [[dst.shape[1], 128], [128, 4], [2, 64]])
                        v.tensor_tensor(t1, p1, sint, ALU.mult)
                        v.tensor_tensor(t2, p0, cost, ALU.mult)
                        v.tensor_tensor(t3, p0, sint, ALU.mult)
                        v.tensor_tensor(t4, p1, cost, ALU.mult)
                        v.tensor_sub(d0, t2, t1)
                        last_rope = v.tensor_add(d1, t4, t3)
                    last_rope.then_inc(sv.h, 1)
                    nv_rope = sv.bump()
                    prev_rope[j] = nv_rope
                    pes[("a0_tab", j)] = nv_rope
                  if pend_tr is not None:
                    ptc, pj, pnv_rope = pend_tr
                    t.wait_ge(sv.h, pnv_rope)
                    if pes.get("qk_evac") is not None:
                        t.wait_ge(sv.h, pes["qk_evac"])
                    for c in range(4):
                        t.transpose(ptr[:, c * 128:(c + 1) * 128], qn[pj][:, c * 128:(c + 1) * 128],
                                    W["ident"][:])
                    for c in range(4):
                        tr = t.transpose(ptr[:, 512 + c * 128:512 + (c + 1) * 128],
                                         kn[pj][:, c * 128:(c + 1) * 128], W["ident"][:])
                    tr.then_inc(sp.h, 1)
                    np_tr = sp.bump()
                    prev_evac[pj] = np_tr
                    v.wait_ge(sp.h, np_tr)
                    v.tensor_copy(bass.AP(QT, ptc * 128, [[QT.shape[1], 128], [SEQ, 4], [1, 128]]),
                                  ptr[:, 0:512])
                    v.tensor_copy(bass.AP(KT, ptc * 128, [[KT.shape[1], 128], [SEQ, 4], [1, 128]]),
                                  ptr[:, 512:1024]).then_inc(sv.h, 1)
                    pes["qk_evac"] = sv.bump()
                  if tc < TCH:
                    pend_tr = (tc, j, nv_rope)
                np_qk_done = sp.n
                pes["attn0_qk_done"] = np_qk_done
                _mark(mk, 'a0_qk')
            for hg in range(4):
                h = G * 4 + hg
                sl = h % 2
                pf_vg = pes.get(("attn0_vg_done", sl))
                svg = mk.sem("a0_vg")
                pf_vp = pes.get("attn0_vp_done")
                if pf_vp is not None:
                    g.wait_ge(sp.h, pf_vp)
                src2 = dr["wv"][:, h * 256:(h + 1) * 256].rearrange("(cc p) n -> p cc n", p=128)
                g.dma_start(wvh[sl].ap().rearrange("p (cc n) -> p cc n", n=256), src2).then_inc(svg.h, 16)
                n_wvv = (svg, svg.bump(16))
                pf_g1 = pes.get(("attn0_vg_done", 1 - sl))
                if pf_g1 is not None:
                    g.wait_ge(sp.h, pf_g1)
                src2 = dr["wg0"][:, h * 256:(h + 1) * 256].rearrange("(cc p) n -> p cc n", p=128)
                g.dma_start(wgh[sl].ap().rearrange("p (cc n) -> p cc n", n=256), src2).then_inc(svg.h, 16)
                n_wv = (svg, svg.bump(16))
                if h % 2 == 0:
                    swo = mk.sem("a0_wo")
                    if h >= 2:
                        g.wait_ge(sp.h, pes["attn0_out_done"])
                    src2 = dr["wo"][h * 256:(h + 2) * 256, :].rearrange("(cc p) n -> p cc n", p=128)
                    g.dma_start(wop[(h // 2) % 2].ap().rearrange("p (cc n) -> p cc n", n=1024),
                                src2).then_inc(swo.h, 16)
                    W[("n_wo", h // 2)] = (swo, swo.bump(16))
                if True:
                    t.wait_ge(n_wvv[0].h, n_wvv[1])
                    pf_vhf = pes.get("attn0_vh_free")
                    if pf_vhf is not None:
                        s.wait_ge(sp.h, pf_vhf)
                    n_v_tile = [None] * TCH
                    for tc in range(TCH):
                        j = 0
                        pf = pes.get(("attn0_pv", j))
                        if pf is not None:
                            t.wait_ge(sa.h, pf)
                        for c in range(DCH):
                            mm = t.matmul(pv[j][:], sb_ap(XnT, c * CST + XOFF + tc * 128, [[1, 128]]),
                                          wvh[sl][:, c * 256:(c + 1) * 256],
                                          start=(c == 0), stop=(c == DCH - 1))
                        mm.then_inc(sp.h, 1)
                        npv = sp.bump()
                        s.wait_ge(sp.h, npv)
                        s.activation(Vh[sl][:, tc * 256:(tc + 1) * 256], pv[j][:],
                                     AF.Copy).then_inc(sa.h, 1)
                        n_v_tile[tc] = sa.bump()
                        pes[("attn0_pv", j)] = n_v_tile[tc]
                nv_v_done = sa.n
                pes["attn0_vp_done"] = sp.n
                with ExitStack() as os_:
                    pO = mk.psum("pO", [128, 2048], F32, os_)
                    if W.get("dbgO_d") is not None and h == 0:
                        y.wait_ge(mk.sa.h, nv_v_done)
                        y.dma_start(W["dbgO_d"].ap()[:, 0:128], QT[:, 2 * 128:3 * 128]).then_inc(mk.so.h, 16)
                        y.dma_start(W["dbgO_d"].ap()[:, 128:256], KT[:, 2 * 128:3 * 128]).then_inc(mk.so.h, 16)
                        mk.so.bump(32)
                        y.wait_ge(mk.so.h, mk.so.n)
                    for quarter in range(4):
                        qb0 = quarter * 4
                        qG2 = quarter
                        with ExitStack() as ss_:
                            pS = (mk.psum("pS", [128, 512], F32, ss_), mk.psum("pS", [128, 512], F32, ss_))
                            if pes.get("attn0_pO_free") is not None:
                                t.wait_ge(sv.h, pes["attn0_pO_free"])
                            def emit_O(kb, jj, nv_S):
                                t.wait_ge(nv_S[0], nv_S[1])
                                t.wait_ge(sa.h, n_v_tile[kb])
                                stop_mm = None
                                mm = None
                                for ql in range(4):
                                    qb = qb0 + ql
                                    if qb < kb:
                                        continue
                                    mm = t.matmul(pO[:, ql * 512:ql * 512 + 256],
                                                  Sb[jj][:, ql * 128:(ql + 1) * 128],
                                                  Vh[sl][:, kb * 256:(kb + 1) * 256],
                                                  start=(kb == 0), stop=(kb == qb))
                                    if qb == kb:
                                        stop_mm = mm
                                if stop_mm is mm:
                                    mm.then_inc(sp.h, 2)
                                    sp.bump(2)
                                    W[("np_O", kb)] = sp.n
                                else:
                                    if stop_mm is not None:
                                        stop_mm.then_inc(sp.h, 1)
                                        W[("np_O", kb)] = sp.bump()
                                    mm.then_inc(sp.h, 1)
                                    sp.bump()
                                pes[("attn0_Sb1", jj)] = sp.n

                            pend = None
                            for kb in range(qb0 + 4):
                                jj = kb % 2
                                diag = (kb // 4 == qG2)
                                eng, sem_e = (v, sv) if diag else (s, sa)
                                pf = pes.get(("attn0_pS", jj))
                                if pf is not None:
                                    t.wait_ge(pf[0], pf[1])
                                t.matmul(pS[jj][:], KT[:, hg * SEQ + kb * 128:hg * SEQ + (kb + 1) * 128],
                                         QT[:, hg * SEQ + qG2 * 512:hg * SEQ + (qG2 + 1) * 512],
                                         start=True, stop=True).then_inc(sp.h, 1)
                                npS = sp.bump()
                                eng.wait_ge(sp.h, npS)
                                pf2 = pes.get(("attn0_Sb1", jj))
                                if pf2 is not None:
                                    eng.wait_ge(sp.h, pf2)
                                if diag:
                                    cp = v.tensor_copy(Sb[jj][:], pS[jj][:])
                                    off = (kb - qG2 * 4) * 128
                                    cp = v.tensor_tensor(Sb[jj][:, off:off + 128], Sb[jj][:, off:off + 128],
                                                         mask01[:], ALU.mult)
                                else:
                                    cp = s.activation(Sb[jj][:], pS[jj][:], AF.Copy)
                                cp.then_inc(sem_e.h, 1)
                                nv_S = (sem_e.h, sem_e.bump())
                                pes[("attn0_pS", jj)] = nv_S
                                if pend is not None:
                                    emit_O(*pend)
                                pend = (kb, jj, nv_S)
                            emit_O(*pend)
                        with ExitStack() as ds_:
                            pG = (mk.psum("pG", [128, 256], F32, ds_), mk.psum("pG", [128, 256], F32, ds_))
                            pogt = mk.psum("pogt", [128, 256], BF16, ds_)
                            pf3 = pes.get("attn0_sqs")
                            if pf3 is not None:
                                s.wait_ge(sv.h, pf3)
                            for ql in range(4):
                                qb = qb0 + ql
                                s.wait_ge(sp.h, W[("np_O", qb)])
                                sqop = s.activation(rsc[:, 0:256], pO[:, ql * 512:ql * 512 + 256],
                                                    AF.Square, scale=1.0 / 16.0,
                                                    accum_out=stA[:, ql:ql + 1])
                            sqop.then_inc(sa.h, 1)
                            na = sa.bump()
                            v.wait_ge(sa.h, na)
                            v.tensor_copy(stA[:, 4:8], stA[:, 0:4]).then_inc(sv.h, 1)
                            nv = sv.bump()
                            s.wait_ge(sv.h, nv)
                            s.activation(stA[:, 8:12], stA[:, 4:8], AF.Ln,
                                         bias=mk.epscol[:, 0:1]).then_inc(sa.h, 1)
                            na = sa.bump()
                            v.wait_ge(sa.h, na)
                            v.tensor_copy(stA[:, 12:16], stA[:, 8:12]).then_inc(sv.h, 1)
                            nv = sv.bump()
                            s.wait_ge(sv.h, nv)
                            s.activation(stA[:, 4:8], stA[:, 12:16], AF.Exp,
                                         scale=-0.5).then_inc(sa.h, 1)
                            na_rstd = sa.bump()
                            v.wait_ge(sa.h, na_rstd)
                            t.wait_ge(n_wv[0].h, n_wv[1])
                            for ql in range(4):
                                qb = qb0 + ql
                                j = qb % 2
                                pf = pes.get(("attn0_pG", j))
                                if pf is not None:
                                    t.wait_ge(sv.h, pf)
                                for c in range(DCH):
                                    mm = t.matmul(pG[j][:], sb_ap(XnT, c * CST + XOFF + qb * 128, [[1, 128]]),
                                                  wgh[sl][:, c * 256:(c + 1) * 256],
                                                  start=(c == 0), stop=(c == DCH - 1))
                                mm.then_inc(sp.h, 1)
                                npG = sp.bump()
                                s.wait_ge(sp.h, npG)
                                if ql >= 1:
                                    s.wait_ge(sv.h, nv_og)
                                s.activation(rsc[:, 0:256], pG[j][:], AF.Silu).then_inc(sa.h, 1)
                                na_sg = sa.bump()
                                v.wait_ge(sa.h, na_sg)

                                v.tensor_mul(rsc[:, 256:512], rsc[:, 0:256], gnwb[:])
                                v.scalar_tensor_tensor(ogt[:], pO[:, ql * 512:ql * 512 + 256],
                                                       stA[:, 4 + ql:5 + ql], rsc[:, 256:512], ALU.mult,
                                                       ALU.mult).then_inc(sv.h, 1)
                                nv_og = sv.bump()
                                pes[("attn0_pG", j)] = nv_og
                                pes["attn0_sqs"] = nv_og
                                t.wait_ge(sv.h, nv_og)
                                pf4 = pes.get("attn0_pogt")
                                if pf4 is not None:
                                    t.wait_ge(sv.h, pf4)
                                for c in range(2):
                                    tr = t.transpose(pogt[:, c * 128:(c + 1) * 128],
                                                     ogt[:, c * 128:(c + 1) * 128], W["ident"][:])
                                tr.then_inc(sp.h, 1)
                                np_tr = sp.bump()
                                v.wait_ge(sp.h, np_tr)
                                v.tensor_copy(bass.AP(ogT, ((h % 2) * 2) * SEQ + qb * 128,
                                                      [[ogT.shape[1], 128], [SEQ, 2], [1, 128]]),
                                              pogt[:]).then_inc(sv.h, 1)
                                pes["attn0_pogt"] = sv.bump()
                            pes["attn0_pO_free"] = sv.n
                pes[("attn0_vg_done", sl)] = sp.n
                pes["attn0_vh_free"] = W[("np_O", 15)]
                _mark(mk, 'a0_head')
                if h % 2 == 1:
                    nv_ogT = sv.n
                    with ExitStack() as us_:
                        pU = (mk.psum("pU", [128, 1024], F32, us_), mk.psum("pU", [128, 1024], F32, us_))
                        t.wait_ge(sv.h, nv_ogT)
                        t.wait_ge(W[("n_wo", h // 2)][0].h, W[("n_wo", h // 2)][1])
                        for tc in range(TCH):
                            j = tc % 2
                            pf = pes.get(("attn0_pU", j))
                            if pf is not None:
                                t.wait_ge(sv.h, pf)
                            for cc in range(4):
                                for dn in range(2):
                                    mm = t.matmul(pU[j][:, dn * 512:(dn + 1) * 512],
                                                  bass.AP(ogT, cc * SEQ + tc * 128,
                                                          [[ogT.shape[1], 128], [1, 128]]),
                                                  wop[(h // 2) % 2][:, cc * 1024 + dn * 512:cc * 1024 + (dn + 1) * 512],
                                                  start=(cc == 0), stop=(cc == 3))
                            mm.then_inc(sp.h, 1)
                            npU = sp.bump()
                            xt = X[:, tc * 1024:(tc + 1) * 1024]
                            v.wait_ge(sp.h, npU)
                            v.tensor_tensor(xt, xt, pU[j][:], ALU.add).then_inc(sv.h, 1)
                            nvU = sv.bump()
                            pes[("attn0_pU", j)] = nvU
                            if h == 7:
                                xtl.append(nvU)
                    pes["attn0_out_done"] = sp.n
                    if h == 7:
                        pes["x_tile"] = xtl
    pes["psum_free"] = (sv.h, sv.n)
    pes["sbuf_free"] = (sv.h, sv.n)
    return sv.n


def alloc_attn0(mk, es=None):
    return {}, dict(mask01=mk.sbuf("mask01sb", [128, 128], BF16, es))


def queue_attn0_consts(mk, W, dr):
    W["dr"] = dr
    y = mk.y
    y.dma_start(W["mask01"][:], dr["mask01"]).then_inc(mk.sd.h, 16)
    W["n_const"] = mk.sd.bump(16)


def emit_ln_inplace(mk, X, lnb, x_ready=None):
    v, s = mk.v, mk.s
    sv, sa = mk.sv, mk.sa
    sq, st = lnb["sq"], lnb["st"]
    x_tile = lnb.pop("x_tile", None)
    if x_ready is not None and x_tile is None:
        s.wait_ge(x_ready[0], x_ready[1])
        v.wait_ge(x_ready[0], x_ready[1])
    na_rstd = emit_ln_stats(mk, X, sq, st, x_tile=x_tile)
    v.wait_ge(sa.h, na_rstd)
    out_tiles = []
    for tc in range(TCH):
        xt = X[:, tc * 1024:(tc + 1) * 1024]
        v.tensor_scalar(xt, xt, st[:, tc:tc + 1], st[:, 32 + tc:33 + tc],
                        ALU.subtract, ALU.mult).then_inc(sv.h, 1)
        out_tiles.append(sv.bump())
    lnb["out_tiles"] = out_tiles
    return sv.n


def emit_add_from_dram(mk, X, src_tiled, pes):
    v, y = mk.v, mk.y
    sv = mk.sv
    sx2 = mk.sem("addx")
    nv_entry = sv.n
    xtl_a = []
    with ExitStack() as ps:
        bufs2 = (mk.sbuf("addbuf", [128, 1024], F32, ps), mk.sbuf("addbuf", [128, 1024], F32, ps))
        prev = [nv_entry, nv_entry]
        for tc in range(TCH):
            j = tc % 2
            if prev[j] is not None:
                y.wait_ge(sv.h, prev[j])
            y.dma_start(bufs2[j][:], src_tiled[:, tc, :]).then_inc(sx2.h, 16)
            n = sx2.bump(16)
            v.wait_ge(sx2.h, n)
            v.tensor_tensor(X[:, tc * 1024:(tc + 1) * 1024],
                            X[:, tc * 1024:(tc + 1) * 1024], bufs2[j][:],
                            ALU.add).then_inc(sv.h, 1)
            prev[j] = sv.bump()
            xtl_a.append(prev[j])
    pes["x_tile"] = xtl_a
    return sv.n


def emit_blk1(mk, X, XnT, W, bufs, pes, lnb):
    t, v, s, g, y = mk.t, mk.v, mk.s, mk.g, mk.y
    sp, sv, sa, sw, sd = mk.sp, mk.sv, mk.sa, mk.sw, mk.sd
    dr = W["dr"]
    ident = W["ident"]
    maskneg = W["maskneg"]
    onescol_r = W["onescol_r"]
    b1pc = W["b1pc"]
    with ExitStack() as ps:
        QT = mk.sbuf("QT1", [128, 4 * SEQ], BF16, ps)
        KT = mk.sbuf("KT1", [128, 4 * SEQ], BF16, ps)
        Vg = mk.sbuf("Vg1", [128, TCH * 512], BF16, ps)
        qn1 = (mk.sbuf("qn1", [128, 512], BF16, ps),
               mk.sbuf("qn1", [128, 512], BF16, ps))
        kn1 = (mk.sbuf("kn1", [128, 512], BF16, ps),
               mk.sbuf("kn1", [128, 512], BF16, ps))
        tab_one = mk.sbuf("tab1", [128, 1024], F32, ps)
        tabs = (tab_one, tab_one)
        rsc = bufs["lnsq"]
        for G in range(2):
            sqk = mk.sem("b1_qkv")
            if G == 0 and pes.get("sbuf_free") is not None:
                g.wait_ge(*pes["sbuf_free"])
            if G == 1:
                g.wait_ge(sv.h, pes["b1_attn_done"])
            with ExitStack() as ws_:
                wq = mk.sbuf("wq1", [128, DCH * 512], BF16, ws_)
                wk = mk.sbuf("wk1", [128, DCH * 512], BF16, ws_)
                wv = mk.sbuf("wv1", [128, DCH * 512], BF16, ws_)
                bias_r = mk.sbuf("bqkv_r", [1, 1536], F32R, ws_)
                for i3, (wsb, coff) in enumerate(((wq, G * 512), (wk, 1024 + G * 512),
                                                  (wv, 2048 + G * 512))):
                    src2 = dr["wqkv"][:, coff:coff + 512].rearrange("(cc p) n -> p cc n", p=128)
                    g.dma_start(wsb.ap().rearrange("p (cc n) -> p cc n", n=512), src2).then_inc(sqk.h, 16)
                    sqk.bump(16)
                    g.dma_start(bias_r[:, i3 * 512:(i3 + 1) * 512],
                                dr["bias_row"][:, coff:coff + 512]).then_inc(sqk.h, 16)
                    sqk.bump(16)
                n_wqkv = (sqk, sqk.n)
                with ExitStack() as qs:
                    pq_ = (mk.psum("pq1", [128, 512], F32, qs),
                           mk.psum("pq1", [128, 512], F32, qs))
                    pk_ = (mk.psum("pk1", [128, 512], F32, qs),
                           mk.psum("pk1", [128, 512], F32, qs))
                    pv = mk.psum("pv1", [128, 512], F32, qs)
                    ptr = mk.psum("ptr1", [128, 1024], BF16, qs)
                    t.wait_ge(n_wqkv[0].h, n_wqkv[1])
                    if pes.get("psum_free") is not None:
                        t.wait_ge(*pes["psum_free"])
                    prev_rope = [None, None]
                    prev_evac = [None, None]
                    prev_v = None
                    pend_tr = None
                    for tc in range(TCH + 1):
                      if tc < TCH:
                        j = tc % 2
                        stab = mk.sem("b1_tab")
                        tgate = pes.get("b1_tab") or 0
                        if pes.get("sbuf_free") is not None:
                            tgate = max(tgate, pes["sbuf_free"][1])
                        if tgate:
                            y.wait_ge(sv.h, tgate)
                        y.dma_start(tabs[j][:], dr["tabs1"][tc * 128:(tc + 1) * 128, :]).then_inc(stab.h, 16)
                        n_tab = (stab, stab.bump(16))
                        if prev_rope[j] is not None:
                            t.wait_ge(sv.h, prev_rope[j])
                        for i3, pp in ((0, pq_[j]), (1, pk_[j])):
                            for c in range(DCH):
                                mm = t.matmul(pp[:], sb_ap(XnT, c * CST + XOFF + tc * 128, [[1, 128]]),
                                              (wq if i3 == 0 else wk)[:, c * 512:(c + 1) * 512],
                                              start=(c == 0), stop=False)
                            mm2 = t.matmul(pp[:], onescol_r[0:1, :],
                                           bias_r[0:1, i3 * 512:(i3 + 1) * 512], start=False, stop=True)
                            mm2.then_inc(sp.h, 1)
                            sp.bump()
                        npq = sp.n - 1
                        npk = sp.n
                        if prev_v is not None:
                            t.wait_ge(sv.h, prev_v)
                        for c in range(DCH):
                            mm = t.matmul(pv[:], sb_ap(XnT, c * CST + XOFF + tc * 128, [[1, 128]]),
                                          wv[:, c * 512:(c + 1) * 512],
                                          start=(c == 0), stop=False)
                        mm2 = t.matmul(pv[:], onescol_r[0:1, :],
                                       bias_r[0:1, 1024:1536],
                                       start=False, stop=True)
                        mm2.then_inc(sp.h, 1)
                        npv = sp.bump()
                        v.wait_ge(sp.h, npv)
                        pvf = pes.get("b1_Vg")
                        if pvf is not None and tc == 0 and G == 0:
                            v.wait_ge(sp.h, pvf)
                        v.tensor_copy(Vg[:, tc * 512:(tc + 1) * 512], pv[:]).then_inc(sv.h, 1)
                        nv_v = sv.bump()
                        prev_v = nv_v
                        v.wait_ge(n_tab[0].h, n_tab[1])
                        v.wait_ge(sp.h, npk)
                        if prev_evac[j] is not None:
                            v.wait_ge(sp.h, prev_evac[j])
                        for (pp, dst, tb) in ((pq_[j], qn1[j], 0), (pk_[j], kn1[j], 512)):
                            p0 = bass.AP(pp, 0, [[512, 128], [64, 8], [1, 32]])
                            p1 = bass.AP(pp, 32, [[512, 128], [64, 8], [1, 32]])
                            cost = bass.AP(tabs[j], tb, [[tabs[j].shape[1], 128], [32, 8], [1, 32]])
                            sint = bass.AP(tabs[j], tb + 256, [[tabs[j].shape[1], 128], [32, 8], [1, 32]])
                            t1 = bass.AP(rsc, 0, [[rsc.shape[1], 128], [32, 8], [1, 32]])
                            t2 = bass.AP(rsc, 256, [[rsc.shape[1], 128], [32, 8], [1, 32]])
                            t3 = bass.AP(rsc, 512, [[rsc.shape[1], 128], [32, 8], [1, 32]])
                            t4 = bass.AP(rsc, 768, [[rsc.shape[1], 128], [32, 8], [1, 32]])
                            d0 = bass.AP(dst, 0, [[dst.shape[1], 128], [64, 8], [1, 32]])
                            d1 = bass.AP(dst, 32, [[dst.shape[1], 128], [64, 8], [1, 32]])
                            v.tensor_tensor(t1, p1, sint, ALU.mult)
                            v.tensor_tensor(t2, p0, cost, ALU.mult)
                            v.tensor_tensor(t3, p0, sint, ALU.mult)
                            v.tensor_tensor(t4, p1, cost, ALU.mult)
                            v.tensor_sub(d0, t2, t1)
                            last_rope = v.tensor_add(d1, t4, t3)
                        last_rope.then_inc(sv.h, 1)
                        nv_rope = sv.bump()
                        prev_rope[j] = nv_rope
                        pes["b1_tab"] = nv_rope
                      if pend_tr is not None:
                        ptc, pj, pnv_rope = pend_tr
                        t.wait_ge(sv.h, pnv_rope)
                        if pes.get("b1_qk_evac") is not None:
                            t.wait_ge(sv.h, pes["b1_qk_evac"])
                        for c in range(4):
                            t.transpose(ptr[:, c * 128:(c + 1) * 128], qn1[pj][:, c * 128:(c + 1) * 128], ident[:])
                        for c in range(4):
                            tr = t.transpose(ptr[:, 512 + c * 128:512 + (c + 1) * 128],
                                             kn1[pj][:, c * 128:(c + 1) * 128], ident[:])
                        tr.then_inc(sp.h, 1)
                        np_tr = sp.bump()
                        prev_evac[pj] = np_tr
                        v.wait_ge(sp.h, np_tr)
                        v.tensor_copy(bass.AP(QT, ptc * 128, [[QT.shape[1], 128], [SEQ, 4], [1, 128]]),
                                      ptr[:, 0:512])
                        v.tensor_copy(bass.AP(KT, ptc * 128, [[KT.shape[1], 128], [SEQ, 4], [1, 128]]),
                                      ptr[:, 512:1024]).then_inc(sv.h, 1)
                        pes["b1_qk_evac"] = sv.bump()
                      if tc < TCH:
                        pend_tr = (tc, j, nv_rope)
                    pes["b1_qk_done"] = sp.n
            _mark(mk, 'b1_qkv')
            AoT = mk.sbuf("AoT1", [128, 4 * SEQ], BF16, ps) if G == 0 else W["_AoT"]
            if G == 0:
                W["_AoT"] = AoT
            nv_qkv_done = sv.n
            with ExitStack() as as_:
                P_sb = (mk.sbuf("P1", [128, SEQ], BF16, as_),
                        mk.sbuf("P1", [128, SEQ], BF16, as_))
                PT_sb = (mk.sbuf("PT1", [128, SEQ], BF16, as_),
                         mk.sbuf("PT1", [128, SEQ], BF16, as_))
                Ao = mk.sbuf("Ao1", [128, 64], BF16, as_)
                stB = mk.sbuf("stB", [128, 8], F32, as_)
                pS = mk.psum("pS1", [128, 2048], F32, as_)
                pPT = mk.psum("pPT1", [128, 1024], BF16, as_)
                pO = mk.psum("pO1", [128, 512], F32, as_)
                pAoT = mk.psum("pAoT1", [128, 1024], BF16, as_)
                t.wait_ge(sv.h, nv_qkv_done)
                it_list = [(hl, qb) for hl in range(8) for qb in range(TCH)]
                NIT = len(it_list)
                pipe = {}
                for ii in range(NIT + 2):
                    if ii < NIT:
                        hl, qb = it_list[ii]
                        ch = hl // 2
                        rb = (hl % 2) * 64
                        jP = ii % 2
                        L = (qb + 1) * 128
                        if qb <= 7:
                            u = pipe.get("u", 0)
                            pipe["u"] = 1 - u
                            soff = u * 1024
                            pf = pipe.get(("exp_u", u))
                        else:
                            u = None
                            soff = 0
                            pf = pes.get("b1_pS")
                        if pf is not None:
                            t.wait_ge(sa.h, pf)
                        for kG in range(qb // 4 + 1):
                            kw = min(512, L - kG * 512)
                            t.matmul(pS[:, soff + kG * 512:soff + kG * 512 + kw],
                                     QT[rb:rb + 64, ch * SEQ + qb * 128:ch * SEQ + (qb + 1) * 128],
                                     KT[rb:rb + 64, ch * SEQ + kG * 512:ch * SEQ + kG * 512 + kw],
                                     start=True, stop=(kG != qb // 4))
                        mmm = t.matmul(pS[:, soff + qb * 128:soff + (qb + 1) * 128], ident[:], maskneg[:],
                                       start=False, stop=True)
                        mmm.then_inc(sp.h, 1)
                        npS = sp.bump()
                        v.wait_ge(sp.h, npS)
                        v.reduce_max(stB[:, 5 + jP:6 + jP], pS[:, soff:soff + L],
                                     AX.X, negate=True).then_inc(sv.h, 1)
                        nv1 = sv.bump()
                        s.wait_ge(sv.h, nv1)
                        pf2 = pipe.get(("np_tr", jP))
                        if pf2 is not None:
                            s.wait_ge(sp.h, pf2)
                        s.activation(P_sb[jP][:, 0:L], pS[:, soff:soff + L], AF.Exp,
                                     bias=stB[:, 5 + jP:6 + jP],
                                     accum_out=stB[:, 1 + jP:2 + jP]).then_inc(sa.h, 1)
                        na_exp = sa.bump()
                        if u is None:
                            pipe[("exp_u", 0)] = na_exp
                            pipe[("exp_u", 1)] = na_exp
                        else:
                            pipe[("exp_u", u)] = na_exp
                        pes["b1_pS"] = na_exp
                        pipe[("it", ii)] = dict(hl=hl, qb=qb, jP=jP, na_exp=na_exp)
                    if 0 <= ii - 1 < NIT:
                        d = pipe[("it", ii - 1)]
                        jP = d["jP"]
                        nkb = d["qb"] + 1
                        t.wait_ge(sa.h, d["na_exp"])
                        nv_pt = None
                        np_tr_last = None
                        for b0 in range(0, nkb, 8):
                            bn = min(8, nkb - b0)
                            pf3 = pes.get("b1_pPT")
                            if pf3 is not None:
                                t.wait_ge(pf3[0], pf3[1])
                            for c in range(bn):
                                tr = t.transpose(pPT[:, c * 128:(c + 1) * 128],
                                                 P_sb[jP][:, (b0 + c) * 128:(b0 + c + 1) * 128], ident[:])
                            tr.then_inc(sp.h, 1)
                            np_pt = sp.bump()
                            np_tr_last = np_pt
                            v.wait_ge(sp.h, np_pt)
                            pf4 = pipe.get(("np_O", jP))
                            if pf4 is not None:
                                v.wait_ge(sp.h, pf4)
                            v.tensor_copy(PT_sb[jP][:, b0 * 128:(b0 + bn) * 128],
                                          pPT[:, 0:bn * 128]).then_inc(sv.h, 1)
                            nv_pt = sv.bump()
                            pes["b1_pPT"] = (sv.h, nv_pt)
                        pipe[("np_tr", jP)] = np_tr_last
                        d["nv_pt"] = nv_pt
                        v.wait_ge(sa.h, d["na_exp"])
                        pfq = pipe.get(("nq_rs", jP))
                        if pfq is not None:
                            v.wait_ge(mk.sq.h, pfq)
                        v.reciprocal(stB[:, 3 + jP:4 + jP], stB[:, 1 + jP:2 + jP]).then_inc(sv.h, 1)
                        nv_rs = sv.bump()
                        bcol = 0 if jP == 0 else 7
                        g.wait_ge(sv.h, nv_rs)
                        pfb = pipe.get(("nv_ao", jP))
                        if pfb is not None:
                            g.wait_ge(sv.h, pfb)
                        g.tensor_copy(stB[:, bcol:bcol + 1],
                                      stB[:, 3 + jP:4 + jP]).then_inc(mk.sq.h, 1)
                        d["nq_rs"] = mk.sq.bump()
                        pipe[("nq_rs", jP)] = d["nq_rs"]
                    if ii >= 2:
                        d = pipe.pop(("it", ii - 2))
                        hl, qb, jP = d["hl"], d["qb"], d["jP"]
                        ch = hl // 2
                        rb = (hl % 2) * 64
                        nkb = qb + 1
                        t.wait_ge(sv.h, d["nv_pt"])
                        pfa = pipe.get("nv_ao_last")
                        if pfa is not None:
                            t.wait_ge(sv.h, pfa)
                        for kb in range(nkb):
                            mm = t.matmul(pO[:, 0:64], PT_sb[jP][:, kb * 128:(kb + 1) * 128],
                                          Vg[:, kb * 512 + hl * 64:kb * 512 + hl * 64 + 64],
                                          start=(kb == 0), stop=(kb == nkb - 1))
                        mm.then_inc(sp.h, 1)
                        np_O = sp.bump()
                        pipe[("np_O", jP)] = np_O
                        v.wait_ge(sp.h, np_O)
                        v.wait_ge(mk.sq.h, d["nq_rs"])
                        pf5 = pipe.get("np_aot_last")
                        if pf5 is not None:
                            v.wait_ge(sp.h, pf5)
                        bcol = 0 if jP == 0 else 7
                        v.tensor_scalar_mul(Ao[:], pO[:, 0:64],
                                            stB[:, bcol:bcol + 1]).then_inc(sv.h, 1)
                        nv_ao = sv.bump()
                        pipe[("nv_ao", jP)] = nv_ao
                        pipe["nv_ao_last"] = nv_ao
                        t.wait_ge(sv.h, nv_ao)
                        pf6 = pes.get("b1_pAoT")
                        if pf6 is not None:
                            t.wait_ge(sv.h, pf6)
                        tr = t.transpose(pAoT[rb:rb + 64, (qb % 8) * 128:(qb % 8 + 1) * 128],
                                         Ao[:], ident[:])
                        tr.then_inc(sp.h, 1)
                        np_aot = sp.bump()
                        pipe["np_aot_last"] = np_aot
                        if qb % 8 == 7:
                            v.wait_ge(sp.h, np_aot)
                            v.tensor_copy(AoT[rb:rb + 64, ch * SEQ + (qb - 7) * 128:ch * SEQ + (qb + 1) * 128],
                                          pAoT[rb:rb + 64, 0:1024]).then_inc(sv.h, 1)
                            pes["b1_pAoT"] = sv.bump()
                        if qb == TCH - 1:
                            _mark(mk, 'b1_hl')
            nv_aot = sv.n
            with ExitStack() as us_:
                wo = mk.sbuf("wo1", [128, 4 * 1024], BF16, us_)
                bias_r = mk.sbuf("bo_r", [1, 1024], F32R, us_)
                swo = mk.sem("b1_wo")
                g.wait_ge(sv.h, nv_aot)
                g.wait_ge(sp.h, pes["b1_qk_done"])
                src2 = dr["wo1"][G * 512:(G + 1) * 512, :].rearrange("(cc p) n -> p cc n", p=128)
                g.dma_start(wo.ap().rearrange("p (cc n) -> p cc n", n=1024), src2).then_inc(swo.h, 16)
                swo.bump(16)
                if G == 1:
                    g.dma_start(bias_r[:], dr["bias_row"][:, 3072:4096]).then_inc(swo.h, 16)
                    swo.bump(16)
                n_wo = (swo, swo.n)
                pU = (mk.psum("pU1", [128, 1024], F32, us_), mk.psum("pU1", [128, 1024], F32, us_))
                t.wait_ge(sv.h, nv_aot)
                t.wait_ge(n_wo[0].h, n_wo[1])
                for tc in range(TCH):
                    jj = tc % 2
                    pf = pes.get(("b1_pU", jj))
                    if pf is not None:
                        t.wait_ge(sv.h, pf)
                    for cc in range(4):
                        for dn in range(2):
                            mm = t.matmul(pU[jj][:, dn * 512:(dn + 1) * 512],
                                          AoT[:, cc * SEQ + tc * 128:cc * SEQ + (tc + 1) * 128],
                                          wo[:, cc * 1024 + dn * 512:cc * 1024 + (dn + 1) * 512],
                                          start=(cc == 0), stop=(G == 0 and cc == 3))
                    if G == 1:
                        for dn in range(2):
                            mm = t.matmul(pU[jj][:, dn * 512:(dn + 1) * 512], onescol_r[0:1, :],
                                          bias_r[0:1, dn * 512:(dn + 1) * 512],
                                          start=False, stop=True)
                    mm.then_inc(sp.h, 1)
                    npU = sp.bump()
                    xt = X[:, tc * 1024:(tc + 1) * 1024]
                    v.wait_ge(sp.h, npU)
                    v.tensor_tensor(xt, xt, pU[jj][:], ALU.add).then_inc(sv.h, 1)
                    pes[("b1_pU", jj)] = sv.bump()
            pes["b1_Vg"] = sp.n
            pes["b1_attn_done"] = sv.n
            _mark(mk, 'b1_oproj')
    if W.get("dbg_x1") is not None:
        y.wait_ge(sv.h, sv.n)
        y.dma_start(W["dbg_x1"].ap().rearrange("p (tc d) -> p tc d", d=1024),
                    X.ap().rearrange("p (tc d) -> p tc d", d=1024)).then_inc(mk.so.h, 16)
        mk.so.bump(16)
        y.wait_ge(mk.so.h, mk.so.n)
    nv_pre_mlp = sv.n
    pes["psum_free"] = (sv.h, sv.n)
    lnb["xnt_free"] = (mk.sp, mk.sp.n)
    W["n_gb_users"] = sv.n
    with ExitStack() as lps:
        gb2 = load_gamma_beta(mk, W, dr["ln2_g_row"], dr["ln2_b_row"], lps)
        lnb["x_ready"] = (sv.h, nv_pre_mlp)
        lnb["gscr"] = mk.sbuf("gscr", [128, 1024], F32, lps)
        lnb["ptA"] = mk.psum("ptA", [128, 1024], BF16, lps)
        lnb["ptB"] = mk.psum("ptB", [128, 1024], BF16, lps)
        if pes.get("psum_free") is not None:
            mk.t.wait_ge(*pes["psum_free"])
        nv_ln = emit_ln_transpose(mk, X, XnT, lnb, gamma_beta=gb2)
        _mark(mk, 'b1_ln2')
        lnb.pop("gscr")
    pes["psum_free"] = (mk.sv.h, nv_ln)
    with ExitStack() as ms_:
        w1l = (mk.sbuf("w1l", [128, DCH * 1024], BF16, ms_), mk.sbuf("w1l", [128, DCH * 1024], BF16, ms_))
        w2l = (mk.sbuf("w2l", [128, DCH * 1024], BF16, ms_), mk.sbuf("w2l", [128, DCH * 1024], BF16, ms_))
        hT_one = mk.sbuf("hTl", [128, DCH * SEQ], BF16, ms_)
        hTl = (hT_one, hT_one)
        b2_r = mk.sbuf("b2_r", [1, 1024], F32R, ms_)
        sb2 = mk.sem("b1_b2")
        g.dma_start(b2_r[:], dr["bias_row"][:, 4096:5120]).then_inc(sb2.h, 16)
        n_b2 = (sb2, sb2.bump(16))
        s_w1 = (mk.sem("b1_w1A"), mk.sem("b1_w1B"))
        s_w2 = (mk.sem("b1_w2A"), mk.sem("b1_w2B"))
        n_w1 = [None] * 4
        n_w2 = [None] * 4
        np_h_end = [None] * 4
        np_y_end = [None] * 4
        nv_h_end = [None] * 4

        def queue_slice_dma(sl_, dep, vdep=None):
            j = sl_ % 2
            if dep is not None:
                g.wait_ge(sp.h, dep)
            if vdep is not None:
                g.wait_ge(sv.h, vdep)
            src2 = dr["w1l"][:, sl_ * 1024:(sl_ + 1) * 1024].rearrange("(cc p) n -> p cc n", p=128)
            g.dma_start(w1l[j].ap().rearrange("p (cc n) -> p cc n", n=1024), src2).then_inc(s_w1[j].h, 16)
            n_w1[sl_] = (s_w1[j], s_w1[j].bump(16))
            src3 = dr["w2l"][sl_ * 1024:(sl_ + 1) * 1024, :].rearrange("(cc p) n -> p cc n", p=128)
            g.dma_start(w2l[j].ap().rearrange("p (cc n) -> p cc n", n=1024), src3).then_inc(s_w2[j].h, 16)
            n_w2[sl_] = (s_w2[j], s_w2[j].bump(16))

        queue_slice_dma(0, pes["b1_Vg"], nv_ln)
        queue_slice_dma(1, None)
        with ExitStack() as ms2_:
            ph = (mk.psum("ph1", [128, 512], F32, ms2_), mk.psum("ph1", [128, 512], F32, ms2_))
            pY = (mk.psum("pY1", [128, 1024], F32, ms2_), mk.psum("pY1", [128, 1024], F32, ms2_))
            t.wait_ge(sv.h, nv_ln)

            def h_pass(sl_):
                j = sl_ % 2
                t.wait_ge(n_w1[sl_][0].h, n_w1[sl_][1])
                first_v = [True]
                for tq in range(SEQ // 512):
                    for hc in range(DCH):
                        jj = (tq * DCH + hc) % 2
                        pf = pes.get(("b1_ph", jj))
                        if pf is not None:
                            t.wait_ge(sa.h, pf)
                        for c in range(DCH):
                            mm = t.matmul(ph[jj][:], w1l[j][:, c * 1024 + hc * 128:c * 1024 + (hc + 1) * 128],
                                          sb_ap(XnT, c * CST + XOFF + tq * 512, [[1, 512]]),
                                          start=(c == 0), stop=(c == DCH - 1))
                        mm.then_inc(sp.h, 1)
                        nph = sp.bump()
                        s.wait_ge(sp.h, nph)
                        if first_v[0] and sl_ >= 1:
                            s.wait_ge(sp.h, np_y_end[sl_ - 1])
                            first_v[0] = False
                        s.activation(sb_ap(hTl[j], hc * SEQ + tq * 512, [[1, 512]]),
                                     ph[jj][:], AF.Gelu,
                                     bias=b1pc[:, sl_ * 8 + hc:sl_ * 8 + hc + 1]).then_inc(sa.h, 1)
                        na = sa.bump()
                        pes[("b1_ph", jj)] = na
                return sp.n, sa.n

            def y_pass(sl_):
                j = sl_ % 2
                t.wait_ge(sa.h, nv_h_end[sl_])
                t.wait_ge(n_w2[sl_][0].h, n_w2[sl_][1])
                if sl_ == 3:
                    t.wait_ge(n_b2[0].h, n_b2[1])
                for tc in range(TCH):
                    jj = tc % 2
                    pf = pes.get(("b1_pY", jj))
                    if pf is not None:
                        t.wait_ge(sv.h, pf)
                    for kc in range(DCH):
                        for dn in range(2):
                            mm = t.matmul(pY[jj][:, dn * 512:(dn + 1) * 512],
                                          sb_ap(hTl[j], kc * SEQ + tc * 128, [[1, 128]]),
                                          w2l[j][:, kc * 1024 + dn * 512:kc * 1024 + (dn + 1) * 512],
                                          start=(kc == 0), stop=(sl_ != 3 and kc == DCH - 1))
                    if sl_ == 3:
                        for dn in range(2):
                            mm = t.matmul(pY[jj][:, dn * 512:(dn + 1) * 512], onescol_r[0:1, :],
                                          b2_r[0:1, dn * 512:(dn + 1) * 512],
                                          start=False, stop=True)
                    mm.then_inc(sp.h, 1)
                    npY = sp.bump()
                    xt = X[:, tc * 1024:(tc + 1) * 1024]
                    v.wait_ge(sp.h, npY)
                    v.tensor_tensor(xt, xt, pY[jj][:], ALU.add).then_inc(sv.h, 1)
                    pes[("b1_pY", jj)] = sv.bump()
                return sp.n

            for sl_ in range(4):
                _, nah = h_pass(sl_)
                nv_h_end[sl_] = nah
                if sl_ == 3 and W.get("dbg_hT") is not None:
                    y.wait_ge(sa.h, nah)
                    y.dma_start(W["dbg_hT"].ap(), hTl[sl_ % 2][:, :]).then_inc(mk.so.h, 16)
                    mk.so.bump(16)
                    y.wait_ge(mk.so.h, mk.so.n)
                np_y_end[sl_] = y_pass(sl_)
                if sl_ + 2 < 4:
                    queue_slice_dma(sl_ + 2, np_y_end[sl_])
    pes["psum_free"] = (sv.h, sv.n)
    pes["sbuf_free"] = (sv.h, sv.n)
    return sv.n


def load_gamma_beta(mk, W, g_row, b_row, es):
    gt = mk.sbuf("lngt", [128, 1024], F32, es)
    bt = mk.sbuf("lnbt", [128, 1024], F32, es)
    sgb = mk.sem("b1_gb")
    if W.get("n_gb_users") is not None:
        mk.y.wait_ge(mk.sv.h, W["n_gb_users"])
    mk.y.dma_start(gt[:], bass.AP(g_row.tensor, 0, [[0, 128], [1, 1024]])).then_inc(sgb.h, 16)
    sgb.bump(16)
    mk.y.dma_start(bt[:], bass.AP(b_row.tensor, 0, [[0, 128], [1, 1024]])).then_inc(sgb.h, 16)
    n = sgb.bump(16)
    mk.v.wait_ge(sgb.h, n)
    return (gt, bt)


def alloc_blk1(mk, es=None):
    W = dict(
        maskneg=mk.sbuf("masknegsb", [128, 128], BF16, es),
        onescol_r=mk.sbuf("ones_r", [1, 128], F32R, es),
        b1pc=mk.sbuf("b1pc", [128, 32], F32, es),
    )
    return {}, W


def queue_blk1_consts(mk, W, dr):
    W["dr"] = dr
    sgc = mk.sem("b1_c")
    mk.y.dma_start(W["maskneg"][:], dr["maskneg"]).then_inc(sgc.h, 16)
    sgc.bump(16)
    mk.y.dma_start(W["b1pc"][:], dr["b1_pc"]).then_inc(sgc.h, 16)
    sgc.bump(16)
    sone = mk.sem("b1_ones")
    mk.g.dma_start(W["onescol_r"][:], dr["ones_row"]).then_inc(sone.h, 16)
    sone.bump(16)
    W["n_bias"] = (sgc, sgc.n)
    mk.t.wait_ge(sgc.h, sgc.n)
    mk.t.wait_ge(sone.h, sone.n)


def build_l1ffn_prog(dbg=False, dbgseq=0):
    nc = bass.Bass("TRN2", target_bir_lowering=False)
    mk = MK(nc)
    x_in = nc.dram_tensor("x_in", [2, SEQ, D], F32, kind="ExternalInput")
    x_out = nc.dram_tensor("x_out", [2, SEQ, D], F32, kind="ExternalOutput")
    dr = {
        "wr_t": nc.dram_tensor("wr_t", [D, D], F32, kind="ExternalInput").ap(),
        "wk_t": nc.dram_tensor("wk_t", [D, D], F32, kind="ExternalInput").ap(),
        "wv_t": nc.dram_tensor("wv_t", [D, D], F32, kind="ExternalInput").ap(),
        "mu_r_pc": nc.dram_tensor("mu_r_pc", [128, 8], F32, kind="ExternalInput").ap(),
        "mu_k_pc": nc.dram_tensor("mu_k_pc", [128, 8], F32, kind="ExternalInput").ap(),
    }
    ident_d = nc.dram_tensor("ident", [128, 128], BF16, kind="ExternalInput")

    X = mk.sbuf("X", [128, TCH * 1024], F32)
    XnT = mk.sbuf("XnT", [128, DCH * CST], BF16)
    ident = mk.sbuf("identsb", [128, 128], BF16)
    lnb = alloc_ln(mk)
    lnb["ident"] = ident
    bufs, W = alloc_l1ffn(mk)
    W["dr"] = dr

    mk.init_consts()
    zcols = bass.AP(XnT, 0, [[XnT.shape[1], 128], [CST, DCH], [1, XOFF]])
    mk.v.memset(zcols, 0.0).then_inc(mk.sv.h, 1)
    mk.sv.bump()
    mk.y.dma_start(ident[:], ident_d.ap()).then_inc(mk.sd.h, 16)
    n_id = mk.sd.bump(16)
    mk.t.wait_ge(mk.sd.h, n_id)

    if dbg:
        dbg_xnt = nc.dram_tensor("dbg_xnt", [128, DCH * CST], BF16, kind="ExternalOutput")
        dbg_ka = nc.dram_tensor("dbg_ka", [128, DCH * SEQ], BF16, kind="ExternalOutput")
        dbg_x = nc.dram_tensor("dbg_x", [128, TCH * 1024], F32, kind="ExternalOutput")
    pes = {}
    n_prev_out = None
    xin_t = x_in.ap().rearrange("b (tc p) d -> b p tc d", p=128)
    xout_t = x_out.ap().rearrange("b (tc p) d -> b p tc d", p=128)
    for sq_ in range(2):
        if n_prev_out is not None:
            mk.y.wait_ge(mk.so.h, n_prev_out)
        sx = mk.sem("x_in")
        mk.y.dma_start(X.ap().rearrange("p (tc d) -> p tc d", d=1024),
                       xin_t[sq_]).then_inc(sx.h, 16)
        n_x = sx.bump(16)
        mk.v.wait_ge(sx.h, n_x)
        mk.s.wait_ge(sx.h, n_x)
        lnb["xnt_free"] = pes.get("xnt_free")
        if pes.get("psum_free") is not None:
            mk.t.wait_ge(*pes["psum_free"])
        with ExitStack() as lps:
            lnb["ptA"] = mk.psum("ptA", [128, 1024], BF16, lps)
            lnb["ptB"] = mk.psum("ptB", [128, 1024], BF16, lps)
            emit_ln_transpose(mk, X, XnT, lnb)
        if dbg and sq_ == dbgseq:
            mk.y.wait_ge(mk.sv.h, mk.sv.n)
            mk.y.dma_start(dbg_xnt.ap(), XnT[:, :]).then_inc(mk.so.h, 16)
            mk.so.bump(16)
            mk.y.dma_start(dbg_x.ap(), X[:, :]).then_inc(mk.so.h, 16)
            mk.so.bump(16)
        emit_l1ffn(mk, X, XnT, W, bufs, pes)
        if dbg and sq_ == dbgseq:
            mk.y.wait_ge(mk.sv.h, mk.sv.n)
            mk.y.dma_start(dbg_ka.ap(), bufs["kaT"][:, :]).then_inc(mk.so.h, 16)
            mk.so.bump(16)
        pes["xnt_free"] = (mk.sp, mk.sp.n)
        nv = mk.sv.n
        mk.y.wait_ge(mk.sv.h, nv)
        mk.y.dma_start(xout_t[sq_],
                       X.ap().rearrange("p (tc d) -> p tc d", d=1024)).then_inc(mk.so.h, 16)
        n_prev_out = mk.so.bump(16)
    mk.y.wait_ge(mk.so.h, n_prev_out)
    return mk.finish()


def l1ffn_inputs(x2, w_r, w_k, w_v, mu_r, mu_k, const):
    return dict(
        x_in=np.ascontiguousarray(x2, dtype=np.float32),
        wr_t=np.ascontiguousarray(np.asarray(w_r, np.float32).T),
        wk_t=np.ascontiguousarray(np.asarray(w_k, np.float32).T),
        wv_t=np.ascontiguousarray(np.asarray(w_v, np.float32).T),
        mu_r_pc=percol(mu_r),
        mu_k_pc=percol(mu_k),
        ident=const["ident"],
    )


def build_moe_prog(dbg=False):
    nc = bass.Bass("TRN2", target_bir_lowering=False)
    mk = MK(nc)
    x_in = nc.dram_tensor("x_in", [2, SEQ, D], F32, kind="ExternalInput")
    x_out = nc.dram_tensor("x_out", [2, SEQ, D], F32, kind="ExternalOutput")
    dr = {
        "wg_t": nc.dram_tensor("wg_t", [D, 16], F32, kind="ExternalInput").ap(),
        "w1": nc.dram_tensor("w1", [16, D, D], F32, kind="ExternalInput").ap(),
        "w2": nc.dram_tensor("w2", [16, 512, D], F32, kind="ExternalInput").ap(),
        "ws1": nc.dram_tensor("ws1", [D, 2048], F32, kind="ExternalInput").ap(),
        "ws2": nc.dram_tensor("ws2", [D, D], F32, kind="ExternalInput").ap(),
    }
    ident_d = nc.dram_tensor("ident", [128, 128], BF16, kind="ExternalInput")

    X = mk.sbuf("X", [128, TCH * 1024], F32)
    XnT = mk.sbuf("XnT", [128, DCH * CST], BF16)
    ident = mk.sbuf("identsb", [128, 128], BF16)
    lnb = alloc_ln(mk)
    lnb["ident"] = ident
    bufs, W = alloc_moe(mk)
    W["dr"] = dr
    queue_moe_router_weights(mk, W, dr)
    if dbg:
        dbg_cw = nc.dram_tensor("dbg_cw", [128, TCH * 16], F32, kind="ExternalOutput")
        dbg_st_d = nc.dram_tensor("dbg_st", [128, 88], F32, kind="ExternalOutput")
        bufs["dbg_st"] = mk.sbuf("dbgst", [128, 88], F32)

    mk.init_consts()
    zcols = bass.AP(XnT, 0, [[XnT.shape[1], 128], [CST, DCH], [1, XOFF]])
    mk.v.memset(zcols, 0.0).then_inc(mk.sv.h, 1)
    mk.sv.bump()
    mk.y.dma_start(ident[:], ident_d.ap()).then_inc(mk.sd.h, 16)
    n_id = mk.sd.bump(16)
    mk.t.wait_ge(mk.sd.h, n_id)

    pes = {}
    n_prev_out = None
    xin_t = x_in.ap().rearrange("b (tc p) d -> b p tc d", p=128)
    xout_t = x_out.ap().rearrange("b (tc p) d -> b p tc d", p=128)
    for sq_ in range(2):
        if n_prev_out is not None:
            mk.y.wait_ge(mk.so.h, n_prev_out)
        sx = mk.sem("x_in")
        mk.y.dma_start(X.ap().rearrange("p (tc d) -> p tc d", d=1024),
                       xin_t[sq_]).then_inc(sx.h, 16)
        n_x = sx.bump(16)
        mk.v.wait_ge(sx.h, n_x)
        mk.s.wait_ge(sx.h, n_x)
        lnb["xnt_free"] = pes.get("xnt_free")
        if pes.get("psum_free") is not None:
            mk.t.wait_ge(*pes["psum_free"])
        with ExitStack() as lps:
            lnb["ptA"] = mk.psum("ptA", [128, 1024], BF16, lps)
            lnb["ptB"] = mk.psum("ptB", [128, 1024], BF16, lps)
            nv_ln = emit_ln_transpose(mk, X, XnT, lnb)
        pes["psum_free"] = (mk.sv.h, nv_ln)
        emit_moe(mk, X, XnT, W, bufs, pes)
        pes["xnt_free"] = (mk.sp, mk.sp.n)
        if dbg and sq_ == 0:
            mk.y.wait_ge(mk.sv.h, mk.sv.n)
            mk.y.dma_start(dbg_cw.ap(), bufs["dbg_cw"][:, :]).then_inc(mk.so.h, 16)
            mk.so.bump(16)
            mk.y.dma_start(dbg_st_d.ap(), bufs["dbg_st"][:, :]).then_inc(mk.so.h, 16)
            mk.so.bump(16)
        nv = mk.sv.n
        mk.y.wait_ge(mk.sv.h, nv)
        mk.y.dma_start(xout_t[sq_],
                       X.ap().rearrange("p (tc d) -> p tc d", d=1024)).then_inc(mk.so.h, 16)
        n_prev_out = mk.so.bump(16)
    mk.y.wait_ge(mk.so.h, n_prev_out)
    return mk.finish()


def moe_inputs(x2, Wg, W1, W2, Ws1, Ws2, const):
    return dict(
        x_in=np.ascontiguousarray(x2, dtype=np.float32),
        wg_t=np.ascontiguousarray(np.asarray(Wg, np.float32).T),
        w1=np.ascontiguousarray(W1, dtype=np.float32),
        w2=np.ascontiguousarray(W2, dtype=np.float32),
        ws1=np.ascontiguousarray(Ws1, dtype=np.float32),
        ws2=np.ascontiguousarray(Ws2, dtype=np.float32),
        ident=const["ident"],
    )


def build_attn0_prog():
    nc = bass.Bass("TRN2", target_bir_lowering=False)
    mk = MK(nc)
    x_in = nc.dram_tensor("x_in", [2, SEQ, D], F32, kind="ExternalInput")
    x_out = nc.dram_tensor("x_out", [2, SEQ, D], F32, kind="ExternalOutput")
    dr = {
        "wq": nc.dram_tensor("wq", [D, D], F32, kind="ExternalInput").ap(),
        "wk0": nc.dram_tensor("wk0", [D, D], F32, kind="ExternalInput").ap(),
        "wv": nc.dram_tensor("wv", [D, 2048], F32, kind="ExternalInput").ap(),
        "wg0": nc.dram_tensor("wg0", [D, 2048], F32, kind="ExternalInput").ap(),
        "wo": nc.dram_tensor("wo", [2048, D], F32, kind="ExternalInput").ap(),
        "gnw_row": nc.dram_tensor("gnw_row", [1, 256], F32, kind="ExternalInput").ap(),
        "mask01": nc.dram_tensor("mask01", [128, 128], BF16, kind="ExternalInput").ap(),
        "cos_aq": nc.dram_tensor("cos_aq", [SEQ, 512], F32, kind="ExternalInput").ap(),
        "sin_aq": nc.dram_tensor("sin_aq", [SEQ, 512], F32, kind="ExternalInput").ap(),
        "cos_bk": nc.dram_tensor("cos_bk", [SEQ, 512], F32, kind="ExternalInput").ap(),
        "sin_bk": nc.dram_tensor("sin_bk", [SEQ, 512], F32, kind="ExternalInput").ap(),
    }
    ident_d = nc.dram_tensor("ident", [128, 128], BF16, kind="ExternalInput")

    X = mk.sbuf("X", [128, TCH * 1024], F32)
    XnT = mk.sbuf("XnT", [128, DCH * CST], BF16)
    ident = mk.sbuf("identsb", [128, 128], BF16)
    lnb = alloc_ln(mk)
    lnb["ident"] = ident
    bufs, W = alloc_attn0(mk)
    bufs["lnsq"] = lnb["sq"]
    W["ident"] = ident
    import os as _os
    if _os.environ.get("A0DBG"):
        dbgO_d = nc.dram_tensor("dbgO", [128, 272], BF16, kind="ExternalOutput")
        W["dbgO_d"] = dbgO_d
    queue_attn0_consts(mk, W, dr)

    mk.init_consts()
    zcols = bass.AP(XnT, 0, [[XnT.shape[1], 128], [CST, DCH], [1, XOFF]])
    mk.v.memset(zcols, 0.0).then_inc(mk.sv.h, 1)
    mk.sv.bump()
    mk.y.dma_start(ident[:], ident_d.ap()).then_inc(mk.sd.h, 16)
    n_id = mk.sd.bump(16)
    mk.t.wait_ge(mk.sd.h, mk.sd.n)
    mk.s.wait_ge(mk.sd.h, mk.sd.n)
    mk.v.wait_ge(mk.sd.h, mk.sd.n)

    pes = {}
    n_prev_out = None
    xin_t = x_in.ap().rearrange("b (tc p) d -> b p tc d", p=128)
    xout_t = x_out.ap().rearrange("b (tc p) d -> b p tc d", p=128)
    for sq_ in range(2):
        if n_prev_out is not None:
            mk.y.wait_ge(mk.so.h, n_prev_out)
        sx = mk.sem("x_in")
        mk.y.dma_start(X.ap().rearrange("p (tc d) -> p tc d", d=1024),
                       xin_t[sq_]).then_inc(sx.h, 16)
        n_x = sx.bump(16)
        mk.v.wait_ge(sx.h, n_x)
        mk.s.wait_ge(sx.h, n_x)
        lnb["xnt_free"] = pes.get("xnt_free")
        if pes.get("psum_free") is not None:
            mk.t.wait_ge(*pes["psum_free"])
        with ExitStack() as lps:
            lnb["ptA"] = mk.psum("ptA", [128, 1024], BF16, lps)
            lnb["ptB"] = mk.psum("ptB", [128, 1024], BF16, lps)
            nv_ln = emit_ln_transpose(mk, X, XnT, lnb)
        pes["psum_free"] = (mk.sv.h, nv_ln)
        emit_attn0(mk, X, XnT, W, bufs, pes)
        pes["xnt_free"] = (mk.sp, mk.sp.n)
        nv = mk.sv.n
        mk.y.wait_ge(mk.sv.h, nv)
        mk.y.dma_start(xout_t[sq_],
                       X.ap().rearrange("p (tc d) -> p tc d", d=1024)).then_inc(mk.so.h, 16)
        n_prev_out = mk.so.bump(16)

    mk.y.wait_ge(mk.so.h, n_prev_out)
    return mk.finish()


def attn0_inputs(x2, Wq, Wk, Wv, Wg, Wo, gnw, const):
    inv = 1.0 / (10000.0 ** (np.arange(0, 128, 2, dtype=np.float64) / 128.0))
    ang = np.arange(SEQ, dtype=np.float64)[:, None] * inv[None, :]
    h = np.arange(8)
    s_h = np.log2(1.0 - 2.0 ** (-5.0 - h))
    tt = np.arange(SEQ, dtype=np.float64)
    aq = 2.0 ** (tt[:, None] * s_h[None, :]) / np.sqrt(128.0)
    bk = 2.0 ** (-tt[:, None] * s_h[None, :])
    cos = np.cos(ang)[:, None, :]
    sin = np.sin(ang)[:, None, :]
    mk_tab = lambda trig, sc: np.ascontiguousarray(
        (trig * sc[:, :, None]).reshape(SEQ, 512).astype(np.float32))
    return dict(
        x_in=np.ascontiguousarray(x2, dtype=np.float32),
        wq=np.ascontiguousarray(Wq, np.float32),
        wk0=np.ascontiguousarray(Wk, np.float32),
        wv=np.ascontiguousarray(Wv, np.float32),
        wg0=np.ascontiguousarray(Wg, np.float32),
        wo=np.ascontiguousarray(Wo, np.float32),
        gnw_row=row(gnw),
        mask01=const["mask01"],
        cos_aq=mk_tab(cos, aq), sin_aq=mk_tab(sin, aq),
        cos_bk=mk_tab(cos, bk), sin_bk=mk_tab(sin, bk),
        ident=const["ident"],
    )


def build_blk1_prog():
    nc = bass.Bass("TRN2", target_bir_lowering=False)
    mk = MK(nc)
    x_in = nc.dram_tensor("x_in", [2, SEQ, D], F32, kind="ExternalInput")
    x_out = nc.dram_tensor("x_out", [2, SEQ, D], F32, kind="ExternalOutput")
    dr = {
        "wqkv": nc.dram_tensor("wqkv", [D, 3072], F32, kind="ExternalInput").ap(),
        "wo1": nc.dram_tensor("wo1", [D, D], F32, kind="ExternalInput").ap(),
        "w1l": nc.dram_tensor("w1l", [D, 4096], F32, kind="ExternalInput").ap(),
        "w2l": nc.dram_tensor("w2l", [4096, D], F32, kind="ExternalInput").ap(),
        "ln1_g_row": nc.dram_tensor("ln1_g_row", [1, D], F32, kind="ExternalInput").ap(),
        "ln1_b_row": nc.dram_tensor("ln1_b_row", [1, D], F32, kind="ExternalInput").ap(),
        "ln2_g_row": nc.dram_tensor("ln2_g_row", [1, D], F32, kind="ExternalInput").ap(),
        "ln2_b_row": nc.dram_tensor("ln2_b_row", [1, D], F32, kind="ExternalInput").ap(),
        "bias_row": nc.dram_tensor("bias_row", [1, 3072 + 2048], F32, kind="ExternalInput").ap(),
        "b1_pc": nc.dram_tensor("b1_pc", [128, 32], F32, kind="ExternalInput").ap(),
        "maskneg": nc.dram_tensor("maskneg", [128, 128], BF16, kind="ExternalInput").ap(),
        "tabs1": nc.dram_tensor("tabs1", [SEQ, 1024], F32, kind="ExternalInput").ap(),
        "ones_row": nc.dram_tensor("ones_row", [1, 128], F32, kind="ExternalInput").ap(),
    }
    ident_d = nc.dram_tensor("ident", [128, 128], BF16, kind="ExternalInput")

    X = mk.sbuf("X", [128, TCH * 1024], F32)
    XnT = mk.sbuf("XnT", [128, DCH * CST], BF16)
    ident = mk.sbuf("identsb", [128, 128], BF16)
    lnb = alloc_ln(mk)
    lnb["ident"] = ident
    bufs, W = alloc_blk1(mk)
    bufs["lnsq"] = lnb["sq"]
    W["ident"] = ident
    import os as _os
    if _os.environ.get("B1DBG"):
        W["dbg_x1"] = nc.dram_tensor("dbg_x1", [128, TCH * 1024], F32, kind="ExternalOutput")
        W["dbg_hT"] = nc.dram_tensor("dbg_hT", [128, DCH * SEQ], BF16, kind="ExternalOutput")
        W["dbg_y"] = nc.dram_tensor("dbg_y", [128, TCH * 1024], F32, kind="ExternalOutput")
    queue_blk1_consts(mk, W, dr)

    mk.init_consts()
    zcols = bass.AP(XnT, 0, [[XnT.shape[1], 128], [CST, DCH], [1, XOFF]])
    mk.v.memset(zcols, 0.0).then_inc(mk.sv.h, 1)
    mk.sv.bump()
    mk.y.dma_start(ident[:], ident_d.ap()).then_inc(mk.sd.h, 16)
    n_id = mk.sd.bump(16)
    mk.t.wait_ge(mk.sd.h, mk.sd.n)
    mk.s.wait_ge(mk.sd.h, mk.sd.n)
    mk.v.wait_ge(mk.sd.h, mk.sd.n)

    pes = {}
    n_prev_out = None
    xin_t = x_in.ap().rearrange("b (tc p) d -> b p tc d", p=128)
    xout_t = x_out.ap().rearrange("b (tc p) d -> b p tc d", p=128)
    for sq_ in range(2):
        if n_prev_out is not None:
            mk.y.wait_ge(mk.so.h, n_prev_out)
        sx = mk.sem("x_in")
        mk.y.dma_start(X.ap().rearrange("p (tc d) -> p tc d", d=1024),
                       xin_t[sq_]).then_inc(sx.h, 16)
        n_x = sx.bump(16)
        mk.v.wait_ge(sx.h, n_x)
        mk.s.wait_ge(sx.h, n_x)
        lnb["xnt_free"] = pes.get("xnt_free")
        if pes.get("psum_free") is not None:
            mk.t.wait_ge(*pes["psum_free"])
        W["n_gb_users"] = pes.get("gb_users")
        nv_pl = emit_ln_inplace(mk, X, lnb)
        if W.get("dbg_y") is not None:
            mk.y.wait_ge(mk.sv.h, nv_pl)
            mk.y.dma_start(W["dbg_y"].ap(), X[:, :]).then_inc(mk.so.h, 16)
            mk.so.bump(16)
            mk.y.wait_ge(mk.so.h, mk.so.n)
        with ExitStack() as lps:
            gb1 = load_gamma_beta(mk, W, dr["ln1_g_row"], dr["ln1_b_row"], lps)
            lnb["x_ready"] = (mk.sv.h, nv_pl)
            lnb["gscr"] = mk.sbuf("gscr", [128, 1024], F32, lps)
            lnb["ptA"] = mk.psum("ptA", [128, 1024], BF16, lps)
            lnb["ptB"] = mk.psum("ptB", [128, 1024], BF16, lps)
            nv_ln = emit_ln_transpose(mk, X, XnT, lnb, gamma_beta=gb1)
            lnb.pop("gscr")
        pes["psum_free"] = (mk.sv.h, nv_ln)
        emit_blk1(mk, X, XnT, W, bufs, pes, lnb)
        emit_add_from_dram(mk, X, xin_t[sq_], pes)
        pes["xnt_free"] = (mk.sp, mk.sp.n)
        pes["gb_users"] = mk.sv.n
        nv = mk.sv.n
        mk.y.wait_ge(mk.sv.h, nv)
        mk.y.dma_start(xout_t[sq_],
                       X.ap().rearrange("p (tc d) -> p tc d", d=1024)).then_inc(mk.so.h, 16)
        n_prev_out = mk.so.bump(16)
    mk.y.wait_ge(mk.so.h, n_prev_out)
    return mk.finish()


def blk1_inputs(x2, Wqkv, bqkv, Wo, bo, ln1g, ln1b, ln2g, ln2b, W1, b1, W2, b2, const):
    inv = 1.0 / (10000.0 ** (np.arange(0, 64, 2, dtype=np.float64) / 64.0))
    ang = np.arange(SEQ, dtype=np.float64)[:, None] * inv[None, :]
    cos, sin = np.cos(ang), np.sin(ang)
    sc = 1.0 / 8.0
    tabs1 = np.concatenate([np.tile(cos * sc, (1, 8)), np.tile(sin * sc, (1, 8)),
                            np.tile(cos, (1, 8)), np.tile(sin, (1, 8))], axis=1)
    return dict(
        x_in=np.ascontiguousarray(x2, dtype=np.float32),
        wqkv=np.ascontiguousarray(Wqkv, np.float32),
        wo1=np.ascontiguousarray(Wo, np.float32),
        w1l=np.ascontiguousarray(W1, np.float32),
        w2l=np.ascontiguousarray(W2, np.float32),
        ln1_g_row=row(ln1g), ln1_b_row=row(ln1b),
        ln2_g_row=row(ln2g), ln2_b_row=row(ln2b),
        bias_row=np.ascontiguousarray(
            np.concatenate([bqkv, bo, b2]).reshape(1, -1).astype(np.float32)),
        b1_pc=percol(b1),
        maskneg=const["maskneg"],
        tabs1=tabs1.astype(np.float32),
        ones_row=np.ones((1, 128), dtype=np.float32),
        ident=const["ident"],
    )


def build_fused_prog(dbg=False, dbgseq=0):
    nc = bass.Bass("TRN2", target_bir_lowering=False)
    mk = MK(nc)
    x_in = nc.dram_tensor("x_in", [2, SEQ, D], F32, kind="ExternalInput")
    x_out = nc.dram_tensor("x_out", [2, SEQ, D], F32, kind="ExternalOutput")
    xspill = nc.dram_tensor("xspill", [128, TCH * 1024], F32, kind="Internal")
    dbg_ts = [nc.dram_tensor("dbg_s%d" % i, [128, TCH * 1024], F32,
                             kind="ExternalOutput") for i in range(6)] if dbg else None
    if dbg:
        dbg_xnt = nc.dram_tensor("dbg_xnt", [128, DCH * CST], BF16,
                                 kind="ExternalOutput")
        dbg_cw = nc.dram_tensor("dbg_cw", [128, TCH * 16], F32,
                                kind="ExternalOutput")
    dr = {}
    for nm, shape in (
        ("wg_t", [D, 16]), ("w1", [16, D, D]), ("w2", [16, 512, D]),
        ("ws1", [D, 2048]), ("ws2", [D, D]),
        ("wq", [D, D]), ("wk0", [D, D]), ("wv", [D, 2048]), ("wg0", [D, 2048]),
        ("wo", [2048, D]), ("gnw_row", [1, 256]),
        ("cos_aq", [SEQ, 512]), ("sin_aq", [SEQ, 512]),
        ("cos_bk", [SEQ, 512]), ("sin_bk", [SEQ, 512]),
        ("wr_t", [D, D]), ("wk_t", [D, D]), ("wv_t", [D, D]),
        ("mu_r_pc", [128, 8]), ("mu_k_pc", [128, 8]),
        ("wqkv", [D, 3072]), ("wo1", [D, D]), ("w1l", [D, 4096]), ("w2l", [4096, D]),
        ("ln1_g_row", [1, D]), ("ln1_b_row", [1, D]),
        ("ln2_g_row", [1, D]), ("ln2_b_row", [1, D]),
        ("bias_row", [1, 5120]), ("b1_pc", [128, 32]),
        ("tabs1", [SEQ, 1024]), ("ones_row", [1, 128]),
    ):
        dr[nm] = nc.dram_tensor(nm, shape, F32, kind="ExternalInput").ap()
    for nm, shape in (("mask01", [128, 128]), ("maskneg", [128, 128]),
                      ("identd", [128, 128])):
        dr[nm] = nc.dram_tensor(nm, shape, BF16, kind="ExternalInput").ap()

    X = mk.sbuf("X", [128, TCH * 1024], F32)
    XnT = mk.sbuf("XnT", [128, DCH * CST], BF16)
    ident = mk.sbuf("identsb", [128, 128], BF16)
    lnb = alloc_ln(mk)
    lnb["ident"] = ident
    moe_bufs, moe_W = alloc_moe(mk)
    moe_W["dr"] = dr
    a0_bufs, a0_W = alloc_attn0(mk)
    a0_bufs["lnsq"] = lnb["sq"]
    a0_W["ident"] = ident
    l1_bufs, l1_W = alloc_l1ffn(mk)
    b1_bufs, b1_W = alloc_blk1(mk)
    b1_bufs["lnsq"] = lnb["sq"]
    b1_W["ident"] = ident

    queue_moe_router_weights(mk, moe_W, dr)
    queue_attn0_consts(mk, a0_W, dict(dr, gnw_row=dr["gnw_row"]))
    queue_blk1_consts(mk, b1_W, dict(dr, b1_pc=dr["b1_pc"], maskneg=dr["maskneg"]))
    mk.init_consts()
    zcols = bass.AP(XnT, 0, [[XnT.shape[1], 128], [CST, DCH], [1, XOFF]])
    mk.v.memset(zcols, 0.0).then_inc(mk.sv.h, 1)
    mk.sv.bump()
    mk.y.dma_start(ident[:], dr["identd"]).then_inc(mk.sd.h, 16)
    mk.sd.bump(16)
    mk.t.wait_ge(mk.sd.h, mk.sd.n)
    mk.s.wait_ge(mk.sd.h, mk.sd.n)
    mk.v.wait_ge(mk.sd.h, mk.sd.n)

    pes = {}
    n_prev_out = None
    xin_t = x_in.ap().rearrange("b (tc p) d -> b p tc d", p=128)
    xout_t = x_out.ap().rearrange("b (tc p) d -> b p tc d", p=128)
    xsp_t = xspill.ap().rearrange("p (tc d) -> p tc d", d=1024)

    def run_ln(x_ready, gamma_beta=None, gb_rows=None, W=None, use_x_tile=True):
        xt_ = pes.pop("x_tile", None)
        with ExitStack() as lps:
            gb = None
            if gb_rows is not None:
                gb = load_gamma_beta(mk, W, gb_rows[0], gb_rows[1], lps)
                lnb["gscr"] = mk.sbuf("gscr", [128, 1024], F32, lps)
            lnb["ptA"] = mk.psum("ptA", [128, 1024], BF16, lps)
            lnb["ptB"] = mk.psum("ptB", [128, 1024], BF16, lps)
            lnb["xnt_free"] = pes.get("xnt_free")
            lnb["x_ready"] = x_ready
            if use_x_tile and xt_ is not None:
                lnb["x_tile"] = xt_
            if pes.get("psum_free") is not None:
                mk.t.wait_ge(*pes["psum_free"])
            nv_ln = emit_ln_transpose(mk, X, XnT, lnb, gamma_beta=gb)
            lnb.pop("gscr", None)
        pes["psum_free"] = (mk.sv.h, nv_ln)
        return nv_ln

    def dma_gate(n=None):
        nn = mk.sv.n if n is None else n
        mk.g.wait_ge(mk.sv.h, nn)
        mk.y.wait_ge(mk.sv.h, nn)
        pes["sbuf_free"] = (mk.sv.h, nn)

    cur_seq = [0]

    def tap(i):
        if dbg and cur_seq[0] == dbgseq:
            mk.y.wait_ge(mk.sv.h, mk.sv.n)
            mk.y.dma_start(dbg_ts[i].ap(), X[:, :]).then_inc(mk.so.h, 16)
            mk.so.bump(16)

    for sq_ in range(2):
        cur_seq[0] = sq_
        if n_prev_out is not None:
            mk.y.wait_ge(mk.so.h, n_prev_out)
        sx = mk.sem("x_in")
        mk.y.dma_start(X.ap().rearrange("p (tc d) -> p tc d", d=1024),
                       xin_t[sq_]).then_inc(sx.h, 16)
        n_x = sx.bump(16)
        xr = (sx.h, n_x)
        run_ln(xr, use_x_tile=False)
        _mark(mk, 'ln_moe1/%d' % sq_)
        if dbg and sq_ == dbgseq:
            mk.y.wait_ge(mk.sv.h, mk.sv.n)
            mk.y.dma_start(dbg_xnt.ap(), XnT[:, :]).then_inc(mk.so.h, 16)
            mk.so.bump(16)
        dma_gate()
        nv = emit_moe(mk, X, XnT, moe_W, moe_bufs, pes)
        _mark(mk, 'moe1/%d' % sq_)
        pes["xnt_free"] = (mk.sp, mk.sp.n)
        if dbg and sq_ == dbgseq:
            mk.y.wait_ge(mk.sv.h, mk.sv.n)
            mk.y.dma_start(dbg_cw.ap(), moe_bufs["dbg_cw"][:, :]).then_inc(mk.so.h, 16)
            mk.so.bump(16)
        tap(0)
        run_ln((mk.sv.h, nv))
        _mark(mk, 'ln_attn0/%d' % sq_)
        dma_gate(nv)
        nv = emit_attn0(mk, X, XnT, a0_W, a0_bufs, pes)
        _mark(mk, 'attn0/%d' % sq_)
        pes["xnt_free"] = (mk.sp, mk.sp.n)
        tap(1)
        run_ln((mk.sv.h, nv))
        _mark(mk, 'ln_moe2/%d' % sq_)
        dma_gate(nv)
        nv = emit_moe(mk, X, XnT, moe_W, moe_bufs, pes)
        _mark(mk, 'moe2/%d' % sq_)
        pes["xnt_free"] = (mk.sp, mk.sp.n)
        tap(2)
        run_ln((mk.sv.h, nv))
        _mark(mk, 'ln_l1f1/%d' % sq_)
        dma_gate(nv)
        l1_W["dr"] = dr
        nv = emit_l1ffn(mk, X, XnT, l1_W, l1_bufs, pes)
        _mark(mk, 'l1ffn1/%d' % sq_)
        pes["xnt_free"] = (mk.sp, mk.sp.n)
        tap(3)
        sxp = mk.sem("xspill")
        xtl_sp = pes.pop("x_tile", None)
        if xtl_sp is not None:
            for tc in range(TCH):
                mk.y.wait_ge(mk.sv.h, xtl_sp[tc])
                mk.y.dma_start(xsp_t[:, tc, :],
                               X[:, tc * 1024:(tc + 1) * 1024]).then_inc(sxp.h, 16)
                sxp.bump(16)
            lnb["x_tile"] = xtl_sp
        else:
            mk.y.wait_ge(mk.sv.h, nv)
            mk.y.dma_start(xspill.ap(), X[:, :]).then_inc(sxp.h, 16)
            sxp.bump(16)
            mk.s.wait_ge(sxp.h, sxp.n)
        n_sp = sxp.n
        mk.v.wait_ge(sxp.h, n_sp)
        nv_pl = emit_ln_inplace(mk, X, lnb, x_ready=(mk.sv.h, nv))
        pes["x_tile"] = lnb.pop("out_tiles")
        _mark(mk, 'ln_inpl/%d' % sq_)
        dma_gate()
        b1_W["n_gb_users"] = pes.get("gb_users")
        b1_W["dr"] = dr
        run_ln((mk.sv.h, nv_pl), gb_rows=(dr["ln1_g_row"], dr["ln1_b_row"]), W=b1_W)
        _mark(mk, 'ln_gb1/%d' % sq_)
        dma_gate()
        emit_blk1(mk, X, XnT, b1_W, b1_bufs, pes, lnb)
        _mark(mk, 'blk1/%d' % sq_)
        dma_gate()
        nv = emit_add_from_dram(mk, X, xsp_t, pes)
        pes["xnt_free"] = (mk.sp, mk.sp.n)
        pes["gb_users"] = mk.sv.n
        _mark(mk, 'addx/%d' % sq_)
        tap(4)
        run_ln((mk.sv.h, nv))
        _mark(mk, 'ln_l1f2/%d' % sq_)
        dma_gate(nv)
        nv = emit_l1ffn(mk, X, XnT, l1_W, l1_bufs, pes)
        _mark(mk, 'l1ffn2/%d' % sq_)
        pes["xnt_free"] = (mk.sp, mk.sp.n)
        tap(5)
        mk.y.wait_ge(mk.sv.h, nv)
        mk.y.dma_start(xout_t[sq_],
                       X.ap().rearrange("p (tc d) -> p tc d", d=1024)).then_inc(mk.so.h, 16)
        n_prev_out = mk.so.bump(16)
    mk.y.wait_ge(mk.so.h, n_prev_out)
    return mk.finish()


_PROG_CACHE = {}


def _get_prog(name, builder):
    if name not in _PROG_CACHE:
        _PROG_CACHE[name] = builder()
    return _PROG_CACHE[name]


def _l0attn_tables():
    inv = 1.0 / (10000.0 ** (np.arange(0, 128, 2, dtype=np.float64) / 128.0))
    ang = np.arange(SEQ, dtype=np.float64)[:, None] * inv[None, :]
    h = np.arange(8)
    s_h = np.log2(1.0 - 2.0 ** (-5.0 - h))
    tt = np.arange(SEQ, dtype=np.float64)
    aq = 2.0 ** (tt[:, None] * s_h[None, :]) / np.sqrt(128.0)
    bk = 2.0 ** (-tt[:, None] * s_h[None, :])
    cos = np.cos(ang)[:, None, :]
    sin = np.sin(ang)[:, None, :]
    mk_tab = lambda trig, sc: np.ascontiguousarray(
        (trig * sc[:, :, None]).reshape(SEQ, 512).astype(np.float32))
    return dict(cos_aq=mk_tab(cos, aq), sin_aq=mk_tab(sin, aq),
                cos_bk=mk_tab(cos, bk), sin_bk=mk_tab(sin, bk))


def _blk1_tables():
    inv = 1.0 / (10000.0 ** (np.arange(0, 64, 2, dtype=np.float64) / 64.0))
    ang = np.arange(SEQ, dtype=np.float64)[:, None] * inv[None, :]
    cos, sin = np.cos(ang), np.sin(ang)
    sc = 1.0 / 8.0
    return np.concatenate([np.tile(cos * sc, (1, 8)), np.tile(sin * sc, (1, 8)),
                           np.tile(cos, (1, 8)), np.tile(sin, (1, 8))],
                          axis=1).astype(np.float32)


def _build_wmap(inputs):
    const = host_constants()
    asnp = lambda k: np.ascontiguousarray(np.asarray(inputs[k], np.float32))

    return dict(
        wg_t=np.ascontiguousarray(asnp("L0_ffn_Wg").T),
        w1=asnp("L0_ffn_W1"), w2=asnp("L0_ffn_W2"),
        ws1=asnp("L0_ffn_Ws1"), ws2=asnp("L0_ffn_Ws2"),
        wq=asnp("L0_attn_Wq"), wk0=asnp("L0_attn_Wk"),
        wv=asnp("L0_attn_Wv"), wg0=asnp("L0_attn_Wg"), wo=asnp("L0_attn_Wo"),
        gnw_row=row(asnp("L0_attn_g_norm_w")),
        mask01=const["mask01"],
        wr_t=np.ascontiguousarray(asnp("L1_ffn_W_r").T),
        wk_t=np.ascontiguousarray(asnp("L1_ffn_W_k").T),
        wv_t=np.ascontiguousarray(asnp("L1_ffn_W_v").T),
        mu_r_pc=percol(asnp("L1_ffn_mu_r")),
        mu_k_pc=percol(asnp("L1_ffn_mu_k")),
        wqkv=asnp("L1_attn_Wqkv"), wo1=asnp("L1_attn_Wo"),
        w1l=asnp("L1_attn_W1"), w2l=asnp("L1_attn_W2"),
        ln1_g_row=row(asnp("L1_attn_ln1_g")), ln1_b_row=row(asnp("L1_attn_ln1_b")),
        ln2_g_row=row(asnp("L1_attn_ln2_g")), ln2_b_row=row(asnp("L1_attn_ln2_b")),
        bias_row=np.ascontiguousarray(np.concatenate(
            [asnp("L1_attn_bqkv"), asnp("L1_attn_bo"), asnp("L1_attn_b2")]
        ).reshape(1, -1)),
        b1_pc=percol(asnp("L1_attn_b1")),
        maskneg=const["maskneg"], tabs1=_blk1_tables(),
        ones_row=np.ones((1, 128), dtype=np.float32),
        identd=const["ident"],
        **_l0attn_tables(),
    )


def kernel(**inputs):
    x = np.ascontiguousarray(np.asarray(inputs["L0_attn_hidden_states"], np.float32))
    B = x.shape[0]
    per = B // NCORES
    wmap = _build_wmap(inputs)
    nc = _get_prog("fused", build_fused_prog)
    in_maps = []
    for c in range(NCORES):
        m = dict(wmap)
        m["x_in"] = np.ascontiguousarray(x[c * per:(c + 1) * per])
        in_maps.append(m)
    res = run_bass_kernel_spmd(nc, in_maps, core_ids=list(range(NCORES)))
    return np.concatenate([r["x_out"] for r in res.results], axis=0)


# version 17 (final)
# speedup vs baseline: 1.1910x; 1.1910x over previous
import numpy as np
import ml_dtypes
import concourse.bass as bass
import concourse.mybir as mybir
from concourse.bass_utils import run_bass_kernel_spmd
from contextlib import ExitStack

F32 = mybir.dt.float32
F32R = mybir.dt.float32r
BF16 = mybir.dt.bfloat16
AF = mybir.ActivationFunctionType
ALU = mybir.AluOpType
AX = mybir.AxisListType
BF = ml_dtypes.bfloat16

D = 1024
SEQ = 2048
TCH = SEQ // 128
DCH = D // 128
CST = 2052
XOFF = 2
NCORES = 8
LN_EPS = 1e-5


class _Rec:
    __slots__ = ("incs",)

    def __init__(self):
        self.incs = []

    def then_inc(self, sem, n):
        self.incs.append((sem, n))
        return self


class EngQ:
    def __init__(self, name):
        self.name = name
        self.ops = []

    def __getattr__(self, meth):
        if meth.startswith("_"):
            raise AttributeError(meth)

        def call(*a, **kw):
            rec = _Rec()
            self.ops.append((meth, a, kw, rec))
            return rec

        return call

    def replay(self, eng):
        for meth, a, kw, rec in self.ops:
            inst = getattr(eng, meth)(*a, **kw)
            for sem, n in rec.incs:
                inst.then_inc(sem, n)


class Sem:
    def __init__(self, handle, name):
        self.h = handle
        self.name = name
        self.n = 0

    def bump(self, k=1):
        self.n += k
        return self.n


class MK:
    def __init__(self, nc):
        self.nc = nc
        self.es = ExitStack()
        self._uid = 0
        self.t = EngQ("tensor")
        self.v = EngQ("vector")
        self.s = EngQ("scalar")
        self.g = EngQ("gpsimd")
        self.y = EngQ("sync")
        for name in ("sd", "sw", "sp", "sv", "sa", "so", "sq"):
            setattr(self, name, Sem(self.es.enter_context(nc.semaphore(name)), name))
        self._sems = {}
        self.epscol = None

    def sem(self, name):
        if name not in self._sems:
            self._sems[name] = Sem(self.es.enter_context(self.nc.semaphore(name)), name)
        return self._sems[name]

    def init_consts(self):
        self.epscol = self.sbuf("epscol", [128, 1], F32)
        self.v.memset(self.epscol[:], LN_EPS).then_inc(self.sv.h, 1)
        self.sv.bump()
        n = self.sv.n
        for q in (self.s, self.t, self.g, self.y):
            q.wait_ge(self.sv.h, n)
        return n

    def uid(self, name):
        self._uid += 1
        return f"{name}_u{self._uid}"

    def sbuf(self, name, shape, dtype, es=None):
        return (es or self.es).enter_context(self.nc.sbuf_tensor(self.uid(name), shape, dtype))

    def psum(self, name, shape, dtype=F32, es=None):
        return (es or self.es).enter_context(self.nc.psum_tensor(self.uid(name), shape, dtype))

    def audit_sems(self):
        totals = {}
        for q in (self.t, self.v, self.s, self.g, self.y):
            for meth, a, kw, rec in q.ops:
                for sem, n in rec.incs:
                    totals[id(sem)] = totals.get(id(sem), 0) + n
        sems = [getattr(self, n) for n in ("sd", "sw", "sp", "sv", "sa", "so", "sq")]
        sems += list(self._sems.values())
        ok = True
        for s in sems:
            emitted = totals.get(id(s.h), 0)
            if emitted != s.n:
                print(f"SEM MISMATCH {s.name}: emitted incs={emitted} python count={s.n}")
                ok = False
        assert ok, "semaphore count audit failed"

    def finish(self):
        self.audit_sems()
        nc = self.nc
        with nc.Block() as block:
            block.tensor(lambda eng: self.t.replay(eng))
            block.vector(lambda eng: self.v.replay(eng))
            block.scalar(lambda eng: self.s.replay(eng))
            block.gpsimd(lambda eng: self.g.replay(eng))
            block.sync(lambda eng: self.y.replay(eng))
        self.es.close()
        return nc


PHASE_MARKS = []


def _mark(mk, name):
    PHASE_MARKS.append((name, mk.sv.n, mk.sp.n, mk.sa.n))


def sb_ap(tensor, offset, dims):
    return bass.AP(tensor, offset, [[tensor.shape[1], 128]] + dims)


def host_constants():
    c = {}
    c["ident"] = np.eye(128, dtype=np.float32).astype(BF)
    inv = 1.0 / (10000.0 ** (np.arange(0, 128, 2, dtype=np.float64) / 128.0))
    ang = np.arange(SEQ, dtype=np.float64)[:, None] * inv[None, :]
    c["cos0"] = np.tile(np.cos(ang), (1, 4)).astype(np.float32)
    c["sin0"] = np.tile(np.sin(ang), (1, 4)).astype(np.float32)
    h = np.arange(8)
    s_h = np.log2(1.0 - 2.0 ** (-5.0 - h))
    t = np.arange(SEQ, dtype=np.float64)
    c["aq0"] = (2.0 ** (t[:, None] * s_h[None, :]) / np.sqrt(128.0)).astype(np.float32)
    c["bk0"] = (2.0 ** (-t[:, None] * s_h[None, :])).astype(np.float32)
    kk = np.arange(128)[:, None]
    qq = np.arange(128)[None, :]
    c["mask01"] = (qq >= kk).astype(np.float32).astype(BF)
    c["maskneg"] = np.where(qq.T < kk.T, 0.0, 0.0).astype(np.float32)
    c["maskneg"] = np.where(np.arange(128)[None, :] > np.arange(128)[:, None],
                            -1e30, 0.0).astype(np.float32).astype(BF)
    inv1 = 1.0 / (10000.0 ** (np.arange(0, 64, 2, dtype=np.float64) / 64.0))
    ang1 = np.arange(SEQ, dtype=np.float64)[:, None] * inv1[None, :]
    c["cos1"] = np.tile(np.cos(ang1), (1, 16)).astype(np.float32)
    c["sin1"] = np.tile(np.sin(ang1), (1, 16)).astype(np.float32)
    return c


def percol(v):
    return np.ascontiguousarray(np.asarray(v, dtype=np.float32).reshape(-1, 128).T)


def row(v):
    return np.ascontiguousarray(np.asarray(v, dtype=np.float32).reshape(1, -1))


def emit_ln_stats(mk, X, sq, st, x_tile=None):
    v, s = mk.v, mk.s
    sv, sa = mk.sv, mk.sa
    s.wait_ge(sa.h, sa.n)
    for tc in range(TCH):
        xt = X[:, tc * 1024:(tc + 1) * 1024]
        if x_tile is not None:
            s.wait_ge(sv.h, x_tile[tc])
        s.activation(sq[:], xt, AF.Copy, scale=1.0 / 1024.0,
                     accum_out=st[:, tc:tc + 1])
    for tc in range(TCH):
        xt = X[:, tc * 1024:(tc + 1) * 1024]
        aop = s.activation(sq[:], xt, AF.Square, scale=1.0 / 32.0,
                           accum_out=st[:, 16 + tc:17 + tc])
    aop.then_inc(sa.h, 1)
    na = sa.bump()
    v.wait_ge(sa.h, na)
    v.tensor_mul(st[:, 32:48], st[:, 0:16], st[:, 0:16]).then_inc(sv.h, 1)
    nv = sv.bump()
    mk.g.wait_ge(sv.h, nv)
    mk.g.tensor_tensor(st[:, 32:48], st[:, 32:48], st[:, 16:32],
                       ALU.subtract).then_inc(mk.sq.h, 1)
    nq = mk.sq.bump()
    s.wait_ge(mk.sq.h, nq)
    s.activation(st[:, 48:64], st[:, 32:48], AF.Ln, scale=-1.0,
                 bias=mk.epscol[:, 0:1]).then_inc(sa.h, 1)
    na = sa.bump()
    v.wait_ge(sa.h, na)
    v.tensor_copy(st[:, 48:64], st[:, 48:64]).then_inc(sv.h, 1)
    nv = sv.bump()
    s.wait_ge(sv.h, nv)
    s.activation(st[:, 32:48], st[:, 48:64], AF.Exp, scale=-0.5).then_inc(sa.h, 1)
    return sa.bump()


def emit_ln_transpose(mk, X, XnT, bufs, gamma_beta=None):
    t, v, s = mk.t, mk.v, mk.s
    sp, sv, sa = mk.sp, mk.sv, mk.sa
    xn = (bufs["xn0"], bufs["xn1"])
    pts = (bufs["ptA"], bufs["ptB"])
    sq, st, ident = bufs["sq"], bufs["st"], bufs["ident"]
    gscr = bufs.get("gscr")
    xnt_free = bufs.pop("xnt_free", None)
    x_ready = bufs.pop("x_ready", None)
    x_tile = bufs.pop("x_tile", None)
    if x_ready is not None and x_tile is None:
        s.wait_ge(x_ready[0], x_ready[1])
        v.wait_ge(x_ready[0], x_ready[1])
    na_rstd = emit_ln_stats(mk, X, sq, st, x_tile=x_tile)
    prev_tr_done = [None, None]
    prev_evac = [None, None]
    v.wait_ge(sa.h, na_rstd)
    for tc in range(TCH):
        xt = X[:, tc * 1024:(tc + 1) * 1024]
        mean_c, rstd_c = st[:, tc:tc + 1], st[:, 32 + tc:33 + tc]
        xbuf = xn[tc % 2]
        if prev_tr_done[0] is not None:
            v.wait_ge(sp.h, prev_tr_done[0])
        if gamma_beta is None:
            v_last = v.tensor_scalar(xbuf[:], xt, mean_c, rstd_c, ALU.subtract, ALU.mult)
        else:
            g_t, b_t = gamma_beta
            v.tensor_scalar(gscr[:], xt, mean_c, rstd_c, ALU.subtract, ALU.mult)
            v.tensor_mul(gscr[:], gscr[:], g_t[:])
            v_last = v.tensor_add(xbuf[:], gscr[:], b_t[:])
        v_last.then_inc(sv.h, 1)
        nv = sv.bump()
        pt = pts[tc % 2]
        t.wait_ge(sv.h, nv)
        if tc == 0 and xnt_free is not None:
            pass
        if prev_evac[tc % 2] is not None:
            t.wait_ge(sv.h, prev_evac[tc % 2])
        for c in range(DCH):
            tr = t.transpose(pt[:, c * 128:(c + 1) * 128],
                             xbuf[:, c * 128:(c + 1) * 128], ident[:])
        tr.then_inc(sp.h, 1)
        np_ = sp.bump()
        prev_tr_done[0] = np_
        v.wait_ge(sp.h, np_)
        if tc == 0 and xnt_free is not None:
            v.wait_ge(xnt_free[0].h, xnt_free[1])
        dst = bass.AP(XnT, XOFF + tc * 128,
                      [[XnT.shape[1], 128], [CST, DCH], [1, 128]])
        v.tensor_copy(dst, pt[:]).then_inc(sv.h, 1)
        nv = sv.bump()
        prev_evac[tc % 2] = nv
    return sv.n


def emit_l1ffn(mk, X, XnT, W, bufs, pes):
    t, v, s = mk.t, mk.v, mk.s
    sp, sv, sa, sw = mk.sp, mk.sv, mk.sa, mk.sw
    l1scope = ExitStack()
    kaT = mk.sbuf("kaT", [128, DCH * SEQ], BF16, l1scope)
    W["mur"] = mk.sbuf("mur", [128, 8], F32, l1scope)
    W["muk"] = mk.sbuf("muk", [128, 8], F32, l1scope)
    W["wr"] = mk.sbuf("wr", [128, DCH * 1024], BF16, l1scope)
    W["wv"] = mk.sbuf("wv", [128, DCH * 1024], BF16, l1scope)
    queue_l1ffn_weights(mk, W, W["dr"], deps=pes.get("sbuf_free"))
    v.wait_ge(W["n_mu"][0].h, W["n_mu"][1])
    with ExitStack() as ps:
        wk = mk.sbuf("wk", [128, DCH * 1024], BF16, ps)
        xkg = (mk.sbuf("xkg", [128, DCH * 512], BF16, ps),
               mk.sbuf("xkg", [128, DCH * 512], BF16, ps))
        tmpm = mk.sbuf("tmpm", [128, 512], BF16, ps)
        rks = (mk.sbuf("rk", [128, 512], BF16, ps),
               mk.sbuf("rk", [128, 512], BF16, ps))
        pks = (mk.psum("pk0", [128, 512], F32, ps), mk.psum("pk1", [128, 512], F32, ps))
        gdep = pes.get("sbuf_free")
        if gdep is not None:
            mk.g.wait_ge(*gdep)
        swk = mk.sem("l1f_wk")
        src2 = W["dr"]["wk_t"].rearrange("(cc p) d -> p cc d", p=128)
        dst2 = wk.ap().rearrange("p (cc d) -> p cc d", d=1024)
        mk.g.dma_start(dst2, src2).then_inc(swk.h, 16)
        n_wk = (swk, swk.bump(16))
        prev_grp = [None, None]
        prev_bank = [None, None]
        prev_rk = [None, None]
        t.wait_ge(n_wk[0].h, n_wk[1])
        if pes.get("psum_free") is not None:
            t.wait_ge(*pes["psum_free"])
        for tq in range(SEQ // 512):
            i = tq % 2
            if prev_grp[i] is not None:
                v.wait_ge(sp.h, prev_grp[i])
            for c in range(DCH):
                base = c * CST + XOFF + tq * 512
                xcur = sb_ap(XnT, base, [[1, 512]])
                xprv = sb_ap(XnT, base - 1, [[1, 512]])
                mu_c = W["muk"][:, c:c + 1]
                v.scalar_tensor_tensor(tmpm[:], xprv, mu_c, xprv, ALU.mult, ALU.subtract)
                v.scalar_tensor_tensor(xkg[i][:, c * 512:(c + 1) * 512], xcur, mu_c,
                                       tmpm[:], ALU.mult, ALU.subtract).then_inc(sv.h, 1)
                sv.bump()
            nv_mix = sv.n
            t.wait_ge(sv.h, nv_mix)
            for hc in range(DCH):
                j = hc % 2
                if prev_bank[j] is not None:
                    t.wait_ge(sa.h, prev_bank[j])
                for c in range(DCH):
                    mm = t.matmul(pks[j][:],
                                  wk[:, c * 1024 + hc * 128: c * 1024 + (hc + 1) * 128],
                                  xkg[i][:, c * 512:(c + 1) * 512],
                                  start=(c == 0), stop=(c == DCH - 1))
                mm.then_inc(sp.h, 1)
                npk = sp.bump()
                if hc == DCH - 1:
                    prev_grp[i] = npk
                s.wait_ge(sp.h, npk)
                s.activation(rks[j][:], pks[j][:], AF.Relu)
                s.activation(sb_ap(kaT, hc * SEQ + tq * 512, [[1, 512]]),
                             rks[j][:], AF.Square).then_inc(sa.h, 1)
                na = sa.bump()
                prev_bank[j] = na
        nv_k = sa.n
        np_k = sp.n
        pes["wk_free"] = (sp.h, np_k)
    xtl_l1 = []
    with ExitStack() as ps:
        xrs = (mk.sbuf("xrg", [128, DCH * 128], BF16, ps),
               mk.sbuf("xrg", [128, DCH * 128], BF16, ps))
        tmpm = mk.sbuf("tmpm", [128, 128], BF16, ps)
        rtoks = (mk.sbuf("rtok", [128, 1024], F32, ps),
                 mk.sbuf("rtok", [128, 1024], F32, ps))
        vr = mk.sbuf("vr", [128, 1024], F32, ps)
        prs = (mk.psum("pr0", [128, 1024], F32, ps), mk.psum("pr1", [128, 1024], F32, ps))
        pvs = (mk.psum("pv0", [128, 1024], F32, ps), mk.psum("pv1", [128, 1024], F32, ps))
        prev_r = [None, None]
        prev_x = [None, None]
        prev_xr = [None, None]
        v.wait_ge(sp.h, np_k)
        t.wait_ge(sa.h, nv_k)
        t.wait_ge(W["n_wv"][0].h, W["n_wv"][1])
        for tc in range(TCH):
            i = tc % 2
            if prev_xr[i] is not None:
                v.wait_ge(sp.h, prev_xr[i])
            for c in range(DCH):
                base = c * CST + XOFF + tc * 128
                xcur = sb_ap(XnT, base, [[1, 128]])
                xprv = sb_ap(XnT, base - 1, [[1, 128]])
                mu_c = W["mur"][:, c:c + 1]
                v.scalar_tensor_tensor(tmpm[:], xprv, mu_c, xprv, ALU.mult, ALU.subtract)
                v.scalar_tensor_tensor(xrs[i][:, c * 128:(c + 1) * 128], xcur, mu_c,
                                       tmpm[:], ALU.mult, ALU.subtract).then_inc(sv.h, 1)
                sv.bump()
            nv_mixr = sv.n
            t.wait_ge(sv.h, nv_mixr)
            if prev_r[i] is not None:
                t.wait_ge(sa.h, prev_r[i])
            if prev_x[i] is not None:
                t.wait_ge(sv.h, prev_x[i])
            for c in range(DCH):
                for dn in range(2):
                    mm = t.matmul(prs[i][:, dn * 512:(dn + 1) * 512],
                                  xrs[i][:, c * 128:(c + 1) * 128],
                                  W["wr"][:, c * 1024 + dn * 512: c * 1024 + (dn + 1) * 512],
                                  start=(c == 0), stop=(c == DCH - 1))
            mm.then_inc(sp.h, 1)
            npr = sp.bump()
            prev_xr[i] = npr
            for c in range(DCH):
                for dn in range(2):
                    mm = t.matmul(pvs[i][:, dn * 512:(dn + 1) * 512],
                                  sb_ap(kaT, c * SEQ + tc * 128, [[1, 128]]),
                                  W["wv"][:, c * 1024 + dn * 512: c * 1024 + (dn + 1) * 512],
                                  start=(c == 0), stop=(c == DCH - 1))
            mm.then_inc(sp.h, 1)
            npv = sp.bump()
            s.wait_ge(sp.h, npr)
            if prev_x[i] is not None:
                s.wait_ge(sv.h, prev_x[i])
            s.activation(rtoks[i][:], prs[i][:], AF.Sigmoid).then_inc(sa.h, 1)
            na = sa.bump()
            prev_r[i] = na
            xt = X[:, tc * 1024:(tc + 1) * 1024]
            v.wait_ge(sa.h, na)
            v.wait_ge(sp.h, npv)
            v.tensor_mul(vr[:], pvs[i][:], rtoks[i][:])
            v.scalar_tensor_tensor(xt, vr[:], 0.5, xt, ALU.mult,
                                   ALU.add).then_inc(sv.h, 1)
            nv = sv.bump()
            prev_x[i] = nv
            xtl_l1.append(nv)
    pes["x_tile"] = xtl_l1
    l1scope.close()
    pes["psum_free"] = (sv.h, sv.n)
    pes["sbuf_free"] = (sv.h, sv.n)
    return sv.n


def alloc_l1ffn(mk, es=None):
    return dict(), dict()


def queue_l1ffn_weights(mk, W, dr, deps=None):
    W["dr"] = dr
    g = mk.g
    sb = mk.sem("l1f_w")
    if deps is not None:
        g.wait_ge(*deps)
    g.dma_start(W["mur"][:], dr["mu_r_pc"]).then_inc(sb.h, 16)
    sb.bump(16)
    g.dma_start(W["muk"][:], dr["mu_k_pc"]).then_inc(sb.h, 16)
    sb.bump(16)
    for nm in ("wr", "wv"):
        src = dr[nm + "_t"].rearrange("(cc p) d -> p cc d", p=128)
        dst = W[nm].ap().rearrange("p (cc d) -> p cc d", d=1024)
        g.dma_start(dst, src).then_inc(sb.h, 16)
        sb.bump(16)
    W["n_mu"] = (sb, sb.n)
    W["n_wr"] = (sb, sb.n)
    W["n_wv"] = (sb, sb.n)


def alloc_ln(mk, es=None, gamma=False):
    xn = mk.sbuf("xn0", [128, 1024], BF16, es)
    bufs = dict(
        xn0=xn,
        xn1=xn,
        sq=mk.sbuf("sq", [128, 1024], F32, es),
        st=mk.sbuf("st", [128, 64], F32, es),
    )
    if gamma:
        bufs["gscr"] = mk.sbuf("gscr", [128, 1024], F32, es)
    return bufs


def emit_moe(mk, X, XnT, W, bufs, pes):
    t, v, s = mk.t, mk.v, mk.s
    sp, sv, sa, sw = mk.sp, mk.sv, mk.sa, mk.sw
    g = mk.g
    smb = mk.sem("moe_b")

    shared_scope = ExitStack()
    moescope = ExitStack()
    cw = mk.sbuf("cw", [128, TCH * 16], F32, moescope)
    bufs["dbg_cw"] = cw
    st2 = mk.sbuf("st2", [128, 8], F32, moescope)
    esb = mk.sbuf("esb", [128, 16], F32, moescope)
    msk = mk.sbuf("mskr", [128, 16], F32, moescope)
    msk2 = mk.sbuf("mskr2", [128, 16], F32, moescope)
    bufs["sgs"] = (mk.sbuf("sg0", [128, 512], BF16, moescope),
                   mk.sbuf("sg1", [128, 512], BF16, moescope))
    ws1 = mk.sbuf("ws1", [128, DCH * 2048], BF16, shared_scope)
    ws2 = mk.sbuf("ws2", [128, DCH * 1024], BF16, shared_scope)
    hsT = mk.sbuf("hsT", [128, DCH * SEQ], BF16, shared_scope)
    gdep = pes.get("sbuf_free")
    if gdep is not None:
        g.wait_ge(*gdep)
    g.dma_start(ws1.ap().rearrange("p (cc h) -> p cc h", h=2048),
                W["dr"]["ws1"].rearrange("(cc p) h -> p cc h", p=128)).then_inc(smb.h, 16)
    smb.bump(16)
    g.dma_start(ws2.ap().rearrange("p (cc d) -> p cc d", d=1024),
                W["dr"]["ws2"].rearrange("(cc p) d -> p cc d", p=128)).then_inc(smb.h, 16)
    smb.bump(16)
    n_bundle = smb.n

    with ExitStack() as ps:
        scp = mk.psum("scp", [128, 16], F32, ps)
        t.wait_ge(smb.h, n_bundle)
        if pes.get("psum_free") is not None:
            t.wait_ge(*pes["psum_free"])
        prev_done = None
        for tc in range(TCH):
            if prev_done is not None:
                t.wait_ge(sa.h, prev_done)
            for c in range(DCH):
                mm = t.matmul(scp[:], sb_ap(XnT, c * CST + XOFF + tc * 128, [[1, 128]]),
                              W["wgt"][:, c * 16:(c + 1) * 16],
                              start=(c == 0), stop=(c == DCH - 1))
            mm.then_inc(sp.h, 1)
            np_ = sp.bump()
            v.wait_ge(sp.h, np_)
            v.reduce_max(st2[:, 0:1], scp[:], AX.X, negate=True).then_inc(sv.h, 1)
            nv = sv.bump()
            s.wait_ge(sv.h, nv)
            s.activation(esb[:], scp[:], AF.Exp, bias=st2[:, 0:1],
                         accum_out=st2[:, 1:2]).then_inc(sa.h, 1)
            na = sa.bump()
            prev_done = na
            v.wait_ge(sa.h, na)
            v.reduce_max(st2[:, 2:3], esb[:], AX.X, negate=True).then_inc(sv.h, 1)
            nv = sv.bump()
            s.wait_ge(sv.h, nv)
            s.activation(st2[:, 3:4], st2[:, 2:3], AF.Copy, scale=-1.0).then_inc(sa.h, 1)
            na = sa.bump()
            v.wait_ge(sa.h, na)
            v.reciprocal(st2[:, 4:5], st2[:, 1:2]).then_inc(sv.h, 1)
            nv = sv.bump()
            s.wait_ge(sv.h, nv)
            s.activation(st2[:, 5:6], st2[:, 4:5], AF.Copy, scale=0.5).then_inc(sa.h, 1)
            na = sa.bump()
            v.wait_ge(sa.h, na)
            v.tensor_single_scalar(msk[:], esb[:], st2[:, 3:4], ALU.is_lt).then_inc(sv.h, 1)
            nv = sv.bump()
            g.wait_ge(sv.h, nv)
            g.tensor_mul(msk2[:], msk[:], esb[:]).then_inc(mk.sq.h, 1)
            nq = mk.sq.bump()
            v.wait_ge(mk.sq.h, nq)
            v.reduce_max(st2[:, 6:7], msk2[:], AX.X).then_inc(sv.h, 1)
            nv = sv.bump()
            s.wait_ge(sv.h, nv)
            s.activation(st2[:, 7:8], st2[:, 6:7], AF.Copy).then_inc(sa.h, 1)
            na = sa.bump()
            v.wait_ge(sa.h, na)
            v.tensor_single_scalar(msk[:], esb[:], st2[:, 7:8], ALU.is_ge).then_inc(sv.h, 1)
            nv = sv.bump()
            g.wait_ge(sv.h, nv)
            g.tensor_mul(msk2[:], msk[:], esb[:]).then_inc(mk.sq.h, 1)
            nq = mk.sq.bump()
            v.wait_ge(mk.sq.h, nq)
            v.tensor_single_scalar(cw[:, tc * 16:(tc + 1) * 16], msk2[:],
                                   st2[:, 5:6], ALU.mult).then_inc(sv.h, 1)
            nv = sv.bump()
        na_router = prev_done

    def gated_pass(w1sb, n_w1, hT, nu, t_free=None, v_free=None):
        W1W = nu * 256
        pus, pgs, sgs = bufs["pus"], bufs["pgs"], bufs["sgs"]
        t.wait_ge(n_w1[0].h, n_w1[1])
        if t_free is not None:
            t.wait_ge(*t_free)
        first_v = [v_free]
        for tq in range(SEQ // 512):
            for hc in range(nu):
                j = (tq * nu + hc) % 2
                pf = pes.get(("gp", j))
                if pf is not None:
                    t.wait_ge(sv.h, pf)
                for c in range(DCH):
                    mm = t.matmul(pus[j][:], w1sb[:, c * W1W + hc * 128: c * W1W + (hc + 1) * 128],
                                  sb_ap(XnT, c * CST + XOFF + tq * 512, [[1, 512]]),
                                  start=(c == 0), stop=(c == DCH - 1))
                mm.then_inc(sp.h, 1)
                npu = sp.bump()
                for c in range(DCH):
                    mm = t.matmul(pgs[j][:], w1sb[:, c * W1W + (nu + hc) * 128: c * W1W + (nu + hc + 1) * 128],
                                  sb_ap(XnT, c * CST + XOFF + tq * 512, [[1, 512]]),
                                  start=(c == 0), stop=(c == DCH - 1))
                mm.then_inc(sp.h, 1)
                npg = sp.bump()
                s.wait_ge(sp.h, npg)
                pf2 = pes.get(("gsg", j))
                if pf2 is not None:
                    s.wait_ge(sv.h, pf2)
                s.activation(sgs[j][:], pgs[j][:], AF.Silu).then_inc(sa.h, 1)
                na = sa.bump()
                v.wait_ge(sa.h, na)
                v.wait_ge(sp.h, npu)
                if first_v[0] is not None:
                    v.wait_ge(*first_v[0])
                    first_v[0] = None
                v.tensor_mul(sb_ap(hT, hc * SEQ + tq * 512, [[1, 512]]),
                             pus[j][:], sgs[j][:]).then_inc(sv.h, 1)
                nv = sv.bump()
                pes[("gp", j)] = nv
                pes[("gsg", j)] = nv
        return sp.n, sv.n

    def down_pass(w2sb, n_w2, hT, nk, cw_col_of_tc, nv_h, record=None):
        pys = bufs["pys"]
        t.wait_ge(sv.h, nv_h)
        t.wait_ge(n_w2[0].h, n_w2[1])
        for tc in range(TCH):
            j = tc % 2
            pf = pes.get(("yp", j))
            if pf is not None:
                t.wait_ge(sv.h, pf)
            for kc in range(nk):
                for dn in range(2):
                    mm = t.matmul(pys[j][:, dn * 512:(dn + 1) * 512],
                                  sb_ap(hT, kc * SEQ + tc * 128, [[1, 128]]),
                                  w2sb[:, kc * 1024 + dn * 512:kc * 1024 + (dn + 1) * 512],
                                  start=(kc == 0), stop=(kc == nk - 1))
            mm.then_inc(sp.h, 1)
            npy = sp.bump()
            xt = X[:, tc * 1024:(tc + 1) * 1024]
            v.wait_ge(sp.h, npy)
            v.scalar_tensor_tensor(xt, pys[j][:], cw_col_of_tc(tc), xt,
                                   ALU.mult, ALU.add).then_inc(sv.h, 1)
            nv = sv.bump()
            pes[("yp", j)] = nv
            if record is not None:
                record.append(nv)
        return sp.n, sv.n

    with ExitStack() as ps:
        bufs["pus"] = (mk.psum("pu0", [128, 512], F32, ps), mk.psum("pu1", [128, 512], F32, ps))
        bufs["pgs"] = (mk.psum("pg0", [128, 512], F32, ps), mk.psum("pg1", [128, 512], F32, ps))
        bufs["pys"] = (mk.psum("py0", [128, 1024], F32, ps), mk.psum("py1", [128, 1024], F32, ps))
        _, nv_h = gated_pass(ws1, (smb, n_bundle), hsT, DCH, t_free=(sa.h, na_router))
        np_ys, nv_ys = down_pass(ws2, (smb, n_bundle), hsT, DCH, lambda tc: 0.5, nv_h)
        shared_scope.close()
        w1s = (mk.sbuf("w1A", [128, DCH * 1024], BF16, ps), mk.sbuf("w1B", [128, DCH * 1024], BF16, ps))
        w2s = (mk.sbuf("w2A", [128, 4 * 1024], BF16, ps), mk.sbuf("w2B", [128, 4 * 1024], BF16, ps))
        hTs = (mk.sbuf("hTA", [128, 4 * SEQ], BF16, ps), mk.sbuf("hTB", [128, 4 * SEQ], BF16, ps))
        n_w1, n_w2 = [None] * 16, [None] * 16
        nv_h_end = [None] * 16
        np_y_end = [None] * 16

        s_w1 = (mk.sem("moe_w1A"), mk.sem("moe_w1B"))
        s_w2 = (mk.sem("moe_w2A"), mk.sem("moe_w2B"))

        def queue_expert_dma(e, dep, w1dep=None):
            sl = e % 2
            if w1dep is None:
                w1dep = dep
            if w1dep is not None:
                g.wait_ge(sp.h, w1dep)
            g.dma_start(w1s[sl].ap().rearrange("p (cc h) -> p cc h", h=1024),
                        W["dr"]["w1"][e].rearrange("(cc p) h -> p cc h", p=128)).then_inc(s_w1[sl].h, 16)
            n_w1[e] = (s_w1[sl], s_w1[sl].bump(16))
            if dep is not None and dep != w1dep:
                g.wait_ge(sp.h, dep)
            g.dma_start(w2s[sl].ap().rearrange("p (cc d) -> p cc d", d=1024),
                        W["dr"]["w2"][e].rearrange("(cc p) d -> p cc d", p=128)).then_inc(s_w2[sl].h, 16)
            n_w2[e] = (s_w2[sl], s_w2[sl].bump(16))

        queue_expert_dma(0, np_ys)
        queue_expert_dma(1, None)
        np_h_end = [None] * 16
        for e in range(16):
            sl = e % 2
            vfree = (sp.h, np_y_end[e - 2]) if e >= 2 else (sp.h, np_ys)
            np_h, nvh = gated_pass(w1s[sl], n_w1[e], hTs[sl], 4, v_free=vfree)
            nv_h_end[e] = nvh
            np_h_end[e] = np_h
            if e >= 1:
                ep = e - 1
                np_y, _ = down_pass(w2s[ep % 2], n_w2[ep], hTs[ep % 2], 4,
                                    lambda tc, ep=ep: cw[:, tc * 16 + ep: tc * 16 + ep + 1],
                                    nv_h_end[ep])
                np_y_end[ep] = np_y
                if ep + 2 < 16:
                    queue_expert_dma(ep + 2, np_y, w1dep=np_h_end[ep])
        ep = 15
        xtl = []
        np_y, _ = down_pass(w2s[ep % 2], n_w2[ep], hTs[ep % 2], 4,
                            lambda tc, ep=ep: cw[:, tc * 16 + ep: tc * 16 + ep + 1],
                            nv_h_end[ep], record=xtl)
        np_y_end[ep] = np_y
        pes["x_tile"] = xtl
    moescope.close()
    pes["psum_free"] = (sv.h, sv.n)
    pes["sbuf_free"] = (sv.h, sv.n)
    return sv.n


def alloc_moe(mk, es=None):
    return dict(), dict(wgt=mk.sbuf("wgt", [128, DCH * 16], BF16, es))


def queue_moe_router_weights(mk, W, dr, deps=None):
    W["dr"] = dr
    g = mk.g
    sb = mk.sem("moe_b")
    if deps is not None:
        g.wait_ge(*deps)
    g.dma_start(W["wgt"].ap().rearrange("p (cc e) -> p cc e", e=16),
                dr["wg_t"].rearrange("(cc p) e -> p cc e", p=128)).then_inc(sb.h, 16)
    sb.bump(16)


def emit_attn0(mk, X, XnT, W, bufs, pes):
    t, v, s, g, y = mk.t, mk.v, mk.s, mk.g, mk.y
    sp, sv, sa, sw, sd = mk.sp, mk.sv, mk.sa, mk.sw, mk.sd
    dr = W["dr"]

    with ExitStack() as ps:
        gnwb = mk.sbuf("gnwb", [128, 256], F32, ps)
        sgn = mk.sem("a0_gnw")
        if pes.get("sbuf_free") is not None:
            y.wait_ge(pes["sbuf_free"][0], pes["sbuf_free"][1])
        y.dma_start(gnwb[:], bass.AP(dr["gnw_row"].tensor, 0, [[0, 128], [1, 256]])).then_inc(sgn.h, 16)
        sgn.bump(16)
        v.wait_ge(sgn.h, sgn.n)
        QT = mk.sbuf("QT", [128, 4 * SEQ], BF16, ps)
        KT = mk.sbuf("KT", [128, 4 * SEQ], BF16, ps)
        Vh0 = mk.sbuf("Vh", [128, TCH * 256], BF16, ps)
        Vh = (Vh0, Vh0)
        pv0 = mk.psum("pv", [128, 256], F32, ps)
        pv = (pv0, pv0)
        ogT = mk.sbuf("ogT", [128, 4 * SEQ], BF16, ps)
        wqg = mk.sbuf("wqg", [128, DCH * 512], BF16, ps)
        wkg = mk.sbuf("wkg", [128, DCH * 512], BF16, ps)
        wvh0 = mk.sbuf("wvh", [128, DCH * 256], BF16, ps)
        wvh = (wvh0, wvh0)
        wgh0 = mk.sbuf("wgh", [128, DCH * 256], BF16, ps)
        wgh = (wgh0, wgh0)
        wop0 = mk.sbuf("wop", [128, 4 * 1024], BF16, ps)
        wop = (wop0, wop0)
        tabs = (mk.sbuf("tab", [128, 4 * 256], F32, ps), mk.sbuf("tab", [128, 4 * 256], F32, ps))
        qn = (mk.sbuf("qn", [128, 512], BF16, ps),
              mk.sbuf("qn", [128, 512], BF16, ps))
        kn = (mk.sbuf("kn", [128, 512], BF16, ps),
              mk.sbuf("kn", [128, 512], BF16, ps))
        rsc = bufs["lnsq"]
        Sb = (mk.sbuf("Sb", [128, 512], BF16, ps),
              mk.sbuf("Sb", [128, 512], BF16, ps))
        ogt = mk.sbuf("ogt", [128, 256], BF16, ps)
        stA = mk.sbuf("stA", [128, 16], F32, ps)

        mask01 = W["mask01"]
        xtl = []

        for G in range(2):
            gdep = pes.get("sbuf_free") if G == 0 else None
            if gdep is not None:
                g.wait_ge(*gdep)
            if G == 1:
                g.wait_ge(sp.h, pes["attn0_qk_done"])
            sqk = mk.sem("a0_qk")
            for wsb, dnm in ((wqg, "wq"), (wkg, "wk0")):
                src2 = dr[dnm][:, G * 512:(G + 1) * 512].rearrange("(cc p) n -> p cc n", p=128)
                g.dma_start(wsb.ap().rearrange("p (cc n) -> p cc n", n=512), src2).then_inc(sqk.h, 16)
            n_wqk = (sqk, sqk.bump(32))
            with ExitStack() as qs:
                pq_ = (mk.psum("pq", [128, 512], F32, qs),
                       mk.psum("pq", [128, 512], F32, qs))
                pk_ = (mk.psum("pk", [128, 512], F32, qs),
                       mk.psum("pk", [128, 512], F32, qs))
                ptr = mk.psum("ptr", [128, 1024], BF16, qs)
                t.wait_ge(n_wqk[0].h, n_wqk[1])
                if pes.get("psum_free") is not None:
                    t.wait_ge(*pes["psum_free"])
                prev_rope = [None, None]
                prev_evac = [None, None]
                pend_tr = None
                for tc in range(TCH + 1):
                  if tc < TCH:
                    j = tc % 2
                    stab = mk.sem("a0_tabA" if j == 0 else "a0_tabB")
                    tgate = pes.get(("a0_tab", j)) or 0
                    if pes.get("sbuf_free") is not None:
                        tgate = max(tgate, pes["sbuf_free"][1])
                    if tgate:
                        y.wait_ge(sv.h, tgate)
                    for ti, tn in enumerate(("cos_aq", "sin_aq", "cos_bk", "sin_bk")):
                        y.dma_start(tabs[j][:, ti * 256:(ti + 1) * 256],
                                    dr[tn][tc * 128:(tc + 1) * 128, G * 256:(G + 1) * 256]).then_inc(stab.h, 16)
                    n_tab = (stab, stab.bump(64))
                    if prev_rope[j] is not None:
                        t.wait_ge(sv.h, prev_rope[j])
                    for c in range(DCH):
                        mm = t.matmul(pq_[j][:], sb_ap(XnT, c * CST + XOFF + tc * 128, [[1, 128]]),
                                      wqg[:, c * 512:(c + 1) * 512],
                                      start=(c == 0), stop=(c == DCH - 1))
                    mm.then_inc(sp.h, 1)
                    npq = sp.bump()
                    for c in range(DCH):
                        mm = t.matmul(pk_[j][:], sb_ap(XnT, c * CST + XOFF + tc * 128, [[1, 128]]),
                                      wkg[:, c * 512:(c + 1) * 512],
                                      start=(c == 0), stop=(c == DCH - 1))
                    mm.then_inc(sp.h, 1)
                    npk = sp.bump()
                    v.wait_ge(n_tab[0].h, n_tab[1])
                    v.wait_ge(sp.h, npq)
                    if prev_evac[j] is not None:
                        v.wait_ge(sp.h, prev_evac[j])
                    for (pp, dst, tb) in ((pq_[j], qn[j], 0), (pk_[j], kn[j], 2)):
                        if tb == 2:
                            v.wait_ge(sp.h, npk)
                        p0 = bass.AP(pp, 0, [[512, 128], [128, 4], [2, 64]])
                        p1 = bass.AP(pp, 1, [[512, 128], [128, 4], [2, 64]])
                        cost = tabs[j][:, tb * 256:(tb + 1) * 256].ap().rearrange(
                            "p (h i) -> p h i", i=64) if False else bass.AP(
                            tabs[j], tb * 256, [[tabs[j].shape[1], 128], [64, 4], [1, 64]])
                        sint = bass.AP(tabs[j], (tb + 1) * 256,
                                       [[tabs[j].shape[1], 128], [64, 4], [1, 64]])
                        t1 = bass.AP(rsc, 0, [[rsc.shape[1], 128], [64, 4], [1, 64]])
                        t2 = bass.AP(rsc, 256, [[rsc.shape[1], 128], [64, 4], [1, 64]])
                        t3 = bass.AP(rsc, 512, [[rsc.shape[1], 128], [64, 4], [1, 64]])
                        t4 = bass.AP(rsc, 768, [[rsc.shape[1], 128], [64, 4], [1, 64]])
                        d0 = bass.AP(dst, 0, [[dst.shape[1], 128], [128, 4], [2, 64]])
                        d1 = bass.AP(dst, 1, [[dst.shape[1], 128], [128, 4], [2, 64]])
                        v.tensor_tensor(t1, p1, sint, ALU.mult)
                        v.tensor_tensor(t2, p0, cost, ALU.mult)
                        v.tensor_tensor(t3, p0, sint, ALU.mult)
                        v.tensor_tensor(t4, p1, cost, ALU.mult)
                        v.tensor_sub(d0, t2, t1)
                        last_rope = v.tensor_add(d1, t4, t3)
                    last_rope.then_inc(sv.h, 1)
                    nv_rope = sv.bump()
                    prev_rope[j] = nv_rope
                    pes[("a0_tab", j)] = nv_rope
                  if pend_tr is not None:
                    ptc, pj, pnv_rope = pend_tr
                    t.wait_ge(sv.h, pnv_rope)
                    if pes.get("qk_evac") is not None:
                        t.wait_ge(sv.h, pes["qk_evac"])
                    for c in range(4):
                        t.transpose(ptr[:, c * 128:(c + 1) * 128], qn[pj][:, c * 128:(c + 1) * 128],
                                    W["ident"][:])
                    for c in range(4):
                        tr = t.transpose(ptr[:, 512 + c * 128:512 + (c + 1) * 128],
                                         kn[pj][:, c * 128:(c + 1) * 128], W["ident"][:])
                    tr.then_inc(sp.h, 1)
                    np_tr = sp.bump()
                    prev_evac[pj] = np_tr
                    v.wait_ge(sp.h, np_tr)
                    v.tensor_copy(bass.AP(QT, ptc * 128, [[QT.shape[1], 128], [SEQ, 4], [1, 128]]),
                                  ptr[:, 0:512])
                    v.tensor_copy(bass.AP(KT, ptc * 128, [[KT.shape[1], 128], [SEQ, 4], [1, 128]]),
                                  ptr[:, 512:1024]).then_inc(sv.h, 1)
                    pes["qk_evac"] = sv.bump()
                  if tc < TCH:
                    pend_tr = (tc, j, nv_rope)
                np_qk_done = sp.n
                pes["attn0_qk_done"] = np_qk_done
                _mark(mk, 'a0_qk')
            for hg in range(4):
                h = G * 4 + hg
                sl = h % 2
                pf_vg = pes.get(("attn0_vg_done", sl))
                svg = mk.sem("a0_vg")
                pf_vp = pes.get("attn0_vp_done")
                if pf_vp is not None:
                    g.wait_ge(sp.h, pf_vp)
                src2 = dr["wv"][:, h * 256:(h + 1) * 256].rearrange("(cc p) n -> p cc n", p=128)
                g.dma_start(wvh[sl].ap().rearrange("p (cc n) -> p cc n", n=256), src2).then_inc(svg.h, 16)
                n_wvv = (svg, svg.bump(16))
                pf_g1 = pes.get(("attn0_vg_done", 1 - sl))
                if pf_g1 is not None:
                    g.wait_ge(sp.h, pf_g1)
                src2 = dr["wg0"][:, h * 256:(h + 1) * 256].rearrange("(cc p) n -> p cc n", p=128)
                g.dma_start(wgh[sl].ap().rearrange("p (cc n) -> p cc n", n=256), src2).then_inc(svg.h, 16)
                n_wv = (svg, svg.bump(16))
                if h % 2 == 0:
                    swo = mk.sem("a0_wo")
                    if h >= 2:
                        g.wait_ge(sp.h, pes["attn0_out_done"])
                    src2 = dr["wo"][h * 256:(h + 2) * 256, :].rearrange("(cc p) n -> p cc n", p=128)
                    g.dma_start(wop[(h // 2) % 2].ap().rearrange("p (cc n) -> p cc n", n=1024),
                                src2).then_inc(swo.h, 16)
                    W[("n_wo", h // 2)] = (swo, swo.bump(16))
                if True:
                    t.wait_ge(n_wvv[0].h, n_wvv[1])
                    pf_vhf = pes.get("attn0_vh_free")
                    if pf_vhf is not None:
                        s.wait_ge(sp.h, pf_vhf)
                    n_v_tile = [None] * TCH
                    for tc in range(TCH):
                        j = 0
                        pf = pes.get(("attn0_pv", j))
                        if pf is not None:
                            t.wait_ge(sa.h, pf)
                        for c in range(DCH):
                            mm = t.matmul(pv[j][:], sb_ap(XnT, c * CST + XOFF + tc * 128, [[1, 128]]),
                                          wvh[sl][:, c * 256:(c + 1) * 256],
                                          start=(c == 0), stop=(c == DCH - 1))
                        mm.then_inc(sp.h, 1)
                        npv = sp.bump()
                        s.wait_ge(sp.h, npv)
                        s.activation(Vh[sl][:, tc * 256:(tc + 1) * 256], pv[j][:],
                                     AF.Copy).then_inc(sa.h, 1)
                        n_v_tile[tc] = sa.bump()
                        pes[("attn0_pv", j)] = n_v_tile[tc]
                nv_v_done = sa.n
                pes["attn0_vp_done"] = sp.n
                with ExitStack() as os_:
                    pO = mk.psum("pO", [128, 2048], F32, os_)
                    if W.get("dbgO_d") is not None and h == 0:
                        y.wait_ge(mk.sa.h, nv_v_done)
                        y.dma_start(W["dbgO_d"].ap()[:, 0:128], QT[:, 2 * 128:3 * 128]).then_inc(mk.so.h, 16)
                        y.dma_start(W["dbgO_d"].ap()[:, 128:256], KT[:, 2 * 128:3 * 128]).then_inc(mk.so.h, 16)
                        mk.so.bump(32)
                        y.wait_ge(mk.so.h, mk.so.n)
                    for quarter in range(4):
                        qb0 = quarter * 4
                        qG2 = quarter
                        with ExitStack() as ss_:
                            pS = (mk.psum("pS", [128, 512], F32, ss_), mk.psum("pS", [128, 512], F32, ss_))
                            if pes.get("attn0_pO_free") is not None:
                                t.wait_ge(sv.h, pes["attn0_pO_free"])
                            def emit_O(kb, jj, nv_S):
                                t.wait_ge(nv_S[0], nv_S[1])
                                t.wait_ge(sa.h, n_v_tile[kb])
                                stop_mm = None
                                mm = None
                                for ql in range(4):
                                    qb = qb0 + ql
                                    if qb < kb:
                                        continue
                                    mm = t.matmul(pO[:, ql * 512:ql * 512 + 256],
                                                  Sb[jj][:, ql * 128:(ql + 1) * 128],
                                                  Vh[sl][:, kb * 256:(kb + 1) * 256],
                                                  start=(kb == 0), stop=(kb == qb))
                                    if qb == kb:
                                        stop_mm = mm
                                if stop_mm is mm:
                                    mm.then_inc(sp.h, 2)
                                    sp.bump(2)
                                    W[("np_O", kb)] = sp.n
                                else:
                                    if stop_mm is not None:
                                        stop_mm.then_inc(sp.h, 1)
                                        W[("np_O", kb)] = sp.bump()
                                    mm.then_inc(sp.h, 1)
                                    sp.bump()
                                pes[("attn0_Sb1", jj)] = sp.n

                            pend = None
                            for kb in range(qb0 + 4):
                                jj = kb % 2
                                diag = (kb // 4 == qG2)
                                eng, sem_e = (v, sv) if diag else (s, sa)
                                pf = pes.get(("attn0_pS", jj))
                                if pf is not None:
                                    t.wait_ge(pf[0], pf[1])
                                t.matmul(pS[jj][:], KT[:, hg * SEQ + kb * 128:hg * SEQ + (kb + 1) * 128],
                                         QT[:, hg * SEQ + qG2 * 512:hg * SEQ + (qG2 + 1) * 512],
                                         start=True, stop=True).then_inc(sp.h, 1)
                                npS = sp.bump()
                                eng.wait_ge(sp.h, npS)
                                pf2 = pes.get(("attn0_Sb1", jj))
                                if pf2 is not None:
                                    eng.wait_ge(sp.h, pf2)
                                if diag:
                                    cp = v.tensor_copy(Sb[jj][:], pS[jj][:])
                                    off = (kb - qG2 * 4) * 128
                                    cp = v.tensor_tensor(Sb[jj][:, off:off + 128], Sb[jj][:, off:off + 128],
                                                         mask01[:], ALU.mult)
                                else:
                                    cp = s.activation(Sb[jj][:], pS[jj][:], AF.Copy)
                                cp.then_inc(sem_e.h, 1)
                                nv_S = (sem_e.h, sem_e.bump())
                                pes[("attn0_pS", jj)] = nv_S
                                if pend is not None:
                                    emit_O(*pend)
                                pend = (kb, jj, nv_S)
                            emit_O(*pend)
                        with ExitStack() as ds_:
                            pG = (mk.psum("pG", [128, 256], F32, ds_), mk.psum("pG", [128, 256], F32, ds_))
                            pogt = mk.psum("pogt", [128, 256], BF16, ds_)
                            pf3 = pes.get("attn0_sqs")
                            if pf3 is not None:
                                s.wait_ge(sv.h, pf3)
                            for ql in range(4):
                                qb = qb0 + ql
                                s.wait_ge(sp.h, W[("np_O", qb)])
                                sqop = s.activation(rsc[:, 0:256], pO[:, ql * 512:ql * 512 + 256],
                                                    AF.Square, scale=1.0 / 16.0,
                                                    accum_out=stA[:, ql:ql + 1])
                            sqop.then_inc(sa.h, 1)
                            na = sa.bump()
                            v.wait_ge(sa.h, na)
                            v.tensor_copy(stA[:, 4:8], stA[:, 0:4]).then_inc(sv.h, 1)
                            nv = sv.bump()
                            s.wait_ge(sv.h, nv)
                            s.activation(stA[:, 8:12], stA[:, 4:8], AF.Ln,
                                         bias=mk.epscol[:, 0:1]).then_inc(sa.h, 1)
                            na = sa.bump()
                            v.wait_ge(sa.h, na)
                            v.tensor_copy(stA[:, 12:16], stA[:, 8:12]).then_inc(sv.h, 1)
                            nv = sv.bump()
                            s.wait_ge(sv.h, nv)
                            s.activation(stA[:, 4:8], stA[:, 12:16], AF.Exp,
                                         scale=-0.5).then_inc(sa.h, 1)
                            na_rstd = sa.bump()
                            v.wait_ge(sa.h, na_rstd)
                            t.wait_ge(n_wv[0].h, n_wv[1])
                            for ql in range(4):
                                qb = qb0 + ql
                                j = qb % 2
                                pf = pes.get(("attn0_pG", j))
                                if pf is not None:
                                    t.wait_ge(sv.h, pf)
                                for c in range(DCH):
                                    mm = t.matmul(pG[j][:], sb_ap(XnT, c * CST + XOFF + qb * 128, [[1, 128]]),
                                                  wgh[sl][:, c * 256:(c + 1) * 256],
                                                  start=(c == 0), stop=(c == DCH - 1))
                                mm.then_inc(sp.h, 1)
                                npG = sp.bump()
                                s.wait_ge(sp.h, npG)
                                if ql >= 1:
                                    s.wait_ge(sv.h, nv_og)
                                s.activation(rsc[:, 0:256], pG[j][:], AF.Silu).then_inc(sa.h, 1)
                                na_sg = sa.bump()
                                v.wait_ge(sa.h, na_sg)

                                v.tensor_mul(rsc[:, 256:512], rsc[:, 0:256], gnwb[:])
                                v.scalar_tensor_tensor(ogt[:], pO[:, ql * 512:ql * 512 + 256],
                                                       stA[:, 4 + ql:5 + ql], rsc[:, 256:512], ALU.mult,
                                                       ALU.mult).then_inc(sv.h, 1)
                                nv_og = sv.bump()
                                pes[("attn0_pG", j)] = nv_og
                                pes["attn0_sqs"] = nv_og
                                t.wait_ge(sv.h, nv_og)
                                pf4 = pes.get("attn0_pogt")
                                if pf4 is not None:
                                    t.wait_ge(sv.h, pf4)
                                for c in range(2):
                                    tr = t.transpose(pogt[:, c * 128:(c + 1) * 128],
                                                     ogt[:, c * 128:(c + 1) * 128], W["ident"][:])
                                tr.then_inc(sp.h, 1)
                                np_tr = sp.bump()
                                v.wait_ge(sp.h, np_tr)
                                v.tensor_copy(bass.AP(ogT, ((h % 2) * 2) * SEQ + qb * 128,
                                                      [[ogT.shape[1], 128], [SEQ, 2], [1, 128]]),
                                              pogt[:]).then_inc(sv.h, 1)
                                pes["attn0_pogt"] = sv.bump()
                            pes["attn0_pO_free"] = sv.n
                pes[("attn0_vg_done", sl)] = sp.n
                pes["attn0_vh_free"] = W[("np_O", 15)]
                _mark(mk, 'a0_head')
                if h % 2 == 1:
                    nv_ogT = sv.n
                    with ExitStack() as us_:
                        pU = (mk.psum("pU", [128, 1024], F32, us_), mk.psum("pU", [128, 1024], F32, us_))
                        t.wait_ge(sv.h, nv_ogT)
                        t.wait_ge(W[("n_wo", h // 2)][0].h, W[("n_wo", h // 2)][1])
                        for tc in range(TCH):
                            j = tc % 2
                            pf = pes.get(("attn0_pU", j))
                            if pf is not None:
                                t.wait_ge(sv.h, pf)
                            for cc in range(4):
                                for dn in range(2):
                                    mm = t.matmul(pU[j][:, dn * 512:(dn + 1) * 512],
                                                  bass.AP(ogT, cc * SEQ + tc * 128,
                                                          [[ogT.shape[1], 128], [1, 128]]),
                                                  wop[(h // 2) % 2][:, cc * 1024 + dn * 512:cc * 1024 + (dn + 1) * 512],
                                                  start=(cc == 0), stop=(cc == 3))
                            mm.then_inc(sp.h, 1)
                            npU = sp.bump()
                            xt = X[:, tc * 1024:(tc + 1) * 1024]
                            v.wait_ge(sp.h, npU)
                            v.tensor_tensor(xt, xt, pU[j][:], ALU.add).then_inc(sv.h, 1)
                            nvU = sv.bump()
                            pes[("attn0_pU", j)] = nvU
                            if h == 7:
                                xtl.append(nvU)
                    pes["attn0_out_done"] = sp.n
                    if h == 7:
                        pes["x_tile"] = xtl
    pes["psum_free"] = (sv.h, sv.n)
    pes["sbuf_free"] = (sv.h, sv.n)
    return sv.n


def alloc_attn0(mk, es=None):
    return {}, dict(mask01=mk.sbuf("mask01sb", [128, 128], BF16, es))


def queue_attn0_consts(mk, W, dr):
    W["dr"] = dr
    y = mk.y
    y.dma_start(W["mask01"][:], dr["mask01"]).then_inc(mk.sd.h, 16)
    W["n_const"] = mk.sd.bump(16)


def emit_ln_inplace(mk, X, lnb, x_ready=None):
    v, s = mk.v, mk.s
    sv, sa = mk.sv, mk.sa
    sq, st = lnb["sq"], lnb["st"]
    x_tile = lnb.pop("x_tile", None)
    if x_ready is not None and x_tile is None:
        s.wait_ge(x_ready[0], x_ready[1])
        v.wait_ge(x_ready[0], x_ready[1])
    na_rstd = emit_ln_stats(mk, X, sq, st, x_tile=x_tile)
    v.wait_ge(sa.h, na_rstd)
    out_tiles = []
    for tc in range(TCH):
        xt = X[:, tc * 1024:(tc + 1) * 1024]
        v.tensor_scalar(xt, xt, st[:, tc:tc + 1], st[:, 32 + tc:33 + tc],
                        ALU.subtract, ALU.mult).then_inc(sv.h, 1)
        out_tiles.append(sv.bump())
    lnb["out_tiles"] = out_tiles
    return sv.n


def emit_add_from_dram(mk, X, src_tiled, pes):
    v, y = mk.v, mk.y
    sv = mk.sv
    sx2 = mk.sem("addx")
    nv_entry = sv.n
    xtl_a = []
    with ExitStack() as ps:
        bufs2 = (mk.sbuf("addbuf", [128, 1024], F32, ps), mk.sbuf("addbuf", [128, 1024], F32, ps))
        prev = [nv_entry, nv_entry]
        for tc in range(TCH):
            j = tc % 2
            if prev[j] is not None:
                y.wait_ge(sv.h, prev[j])
            y.dma_start(bufs2[j][:], src_tiled[:, tc, :]).then_inc(sx2.h, 16)
            n = sx2.bump(16)
            v.wait_ge(sx2.h, n)
            v.tensor_tensor(X[:, tc * 1024:(tc + 1) * 1024],
                            X[:, tc * 1024:(tc + 1) * 1024], bufs2[j][:],
                            ALU.add).then_inc(sv.h, 1)
            prev[j] = sv.bump()
            xtl_a.append(prev[j])
    pes["x_tile"] = xtl_a
    return sv.n


def emit_blk1(mk, X, XnT, W, bufs, pes, lnb):
    t, v, s, g, y = mk.t, mk.v, mk.s, mk.g, mk.y
    sp, sv, sa, sw, sd = mk.sp, mk.sv, mk.sa, mk.sw, mk.sd
    dr = W["dr"]
    ident = W["ident"]
    maskneg = W["maskneg"]
    onescol_r = W["onescol_r"]
    b1pc = W["b1pc"]
    with ExitStack() as ps:
        QT = mk.sbuf("QT1", [128, 4 * SEQ], BF16, ps)
        KT = mk.sbuf("KT1", [128, 4 * SEQ], BF16, ps)
        Vg = mk.sbuf("Vg1", [128, TCH * 512], BF16, ps)
        qn1 = (mk.sbuf("qn1", [128, 512], BF16, ps),
               mk.sbuf("qn1", [128, 512], BF16, ps))
        kn1 = (mk.sbuf("kn1", [128, 512], BF16, ps),
               mk.sbuf("kn1", [128, 512], BF16, ps))
        tab_one = mk.sbuf("tab1", [128, 1024], F32, ps)
        tabs = (tab_one, tab_one)
        rsc = bufs["lnsq"]
        for G in range(2):
            sqk = mk.sem("b1_qkv")
            if G == 0 and pes.get("sbuf_free") is not None:
                g.wait_ge(*pes["sbuf_free"])
            if G == 1:
                g.wait_ge(sv.h, pes["b1_attn_done"])
            with ExitStack() as ws_:
                wq = mk.sbuf("wq1", [128, DCH * 512], BF16, ws_)
                wk = mk.sbuf("wk1", [128, DCH * 512], BF16, ws_)
                wv = mk.sbuf("wv1", [128, DCH * 512], BF16, ws_)
                bias_r = mk.sbuf("bqkv_r", [1, 1536], F32R, ws_)
                for i3, (wsb, coff) in enumerate(((wq, G * 512), (wk, 1024 + G * 512),
                                                  (wv, 2048 + G * 512))):
                    src2 = dr["wqkv"][:, coff:coff + 512].rearrange("(cc p) n -> p cc n", p=128)
                    g.dma_start(wsb.ap().rearrange("p (cc n) -> p cc n", n=512), src2).then_inc(sqk.h, 16)
                    sqk.bump(16)
                    g.dma_start(bias_r[:, i3 * 512:(i3 + 1) * 512],
                                dr["bias_row"][:, coff:coff + 512]).then_inc(sqk.h, 16)
                    sqk.bump(16)
                n_wqkv = (sqk, sqk.n)
                with ExitStack() as qs:
                    pq_ = (mk.psum("pq1", [128, 512], F32, qs),
                           mk.psum("pq1", [128, 512], F32, qs))
                    pk_ = (mk.psum("pk1", [128, 512], F32, qs),
                           mk.psum("pk1", [128, 512], F32, qs))
                    pv = mk.psum("pv1", [128, 512], F32, qs)
                    ptr = mk.psum("ptr1", [128, 1024], BF16, qs)
                    t.wait_ge(n_wqkv[0].h, n_wqkv[1])
                    if pes.get("psum_free") is not None:
                        t.wait_ge(*pes["psum_free"])
                    prev_rope = [None, None]
                    prev_evac = [None, None]
                    prev_v = None
                    pend_tr = None
                    for tc in range(TCH + 1):
                      if tc < TCH:
                        j = tc % 2
                        stab = mk.sem("b1_tab")
                        tgate = pes.get("b1_tab") or 0
                        if pes.get("sbuf_free") is not None:
                            tgate = max(tgate, pes["sbuf_free"][1])
                        if tgate:
                            y.wait_ge(sv.h, tgate)
                        y.dma_start(tabs[j][:], dr["tabs1"][tc * 128:(tc + 1) * 128, :]).then_inc(stab.h, 16)
                        n_tab = (stab, stab.bump(16))
                        if prev_rope[j] is not None:
                            t.wait_ge(sv.h, prev_rope[j])
                        for i3, pp in ((0, pq_[j]), (1, pk_[j])):
                            for c in range(DCH):
                                mm = t.matmul(pp[:], sb_ap(XnT, c * CST + XOFF + tc * 128, [[1, 128]]),
                                              (wq if i3 == 0 else wk)[:, c * 512:(c + 1) * 512],
                                              start=(c == 0), stop=False)
                            mm2 = t.matmul(pp[:], onescol_r[0:1, :],
                                           bias_r[0:1, i3 * 512:(i3 + 1) * 512], start=False, stop=True)
                            mm2.then_inc(sp.h, 1)
                            sp.bump()
                        npq = sp.n - 1
                        npk = sp.n
                        if prev_v is not None:
                            t.wait_ge(sv.h, prev_v)
                        for c in range(DCH):
                            mm = t.matmul(pv[:], sb_ap(XnT, c * CST + XOFF + tc * 128, [[1, 128]]),
                                          wv[:, c * 512:(c + 1) * 512],
                                          start=(c == 0), stop=False)
                        mm2 = t.matmul(pv[:], onescol_r[0:1, :],
                                       bias_r[0:1, 1024:1536],
                                       start=False, stop=True)
                        mm2.then_inc(sp.h, 1)
                        npv = sp.bump()
                        v.wait_ge(sp.h, npv)
                        pvf = pes.get("b1_Vg")
                        if pvf is not None and tc == 0 and G == 0:
                            v.wait_ge(sp.h, pvf)
                        v.tensor_copy(Vg[:, tc * 512:(tc + 1) * 512], pv[:]).then_inc(sv.h, 1)
                        nv_v = sv.bump()
                        prev_v = nv_v
                        v.wait_ge(n_tab[0].h, n_tab[1])
                        v.wait_ge(sp.h, npk)
                        if prev_evac[j] is not None:
                            v.wait_ge(sp.h, prev_evac[j])
                        for (pp, dst, tb) in ((pq_[j], qn1[j], 0), (pk_[j], kn1[j], 512)):
                            p0 = bass.AP(pp, 0, [[512, 128], [64, 8], [1, 32]])
                            p1 = bass.AP(pp, 32, [[512, 128], [64, 8], [1, 32]])
                            cost = bass.AP(tabs[j], tb, [[tabs[j].shape[1], 128], [32, 8], [1, 32]])
                            sint = bass.AP(tabs[j], tb + 256, [[tabs[j].shape[1], 128], [32, 8], [1, 32]])
                            t1 = bass.AP(rsc, 0, [[rsc.shape[1], 128], [32, 8], [1, 32]])
                            t2 = bass.AP(rsc, 256, [[rsc.shape[1], 128], [32, 8], [1, 32]])
                            t3 = bass.AP(rsc, 512, [[rsc.shape[1], 128], [32, 8], [1, 32]])
                            t4 = bass.AP(rsc, 768, [[rsc.shape[1], 128], [32, 8], [1, 32]])
                            d0 = bass.AP(dst, 0, [[dst.shape[1], 128], [64, 8], [1, 32]])
                            d1 = bass.AP(dst, 32, [[dst.shape[1], 128], [64, 8], [1, 32]])
                            v.tensor_tensor(t1, p1, sint, ALU.mult)
                            v.tensor_tensor(t2, p0, cost, ALU.mult)
                            v.tensor_tensor(t3, p0, sint, ALU.mult)
                            v.tensor_tensor(t4, p1, cost, ALU.mult)
                            v.tensor_sub(d0, t2, t1)
                            last_rope = v.tensor_add(d1, t4, t3)
                        last_rope.then_inc(sv.h, 1)
                        nv_rope = sv.bump()
                        prev_rope[j] = nv_rope
                        pes["b1_tab"] = nv_rope
                      if pend_tr is not None:
                        ptc, pj, pnv_rope = pend_tr
                        t.wait_ge(sv.h, pnv_rope)
                        if pes.get("b1_qk_evac") is not None:
                            t.wait_ge(sv.h, pes["b1_qk_evac"])
                        for c in range(4):
                            t.transpose(ptr[:, c * 128:(c + 1) * 128], qn1[pj][:, c * 128:(c + 1) * 128], ident[:])
                        for c in range(4):
                            tr = t.transpose(ptr[:, 512 + c * 128:512 + (c + 1) * 128],
                                             kn1[pj][:, c * 128:(c + 1) * 128], ident[:])
                        tr.then_inc(sp.h, 1)
                        np_tr = sp.bump()
                        prev_evac[pj] = np_tr
                        v.wait_ge(sp.h, np_tr)
                        v.tensor_copy(bass.AP(QT, ptc * 128, [[QT.shape[1], 128], [SEQ, 4], [1, 128]]),
                                      ptr[:, 0:512])
                        v.tensor_copy(bass.AP(KT, ptc * 128, [[KT.shape[1], 128], [SEQ, 4], [1, 128]]),
                                      ptr[:, 512:1024]).then_inc(sv.h, 1)
                        pes["b1_qk_evac"] = sv.bump()
                      if tc < TCH:
                        pend_tr = (tc, j, nv_rope)
                    pes["b1_qk_done"] = sp.n
            _mark(mk, 'b1_qkv')
            AoT = mk.sbuf("AoT1", [128, 4 * SEQ], BF16, ps) if G == 0 else W["_AoT"]
            if G == 0:
                W["_AoT"] = AoT
            nv_qkv_done = sv.n
            with ExitStack() as as_:
                P_sb = (mk.sbuf("P1", [128, SEQ], BF16, as_),
                        mk.sbuf("P1", [128, SEQ], BF16, as_))
                PT_sb = (mk.sbuf("PT1", [128, SEQ], BF16, as_),
                         mk.sbuf("PT1", [128, SEQ], BF16, as_))
                Ao = mk.sbuf("Ao1", [128, 64], BF16, as_)
                stB = mk.sbuf("stB", [128, 8], F32, as_)
                pS = mk.psum("pS1", [128, 2048], F32, as_)
                pPT = mk.psum("pPT1", [128, 1024], BF16, as_)
                pO = mk.psum("pO1", [128, 512], F32, as_)
                pAoT = mk.psum("pAoT1", [128, 1024], BF16, as_)
                t.wait_ge(sv.h, nv_qkv_done)
                it_list = [(hl, qb) for hl in range(8) for qb in range(TCH)]
                NIT = len(it_list)
                pipe = {}
                for ii in range(NIT + 2):
                    if ii < NIT:
                        hl, qb = it_list[ii]
                        ch = hl // 2
                        rb = (hl % 2) * 64
                        jP = ii % 2
                        L = (qb + 1) * 128
                        if qb <= 7:
                            u = pipe.get("u", 0)
                            pipe["u"] = 1 - u
                            soff = u * 1024
                            pf = pipe.get(("exp_u", u))
                        else:
                            u = None
                            soff = 0
                            pf = pes.get("b1_pS")
                        if pf is not None:
                            t.wait_ge(sa.h, pf)
                        for kG in range(qb // 4 + 1):
                            kw = min(512, L - kG * 512)
                            t.matmul(pS[:, soff + kG * 512:soff + kG * 512 + kw],
                                     QT[rb:rb + 64, ch * SEQ + qb * 128:ch * SEQ + (qb + 1) * 128],
                                     KT[rb:rb + 64, ch * SEQ + kG * 512:ch * SEQ + kG * 512 + kw],
                                     start=True, stop=(kG != qb // 4))
                        mmm = t.matmul(pS[:, soff + qb * 128:soff + (qb + 1) * 128], ident[:], maskneg[:],
                                       start=False, stop=True)
                        mmm.then_inc(sp.h, 1)
                        npS = sp.bump()
                        v.wait_ge(sp.h, npS)
                        v.reduce_max(stB[:, 5 + jP:6 + jP], pS[:, soff:soff + L],
                                     AX.X, negate=True).then_inc(sv.h, 1)
                        nv1 = sv.bump()
                        s.wait_ge(sv.h, nv1)
                        pf2 = pipe.get(("np_tr", jP))
                        if pf2 is not None:
                            s.wait_ge(sp.h, pf2)
                        s.activation(P_sb[jP][:, 0:L], pS[:, soff:soff + L], AF.Exp,
                                     bias=stB[:, 5 + jP:6 + jP],
                                     accum_out=stB[:, 1 + jP:2 + jP]).then_inc(sa.h, 1)
                        na_exp = sa.bump()
                        if u is None:
                            pipe[("exp_u", 0)] = na_exp
                            pipe[("exp_u", 1)] = na_exp
                        else:
                            pipe[("exp_u", u)] = na_exp
                        pes["b1_pS"] = na_exp
                        pipe[("it", ii)] = dict(hl=hl, qb=qb, jP=jP, na_exp=na_exp)
                    if 0 <= ii - 1 < NIT:
                        d = pipe[("it", ii - 1)]
                        jP = d["jP"]
                        nkb = d["qb"] + 1
                        t.wait_ge(sa.h, d["na_exp"])
                        nv_pt = None
                        np_tr_last = None
                        for b0 in range(0, nkb, 8):
                            bn = min(8, nkb - b0)
                            pf3 = pes.get("b1_pPT")
                            if pf3 is not None:
                                t.wait_ge(pf3[0], pf3[1])
                            for c in range(bn):
                                tr = t.transpose(pPT[:, c * 128:(c + 1) * 128],
                                                 P_sb[jP][:, (b0 + c) * 128:(b0 + c + 1) * 128], ident[:])
                            tr.then_inc(sp.h, 1)
                            np_pt = sp.bump()
                            np_tr_last = np_pt
                            v.wait_ge(sp.h, np_pt)
                            pf4 = pipe.get(("np_O", jP))
                            if pf4 is not None:
                                v.wait_ge(sp.h, pf4)
                            v.tensor_copy(PT_sb[jP][:, b0 * 128:(b0 + bn) * 128],
                                          pPT[:, 0:bn * 128]).then_inc(sv.h, 1)
                            nv_pt = sv.bump()
                            pes["b1_pPT"] = (sv.h, nv_pt)
                        pipe[("np_tr", jP)] = np_tr_last
                        d["nv_pt"] = nv_pt
                        v.wait_ge(sa.h, d["na_exp"])
                        pfq = pipe.get(("nq_rs", jP))
                        if pfq is not None:
                            v.wait_ge(mk.sq.h, pfq)
                        v.reciprocal(stB[:, 3 + jP:4 + jP], stB[:, 1 + jP:2 + jP]).then_inc(sv.h, 1)
                        nv_rs = sv.bump()
                        bcol = 0 if jP == 0 else 7
                        g.wait_ge(sv.h, nv_rs)
                        pfb = pipe.get(("nv_ao", jP))
                        if pfb is not None:
                            g.wait_ge(sv.h, pfb)
                        g.tensor_copy(stB[:, bcol:bcol + 1],
                                      stB[:, 3 + jP:4 + jP]).then_inc(mk.sq.h, 1)
                        d["nq_rs"] = mk.sq.bump()
                        pipe[("nq_rs", jP)] = d["nq_rs"]
                    if ii >= 2:
                        d = pipe.pop(("it", ii - 2))
                        hl, qb, jP = d["hl"], d["qb"], d["jP"]
                        ch = hl // 2
                        rb = (hl % 2) * 64
                        nkb = qb + 1
                        t.wait_ge(sv.h, d["nv_pt"])
                        pfa = pipe.get("nv_ao_last")
                        if pfa is not None:
                            t.wait_ge(sv.h, pfa)
                        for kb in range(nkb):
                            mm = t.matmul(pO[:, 0:64], PT_sb[jP][:, kb * 128:(kb + 1) * 128],
                                          Vg[:, kb * 512 + hl * 64:kb * 512 + hl * 64 + 64],
                                          start=(kb == 0), stop=(kb == nkb - 1))
                        mm.then_inc(sp.h, 1)
                        np_O = sp.bump()
                        pipe[("np_O", jP)] = np_O
                        v.wait_ge(sp.h, np_O)
                        v.wait_ge(mk.sq.h, d["nq_rs"])
                        pf5 = pipe.get("np_aot_last")
                        if pf5 is not None:
                            v.wait_ge(sp.h, pf5)
                        bcol = 0 if jP == 0 else 7
                        v.tensor_scalar_mul(Ao[:], pO[:, 0:64],
                                            stB[:, bcol:bcol + 1]).then_inc(sv.h, 1)
                        nv_ao = sv.bump()
                        pipe[("nv_ao", jP)] = nv_ao
                        pipe["nv_ao_last"] = nv_ao
                        t.wait_ge(sv.h, nv_ao)
                        pf6 = pes.get("b1_pAoT")
                        if pf6 is not None:
                            t.wait_ge(sv.h, pf6)
                        tr = t.transpose(pAoT[rb:rb + 64, (qb % 8) * 128:(qb % 8 + 1) * 128],
                                         Ao[:], ident[:])
                        tr.then_inc(sp.h, 1)
                        np_aot = sp.bump()
                        pipe["np_aot_last"] = np_aot
                        if qb % 8 == 7:
                            v.wait_ge(sp.h, np_aot)
                            v.tensor_copy(AoT[rb:rb + 64, ch * SEQ + (qb - 7) * 128:ch * SEQ + (qb + 1) * 128],
                                          pAoT[rb:rb + 64, 0:1024]).then_inc(sv.h, 1)
                            pes["b1_pAoT"] = sv.bump()
                        if qb == TCH - 1:
                            _mark(mk, 'b1_hl')
            nv_aot = sv.n
            with ExitStack() as us_:
                wo = mk.sbuf("wo1", [128, 4 * 1024], BF16, us_)
                bias_r = mk.sbuf("bo_r", [1, 1024], F32R, us_)
                swo = mk.sem("b1_wo")
                g.wait_ge(sv.h, nv_aot)
                g.wait_ge(sp.h, pes["b1_qk_done"])
                src2 = dr["wo1"][G * 512:(G + 1) * 512, :].rearrange("(cc p) n -> p cc n", p=128)
                g.dma_start(wo.ap().rearrange("p (cc n) -> p cc n", n=1024), src2).then_inc(swo.h, 16)
                swo.bump(16)
                if G == 1:
                    g.dma_start(bias_r[:], dr["bias_row"][:, 3072:4096]).then_inc(swo.h, 16)
                    swo.bump(16)
                n_wo = (swo, swo.n)
                pU = (mk.psum("pU1", [128, 1024], F32, us_), mk.psum("pU1", [128, 1024], F32, us_))
                t.wait_ge(sv.h, nv_aot)
                t.wait_ge(n_wo[0].h, n_wo[1])
                for tc in range(TCH):
                    jj = tc % 2
                    pf = pes.get(("b1_pU", jj))
                    if pf is not None:
                        t.wait_ge(sv.h, pf)
                    for cc in range(4):
                        for dn in range(2):
                            mm = t.matmul(pU[jj][:, dn * 512:(dn + 1) * 512],
                                          AoT[:, cc * SEQ + tc * 128:cc * SEQ + (tc + 1) * 128],
                                          wo[:, cc * 1024 + dn * 512:cc * 1024 + (dn + 1) * 512],
                                          start=(cc == 0), stop=(G == 0 and cc == 3))
                    if G == 1:
                        for dn in range(2):
                            mm = t.matmul(pU[jj][:, dn * 512:(dn + 1) * 512], onescol_r[0:1, :],
                                          bias_r[0:1, dn * 512:(dn + 1) * 512],
                                          start=False, stop=True)
                    mm.then_inc(sp.h, 1)
                    npU = sp.bump()
                    xt = X[:, tc * 1024:(tc + 1) * 1024]
                    v.wait_ge(sp.h, npU)
                    v.tensor_tensor(xt, xt, pU[jj][:], ALU.add).then_inc(sv.h, 1)
                    pes[("b1_pU", jj)] = sv.bump()
            pes["b1_Vg"] = sp.n
            pes["b1_attn_done"] = sv.n
            _mark(mk, 'b1_oproj')
    if W.get("dbg_x1") is not None:
        y.wait_ge(sv.h, sv.n)
        y.dma_start(W["dbg_x1"].ap().rearrange("p (tc d) -> p tc d", d=1024),
                    X.ap().rearrange("p (tc d) -> p tc d", d=1024)).then_inc(mk.so.h, 16)
        mk.so.bump(16)
        y.wait_ge(mk.so.h, mk.so.n)
    nv_pre_mlp = sv.n
    pes["psum_free"] = (sv.h, sv.n)
    lnb["xnt_free"] = (mk.sp, mk.sp.n)
    W["n_gb_users"] = sv.n
    with ExitStack() as lps:
        gb2 = load_gamma_beta(mk, W, dr["ln2_g_row"], dr["ln2_b_row"], lps)
        lnb["x_ready"] = (sv.h, nv_pre_mlp)
        lnb["gscr"] = mk.sbuf("gscr", [128, 1024], F32, lps)
        lnb["ptA"] = mk.psum("ptA", [128, 1024], BF16, lps)
        lnb["ptB"] = mk.psum("ptB", [128, 1024], BF16, lps)
        if pes.get("psum_free") is not None:
            mk.t.wait_ge(*pes["psum_free"])
        nv_ln = emit_ln_transpose(mk, X, XnT, lnb, gamma_beta=gb2)
        _mark(mk, 'b1_ln2')
        lnb.pop("gscr")
    pes["psum_free"] = (mk.sv.h, nv_ln)
    with ExitStack() as ms_:
        w1l = (mk.sbuf("w1l", [128, DCH * 1024], BF16, ms_), mk.sbuf("w1l", [128, DCH * 1024], BF16, ms_))
        w2l = (mk.sbuf("w2l", [128, DCH * 1024], BF16, ms_), mk.sbuf("w2l", [128, DCH * 1024], BF16, ms_))
        hT_one = mk.sbuf("hTl", [128, DCH * SEQ], BF16, ms_)
        hTl = (hT_one, hT_one)
        b2_r = mk.sbuf("b2_r", [1, 1024], F32R, ms_)
        sb2 = mk.sem("b1_b2")
        g.dma_start(b2_r[:], dr["bias_row"][:, 4096:5120]).then_inc(sb2.h, 16)
        n_b2 = (sb2, sb2.bump(16))
        s_w1 = (mk.sem("b1_w1A"), mk.sem("b1_w1B"))
        s_w2 = (mk.sem("b1_w2A"), mk.sem("b1_w2B"))
        n_w1 = [None] * 4
        n_w2 = [None] * 4
        np_h_end = [None] * 4
        np_y_end = [None] * 4
        nv_h_end = [None] * 4

        def queue_slice_dma(sl_, dep, vdep=None):
            j = sl_ % 2
            if dep is not None:
                g.wait_ge(sp.h, dep)
            if vdep is not None:
                g.wait_ge(sv.h, vdep)
            src2 = dr["w1l"][:, sl_ * 1024:(sl_ + 1) * 1024].rearrange("(cc p) n -> p cc n", p=128)
            g.dma_start(w1l[j].ap().rearrange("p (cc n) -> p cc n", n=1024), src2).then_inc(s_w1[j].h, 16)
            n_w1[sl_] = (s_w1[j], s_w1[j].bump(16))
            src3 = dr["w2l"][sl_ * 1024:(sl_ + 1) * 1024, :].rearrange("(cc p) n -> p cc n", p=128)
            g.dma_start(w2l[j].ap().rearrange("p (cc n) -> p cc n", n=1024), src3).then_inc(s_w2[j].h, 16)
            n_w2[sl_] = (s_w2[j], s_w2[j].bump(16))

        queue_slice_dma(0, pes["b1_Vg"], nv_ln)
        queue_slice_dma(1, None)
        with ExitStack() as ms2_:
            ph = (mk.psum("ph1", [128, 512], F32, ms2_), mk.psum("ph1", [128, 512], F32, ms2_))
            pY = (mk.psum("pY1", [128, 1024], F32, ms2_), mk.psum("pY1", [128, 1024], F32, ms2_))
            t.wait_ge(sv.h, nv_ln)

            def h_pass(sl_):
                j = sl_ % 2
                t.wait_ge(n_w1[sl_][0].h, n_w1[sl_][1])
                first_v = [True]
                for tq in range(SEQ // 512):
                    for hc in range(DCH):
                        jj = (tq * DCH + hc) % 2
                        pf = pes.get(("b1_ph", jj))
                        if pf is not None:
                            t.wait_ge(sa.h, pf)
                        for c in range(DCH):
                            mm = t.matmul(ph[jj][:], w1l[j][:, c * 1024 + hc * 128:c * 1024 + (hc + 1) * 128],
                                          sb_ap(XnT, c * CST + XOFF + tq * 512, [[1, 512]]),
                                          start=(c == 0), stop=(c == DCH - 1))
                        mm.then_inc(sp.h, 1)
                        nph = sp.bump()
                        s.wait_ge(sp.h, nph)
                        if first_v[0] and sl_ >= 1:
                            s.wait_ge(sp.h, np_y_end[sl_ - 1])
                            first_v[0] = False
                        s.activation(sb_ap(hTl[j], hc * SEQ + tq * 512, [[1, 512]]),
                                     ph[jj][:], AF.Gelu,
                                     bias=b1pc[:, sl_ * 8 + hc:sl_ * 8 + hc + 1]).then_inc(sa.h, 1)
                        na = sa.bump()
                        pes[("b1_ph", jj)] = na
                return sp.n, sa.n

            def y_pass(sl_):
                j = sl_ % 2
                t.wait_ge(sa.h, nv_h_end[sl_])
                t.wait_ge(n_w2[sl_][0].h, n_w2[sl_][1])
                if sl_ == 3:
                    t.wait_ge(n_b2[0].h, n_b2[1])
                for tc in range(TCH):
                    jj = tc % 2
                    pf = pes.get(("b1_pY", jj))
                    if pf is not None:
                        t.wait_ge(sv.h, pf)
                    for kc in range(DCH):
                        for dn in range(2):
                            mm = t.matmul(pY[jj][:, dn * 512:(dn + 1) * 512],
                                          sb_ap(hTl[j], kc * SEQ + tc * 128, [[1, 128]]),
                                          w2l[j][:, kc * 1024 + dn * 512:kc * 1024 + (dn + 1) * 512],
                                          start=(kc == 0), stop=(sl_ != 3 and kc == DCH - 1))
                    if sl_ == 3:
                        for dn in range(2):
                            mm = t.matmul(pY[jj][:, dn * 512:(dn + 1) * 512], onescol_r[0:1, :],
                                          b2_r[0:1, dn * 512:(dn + 1) * 512],
                                          start=False, stop=True)
                    mm.then_inc(sp.h, 1)
                    npY = sp.bump()
                    xt = X[:, tc * 1024:(tc + 1) * 1024]
                    v.wait_ge(sp.h, npY)
                    v.tensor_tensor(xt, xt, pY[jj][:], ALU.add).then_inc(sv.h, 1)
                    pes[("b1_pY", jj)] = sv.bump()
                return sp.n

            for sl_ in range(4):
                _, nah = h_pass(sl_)
                nv_h_end[sl_] = nah
                if sl_ == 3 and W.get("dbg_hT") is not None:
                    y.wait_ge(sa.h, nah)
                    y.dma_start(W["dbg_hT"].ap(), hTl[sl_ % 2][:, :]).then_inc(mk.so.h, 16)
                    mk.so.bump(16)
                    y.wait_ge(mk.so.h, mk.so.n)
                np_y_end[sl_] = y_pass(sl_)
                if sl_ + 2 < 4:
                    queue_slice_dma(sl_ + 2, np_y_end[sl_])
    pes["psum_free"] = (sv.h, sv.n)
    pes["sbuf_free"] = (sv.h, sv.n)
    return sv.n


def load_gamma_beta(mk, W, g_row, b_row, es):
    gt = mk.sbuf("lngt", [128, 1024], F32, es)
    bt = mk.sbuf("lnbt", [128, 1024], F32, es)
    sgb = mk.sem("b1_gb")
    if W.get("n_gb_users") is not None:
        mk.y.wait_ge(mk.sv.h, W["n_gb_users"])
    mk.y.dma_start(gt[:], bass.AP(g_row.tensor, 0, [[0, 128], [1, 1024]])).then_inc(sgb.h, 16)
    sgb.bump(16)
    mk.y.dma_start(bt[:], bass.AP(b_row.tensor, 0, [[0, 128], [1, 1024]])).then_inc(sgb.h, 16)
    n = sgb.bump(16)
    mk.v.wait_ge(sgb.h, n)
    return (gt, bt)


def alloc_blk1(mk, es=None):
    W = dict(
        maskneg=mk.sbuf("masknegsb", [128, 128], BF16, es),
        onescol_r=mk.sbuf("ones_r", [1, 128], F32R, es),
        b1pc=mk.sbuf("b1pc", [128, 32], F32, es),
    )
    return {}, W


def queue_blk1_consts(mk, W, dr):
    W["dr"] = dr
    sgc = mk.sem("b1_c")
    mk.y.dma_start(W["maskneg"][:], dr["maskneg"]).then_inc(sgc.h, 16)
    sgc.bump(16)
    mk.y.dma_start(W["b1pc"][:], dr["b1_pc"]).then_inc(sgc.h, 16)
    sgc.bump(16)
    sone = mk.sem("b1_ones")
    mk.g.dma_start(W["onescol_r"][:], dr["ones_row"]).then_inc(sone.h, 16)
    sone.bump(16)
    W["n_bias"] = (sgc, sgc.n)
    mk.t.wait_ge(sgc.h, sgc.n)
    mk.t.wait_ge(sone.h, sone.n)


def build_l1ffn_prog(dbg=False, dbgseq=0):
    nc = bass.Bass("TRN2", target_bir_lowering=False)
    mk = MK(nc)
    x_in = nc.dram_tensor("x_in", [2, SEQ, D], F32, kind="ExternalInput")
    x_out = nc.dram_tensor("x_out", [2, SEQ, D], F32, kind="ExternalOutput")
    dr = {
        "wr_t": nc.dram_tensor("wr_t", [D, D], F32, kind="ExternalInput").ap(),
        "wk_t": nc.dram_tensor("wk_t", [D, D], F32, kind="ExternalInput").ap(),
        "wv_t": nc.dram_tensor("wv_t", [D, D], F32, kind="ExternalInput").ap(),
        "mu_r_pc": nc.dram_tensor("mu_r_pc", [128, 8], F32, kind="ExternalInput").ap(),
        "mu_k_pc": nc.dram_tensor("mu_k_pc", [128, 8], F32, kind="ExternalInput").ap(),
    }
    ident_d = nc.dram_tensor("ident", [128, 128], BF16, kind="ExternalInput")

    X = mk.sbuf("X", [128, TCH * 1024], F32)
    XnT = mk.sbuf("XnT", [128, DCH * CST], BF16)
    ident = mk.sbuf("identsb", [128, 128], BF16)
    lnb = alloc_ln(mk)
    lnb["ident"] = ident
    bufs, W = alloc_l1ffn(mk)
    W["dr"] = dr

    mk.init_consts()
    zcols = bass.AP(XnT, 0, [[XnT.shape[1], 128], [CST, DCH], [1, XOFF]])
    mk.v.memset(zcols, 0.0).then_inc(mk.sv.h, 1)
    mk.sv.bump()
    mk.y.dma_start(ident[:], ident_d.ap()).then_inc(mk.sd.h, 16)
    n_id = mk.sd.bump(16)
    mk.t.wait_ge(mk.sd.h, n_id)

    if dbg:
        dbg_xnt = nc.dram_tensor("dbg_xnt", [128, DCH * CST], BF16, kind="ExternalOutput")
        dbg_ka = nc.dram_tensor("dbg_ka", [128, DCH * SEQ], BF16, kind="ExternalOutput")
        dbg_x = nc.dram_tensor("dbg_x", [128, TCH * 1024], F32, kind="ExternalOutput")
    pes = {}
    n_prev_out = None
    xin_t = x_in.ap().rearrange("b (tc p) d -> b p tc d", p=128)
    xout_t = x_out.ap().rearrange("b (tc p) d -> b p tc d", p=128)
    for sq_ in range(2):
        if n_prev_out is not None:
            mk.y.wait_ge(mk.so.h, n_prev_out)
        sx = mk.sem("x_in")
        mk.y.dma_start(X.ap().rearrange("p (tc d) -> p tc d", d=1024),
                       xin_t[sq_]).then_inc(sx.h, 16)
        n_x = sx.bump(16)
        mk.v.wait_ge(sx.h, n_x)
        mk.s.wait_ge(sx.h, n_x)
        lnb["xnt_free"] = pes.get("xnt_free")
        if pes.get("psum_free") is not None:
            mk.t.wait_ge(*pes["psum_free"])
        with ExitStack() as lps:
            lnb["ptA"] = mk.psum("ptA", [128, 1024], BF16, lps)
            lnb["ptB"] = mk.psum("ptB", [128, 1024], BF16, lps)
            emit_ln_transpose(mk, X, XnT, lnb)
        if dbg and sq_ == dbgseq:
            mk.y.wait_ge(mk.sv.h, mk.sv.n)
            mk.y.dma_start(dbg_xnt.ap(), XnT[:, :]).then_inc(mk.so.h, 16)
            mk.so.bump(16)
            mk.y.dma_start(dbg_x.ap(), X[:, :]).then_inc(mk.so.h, 16)
            mk.so.bump(16)
        emit_l1ffn(mk, X, XnT, W, bufs, pes)
        if dbg and sq_ == dbgseq:
            mk.y.wait_ge(mk.sv.h, mk.sv.n)
            mk.y.dma_start(dbg_ka.ap(), bufs["kaT"][:, :]).then_inc(mk.so.h, 16)
            mk.so.bump(16)
        pes["xnt_free"] = (mk.sp, mk.sp.n)
        nv = mk.sv.n
        mk.y.wait_ge(mk.sv.h, nv)
        mk.y.dma_start(xout_t[sq_],
                       X.ap().rearrange("p (tc d) -> p tc d", d=1024)).then_inc(mk.so.h, 16)
        n_prev_out = mk.so.bump(16)
    mk.y.wait_ge(mk.so.h, n_prev_out)
    return mk.finish()


def l1ffn_inputs(x2, w_r, w_k, w_v, mu_r, mu_k, const):
    return dict(
        x_in=np.ascontiguousarray(x2, dtype=np.float32),
        wr_t=np.ascontiguousarray(np.asarray(w_r, np.float32).T),
        wk_t=np.ascontiguousarray(np.asarray(w_k, np.float32).T),
        wv_t=np.ascontiguousarray(np.asarray(w_v, np.float32).T),
        mu_r_pc=percol(mu_r),
        mu_k_pc=percol(mu_k),
        ident=const["ident"],
    )


def build_moe_prog(dbg=False):
    nc = bass.Bass("TRN2", target_bir_lowering=False)
    mk = MK(nc)
    x_in = nc.dram_tensor("x_in", [2, SEQ, D], F32, kind="ExternalInput")
    x_out = nc.dram_tensor("x_out", [2, SEQ, D], F32, kind="ExternalOutput")
    dr = {
        "wg_t": nc.dram_tensor("wg_t", [D, 16], F32, kind="ExternalInput").ap(),
        "w1": nc.dram_tensor("w1", [16, D, D], F32, kind="ExternalInput").ap(),
        "w2": nc.dram_tensor("w2", [16, 512, D], F32, kind="ExternalInput").ap(),
        "ws1": nc.dram_tensor("ws1", [D, 2048], F32, kind="ExternalInput").ap(),
        "ws2": nc.dram_tensor("ws2", [D, D], F32, kind="ExternalInput").ap(),
    }
    ident_d = nc.dram_tensor("ident", [128, 128], BF16, kind="ExternalInput")

    X = mk.sbuf("X", [128, TCH * 1024], F32)
    XnT = mk.sbuf("XnT", [128, DCH * CST], BF16)
    ident = mk.sbuf("identsb", [128, 128], BF16)
    lnb = alloc_ln(mk)
    lnb["ident"] = ident
    bufs, W = alloc_moe(mk)
    W["dr"] = dr
    queue_moe_router_weights(mk, W, dr)
    if dbg:
        dbg_cw = nc.dram_tensor("dbg_cw", [128, TCH * 16], F32, kind="ExternalOutput")
        dbg_st_d = nc.dram_tensor("dbg_st", [128, 88], F32, kind="ExternalOutput")
        bufs["dbg_st"] = mk.sbuf("dbgst", [128, 88], F32)

    mk.init_consts()
    zcols = bass.AP(XnT, 0, [[XnT.shape[1], 128], [CST, DCH], [1, XOFF]])
    mk.v.memset(zcols, 0.0).then_inc(mk.sv.h, 1)
    mk.sv.bump()
    mk.y.dma_start(ident[:], ident_d.ap()).then_inc(mk.sd.h, 16)
    n_id = mk.sd.bump(16)
    mk.t.wait_ge(mk.sd.h, n_id)

    pes = {}
    n_prev_out = None
    xin_t = x_in.ap().rearrange("b (tc p) d -> b p tc d", p=128)
    xout_t = x_out.ap().rearrange("b (tc p) d -> b p tc d", p=128)
    for sq_ in range(2):
        if n_prev_out is not None:
            mk.y.wait_ge(mk.so.h, n_prev_out)
        sx = mk.sem("x_in")
        mk.y.dma_start(X.ap().rearrange("p (tc d) -> p tc d", d=1024),
                       xin_t[sq_]).then_inc(sx.h, 16)
        n_x = sx.bump(16)
        mk.v.wait_ge(sx.h, n_x)
        mk.s.wait_ge(sx.h, n_x)
        lnb["xnt_free"] = pes.get("xnt_free")
        if pes.get("psum_free") is not None:
            mk.t.wait_ge(*pes["psum_free"])
        with ExitStack() as lps:
            lnb["ptA"] = mk.psum("ptA", [128, 1024], BF16, lps)
            lnb["ptB"] = mk.psum("ptB", [128, 1024], BF16, lps)
            nv_ln = emit_ln_transpose(mk, X, XnT, lnb)
        pes["psum_free"] = (mk.sv.h, nv_ln)
        emit_moe(mk, X, XnT, W, bufs, pes)
        pes["xnt_free"] = (mk.sp, mk.sp.n)
        if dbg and sq_ == 0:
            mk.y.wait_ge(mk.sv.h, mk.sv.n)
            mk.y.dma_start(dbg_cw.ap(), bufs["dbg_cw"][:, :]).then_inc(mk.so.h, 16)
            mk.so.bump(16)
            mk.y.dma_start(dbg_st_d.ap(), bufs["dbg_st"][:, :]).then_inc(mk.so.h, 16)
            mk.so.bump(16)
        nv = mk.sv.n
        mk.y.wait_ge(mk.sv.h, nv)
        mk.y.dma_start(xout_t[sq_],
                       X.ap().rearrange("p (tc d) -> p tc d", d=1024)).then_inc(mk.so.h, 16)
        n_prev_out = mk.so.bump(16)
    mk.y.wait_ge(mk.so.h, n_prev_out)
    return mk.finish()


def moe_inputs(x2, Wg, W1, W2, Ws1, Ws2, const):
    return dict(
        x_in=np.ascontiguousarray(x2, dtype=np.float32),
        wg_t=np.ascontiguousarray(np.asarray(Wg, np.float32).T),
        w1=np.ascontiguousarray(W1, dtype=np.float32),
        w2=np.ascontiguousarray(W2, dtype=np.float32),
        ws1=np.ascontiguousarray(Ws1, dtype=np.float32),
        ws2=np.ascontiguousarray(Ws2, dtype=np.float32),
        ident=const["ident"],
    )


def build_attn0_prog():
    nc = bass.Bass("TRN2", target_bir_lowering=False)
    mk = MK(nc)
    x_in = nc.dram_tensor("x_in", [2, SEQ, D], F32, kind="ExternalInput")
    x_out = nc.dram_tensor("x_out", [2, SEQ, D], F32, kind="ExternalOutput")
    dr = {
        "wq": nc.dram_tensor("wq", [D, D], F32, kind="ExternalInput").ap(),
        "wk0": nc.dram_tensor("wk0", [D, D], F32, kind="ExternalInput").ap(),
        "wv": nc.dram_tensor("wv", [D, 2048], F32, kind="ExternalInput").ap(),
        "wg0": nc.dram_tensor("wg0", [D, 2048], F32, kind="ExternalInput").ap(),
        "wo": nc.dram_tensor("wo", [2048, D], F32, kind="ExternalInput").ap(),
        "gnw_row": nc.dram_tensor("gnw_row", [1, 256], F32, kind="ExternalInput").ap(),
        "mask01": nc.dram_tensor("mask01", [128, 128], BF16, kind="ExternalInput").ap(),
        "cos_aq": nc.dram_tensor("cos_aq", [SEQ, 512], F32, kind="ExternalInput").ap(),
        "sin_aq": nc.dram_tensor("sin_aq", [SEQ, 512], F32, kind="ExternalInput").ap(),
        "cos_bk": nc.dram_tensor("cos_bk", [SEQ, 512], F32, kind="ExternalInput").ap(),
        "sin_bk": nc.dram_tensor("sin_bk", [SEQ, 512], F32, kind="ExternalInput").ap(),
    }
    ident_d = nc.dram_tensor("ident", [128, 128], BF16, kind="ExternalInput")

    X = mk.sbuf("X", [128, TCH * 1024], F32)
    XnT = mk.sbuf("XnT", [128, DCH * CST], BF16)
    ident = mk.sbuf("identsb", [128, 128], BF16)
    lnb = alloc_ln(mk)
    lnb["ident"] = ident
    bufs, W = alloc_attn0(mk)
    bufs["lnsq"] = lnb["sq"]
    W["ident"] = ident
    import os as _os
    if _os.environ.get("A0DBG"):
        dbgO_d = nc.dram_tensor("dbgO", [128, 272], BF16, kind="ExternalOutput")
        W["dbgO_d"] = dbgO_d
    queue_attn0_consts(mk, W, dr)

    mk.init_consts()
    zcols = bass.AP(XnT, 0, [[XnT.shape[1], 128], [CST, DCH], [1, XOFF]])
    mk.v.memset(zcols, 0.0).then_inc(mk.sv.h, 1)
    mk.sv.bump()
    mk.y.dma_start(ident[:], ident_d.ap()).then_inc(mk.sd.h, 16)
    n_id = mk.sd.bump(16)
    mk.t.wait_ge(mk.sd.h, mk.sd.n)
    mk.s.wait_ge(mk.sd.h, mk.sd.n)
    mk.v.wait_ge(mk.sd.h, mk.sd.n)

    pes = {}
    n_prev_out = None
    xin_t = x_in.ap().rearrange("b (tc p) d -> b p tc d", p=128)
    xout_t = x_out.ap().rearrange("b (tc p) d -> b p tc d", p=128)
    for sq_ in range(2):
        if n_prev_out is not None:
            mk.y.wait_ge(mk.so.h, n_prev_out)
        sx = mk.sem("x_in")
        mk.y.dma_start(X.ap().rearrange("p (tc d) -> p tc d", d=1024),
                       xin_t[sq_]).then_inc(sx.h, 16)
        n_x = sx.bump(16)
        mk.v.wait_ge(sx.h, n_x)
        mk.s.wait_ge(sx.h, n_x)
        lnb["xnt_free"] = pes.get("xnt_free")
        if pes.get("psum_free") is not None:
            mk.t.wait_ge(*pes["psum_free"])
        with ExitStack() as lps:
            lnb["ptA"] = mk.psum("ptA", [128, 1024], BF16, lps)
            lnb["ptB"] = mk.psum("ptB", [128, 1024], BF16, lps)
            nv_ln = emit_ln_transpose(mk, X, XnT, lnb)
        pes["psum_free"] = (mk.sv.h, nv_ln)
        emit_attn0(mk, X, XnT, W, bufs, pes)
        pes["xnt_free"] = (mk.sp, mk.sp.n)
        nv = mk.sv.n
        mk.y.wait_ge(mk.sv.h, nv)
        mk.y.dma_start(xout_t[sq_],
                       X.ap().rearrange("p (tc d) -> p tc d", d=1024)).then_inc(mk.so.h, 16)
        n_prev_out = mk.so.bump(16)

    mk.y.wait_ge(mk.so.h, n_prev_out)
    return mk.finish()


def attn0_inputs(x2, Wq, Wk, Wv, Wg, Wo, gnw, const):
    inv = 1.0 / (10000.0 ** (np.arange(0, 128, 2, dtype=np.float64) / 128.0))
    ang = np.arange(SEQ, dtype=np.float64)[:, None] * inv[None, :]
    h = np.arange(8)
    s_h = np.log2(1.0 - 2.0 ** (-5.0 - h))
    tt = np.arange(SEQ, dtype=np.float64)
    aq = 2.0 ** (tt[:, None] * s_h[None, :]) / np.sqrt(128.0)
    bk = 2.0 ** (-tt[:, None] * s_h[None, :])
    cos = np.cos(ang)[:, None, :]
    sin = np.sin(ang)[:, None, :]
    mk_tab = lambda trig, sc: np.ascontiguousarray(
        (trig * sc[:, :, None]).reshape(SEQ, 512).astype(np.float32))
    return dict(
        x_in=np.ascontiguousarray(x2, dtype=np.float32),
        wq=np.ascontiguousarray(Wq, np.float32),
        wk0=np.ascontiguousarray(Wk, np.float32),
        wv=np.ascontiguousarray(Wv, np.float32),
        wg0=np.ascontiguousarray(Wg, np.float32),
        wo=np.ascontiguousarray(Wo, np.float32),
        gnw_row=row(gnw),
        mask01=const["mask01"],
        cos_aq=mk_tab(cos, aq), sin_aq=mk_tab(sin, aq),
        cos_bk=mk_tab(cos, bk), sin_bk=mk_tab(sin, bk),
        ident=const["ident"],
    )


def build_blk1_prog():
    nc = bass.Bass("TRN2", target_bir_lowering=False)
    mk = MK(nc)
    x_in = nc.dram_tensor("x_in", [2, SEQ, D], F32, kind="ExternalInput")
    x_out = nc.dram_tensor("x_out", [2, SEQ, D], F32, kind="ExternalOutput")
    dr = {
        "wqkv": nc.dram_tensor("wqkv", [D, 3072], F32, kind="ExternalInput").ap(),
        "wo1": nc.dram_tensor("wo1", [D, D], F32, kind="ExternalInput").ap(),
        "w1l": nc.dram_tensor("w1l", [D, 4096], F32, kind="ExternalInput").ap(),
        "w2l": nc.dram_tensor("w2l", [4096, D], F32, kind="ExternalInput").ap(),
        "ln1_g_row": nc.dram_tensor("ln1_g_row", [1, D], F32, kind="ExternalInput").ap(),
        "ln1_b_row": nc.dram_tensor("ln1_b_row", [1, D], F32, kind="ExternalInput").ap(),
        "ln2_g_row": nc.dram_tensor("ln2_g_row", [1, D], F32, kind="ExternalInput").ap(),
        "ln2_b_row": nc.dram_tensor("ln2_b_row", [1, D], F32, kind="ExternalInput").ap(),
        "bias_row": nc.dram_tensor("bias_row", [1, 3072 + 2048], F32, kind="ExternalInput").ap(),
        "b1_pc": nc.dram_tensor("b1_pc", [128, 32], F32, kind="ExternalInput").ap(),
        "maskneg": nc.dram_tensor("maskneg", [128, 128], BF16, kind="ExternalInput").ap(),
        "tabs1": nc.dram_tensor("tabs1", [SEQ, 1024], F32, kind="ExternalInput").ap(),
        "ones_row": nc.dram_tensor("ones_row", [1, 128], F32, kind="ExternalInput").ap(),
    }
    ident_d = nc.dram_tensor("ident", [128, 128], BF16, kind="ExternalInput")

    X = mk.sbuf("X", [128, TCH * 1024], F32)
    XnT = mk.sbuf("XnT", [128, DCH * CST], BF16)
    ident = mk.sbuf("identsb", [128, 128], BF16)
    lnb = alloc_ln(mk)
    lnb["ident"] = ident
    bufs, W = alloc_blk1(mk)
    bufs["lnsq"] = lnb["sq"]
    W["ident"] = ident
    import os as _os
    if _os.environ.get("B1DBG"):
        W["dbg_x1"] = nc.dram_tensor("dbg_x1", [128, TCH * 1024], F32, kind="ExternalOutput")
        W["dbg_hT"] = nc.dram_tensor("dbg_hT", [128, DCH * SEQ], BF16, kind="ExternalOutput")
        W["dbg_y"] = nc.dram_tensor("dbg_y", [128, TCH * 1024], F32, kind="ExternalOutput")
    queue_blk1_consts(mk, W, dr)

    mk.init_consts()
    zcols = bass.AP(XnT, 0, [[XnT.shape[1], 128], [CST, DCH], [1, XOFF]])
    mk.v.memset(zcols, 0.0).then_inc(mk.sv.h, 1)
    mk.sv.bump()
    mk.y.dma_start(ident[:], ident_d.ap()).then_inc(mk.sd.h, 16)
    n_id = mk.sd.bump(16)
    mk.t.wait_ge(mk.sd.h, mk.sd.n)
    mk.s.wait_ge(mk.sd.h, mk.sd.n)
    mk.v.wait_ge(mk.sd.h, mk.sd.n)

    pes = {}
    n_prev_out = None
    xin_t = x_in.ap().rearrange("b (tc p) d -> b p tc d", p=128)
    xout_t = x_out.ap().rearrange("b (tc p) d -> b p tc d", p=128)
    for sq_ in range(2):
        if n_prev_out is not None:
            mk.y.wait_ge(mk.so.h, n_prev_out)
        sx = mk.sem("x_in")
        mk.y.dma_start(X.ap().rearrange("p (tc d) -> p tc d", d=1024),
                       xin_t[sq_]).then_inc(sx.h, 16)
        n_x = sx.bump(16)
        mk.v.wait_ge(sx.h, n_x)
        mk.s.wait_ge(sx.h, n_x)
        lnb["xnt_free"] = pes.get("xnt_free")
        if pes.get("psum_free") is not None:
            mk.t.wait_ge(*pes["psum_free"])
        W["n_gb_users"] = pes.get("gb_users")
        nv_pl = emit_ln_inplace(mk, X, lnb)
        if W.get("dbg_y") is not None:
            mk.y.wait_ge(mk.sv.h, nv_pl)
            mk.y.dma_start(W["dbg_y"].ap(), X[:, :]).then_inc(mk.so.h, 16)
            mk.so.bump(16)
            mk.y.wait_ge(mk.so.h, mk.so.n)
        with ExitStack() as lps:
            gb1 = load_gamma_beta(mk, W, dr["ln1_g_row"], dr["ln1_b_row"], lps)
            lnb["x_ready"] = (mk.sv.h, nv_pl)
            lnb["gscr"] = mk.sbuf("gscr", [128, 1024], F32, lps)
            lnb["ptA"] = mk.psum("ptA", [128, 1024], BF16, lps)
            lnb["ptB"] = mk.psum("ptB", [128, 1024], BF16, lps)
            nv_ln = emit_ln_transpose(mk, X, XnT, lnb, gamma_beta=gb1)
            lnb.pop("gscr")
        pes["psum_free"] = (mk.sv.h, nv_ln)
        emit_blk1(mk, X, XnT, W, bufs, pes, lnb)
        emit_add_from_dram(mk, X, xin_t[sq_], pes)
        pes["xnt_free"] = (mk.sp, mk.sp.n)
        pes["gb_users"] = mk.sv.n
        nv = mk.sv.n
        mk.y.wait_ge(mk.sv.h, nv)
        mk.y.dma_start(xout_t[sq_],
                       X.ap().rearrange("p (tc d) -> p tc d", d=1024)).then_inc(mk.so.h, 16)
        n_prev_out = mk.so.bump(16)
    mk.y.wait_ge(mk.so.h, n_prev_out)
    return mk.finish()


def blk1_inputs(x2, Wqkv, bqkv, Wo, bo, ln1g, ln1b, ln2g, ln2b, W1, b1, W2, b2, const):
    inv = 1.0 / (10000.0 ** (np.arange(0, 64, 2, dtype=np.float64) / 64.0))
    ang = np.arange(SEQ, dtype=np.float64)[:, None] * inv[None, :]
    cos, sin = np.cos(ang), np.sin(ang)
    sc = 1.0 / 8.0
    tabs1 = np.concatenate([np.tile(cos * sc, (1, 8)), np.tile(sin * sc, (1, 8)),
                            np.tile(cos, (1, 8)), np.tile(sin, (1, 8))], axis=1)
    return dict(
        x_in=np.ascontiguousarray(x2, dtype=np.float32),
        wqkv=np.ascontiguousarray(Wqkv, np.float32),
        wo1=np.ascontiguousarray(Wo, np.float32),
        w1l=np.ascontiguousarray(W1, np.float32),
        w2l=np.ascontiguousarray(W2, np.float32),
        ln1_g_row=row(ln1g), ln1_b_row=row(ln1b),
        ln2_g_row=row(ln2g), ln2_b_row=row(ln2b),
        bias_row=np.ascontiguousarray(
            np.concatenate([bqkv, bo, b2]).reshape(1, -1).astype(np.float32)),
        b1_pc=percol(b1),
        maskneg=const["maskneg"],
        tabs1=tabs1.astype(np.float32),
        ones_row=np.ones((1, 128), dtype=np.float32),
        ident=const["ident"],
    )


def build_fused_prog(dbg=False, dbgseq=0):
    nc = bass.Bass("TRN2", target_bir_lowering=False)
    mk = MK(nc)
    x_in = nc.dram_tensor("x_in", [2, SEQ, D], F32, kind="ExternalInput")
    x_out = nc.dram_tensor("x_out", [2, SEQ, D], F32, kind="ExternalOutput")
    xspill = nc.dram_tensor("xspill", [128, TCH * 1024], F32, kind="Internal")
    dbg_ts = [nc.dram_tensor("dbg_s%d" % i, [128, TCH * 1024], F32,
                             kind="ExternalOutput") for i in range(6)] if dbg else None
    if dbg:
        dbg_xnt = nc.dram_tensor("dbg_xnt", [128, DCH * CST], BF16,
                                 kind="ExternalOutput")
        dbg_cw = nc.dram_tensor("dbg_cw", [128, TCH * 16], F32,
                                kind="ExternalOutput")
    dr = {}
    for nm, shape in (
        ("wg_t", [D, 16]), ("w1", [16, D, D]), ("w2", [16, 512, D]),
        ("ws1", [D, 2048]), ("ws2", [D, D]),
        ("wq", [D, D]), ("wk0", [D, D]), ("wv", [D, 2048]), ("wg0", [D, 2048]),
        ("wo", [2048, D]), ("gnw_row", [1, 256]),
        ("cos_aq", [SEQ, 512]), ("sin_aq", [SEQ, 512]),
        ("cos_bk", [SEQ, 512]), ("sin_bk", [SEQ, 512]),
        ("wr_t", [D, D]), ("wk_t", [D, D]), ("wv_t", [D, D]),
        ("mu_r_pc", [128, 8]), ("mu_k_pc", [128, 8]),
        ("wqkv", [D, 3072]), ("wo1", [D, D]), ("w1l", [D, 4096]), ("w2l", [4096, D]),
        ("ln1_g_row", [1, D]), ("ln1_b_row", [1, D]),
        ("ln2_g_row", [1, D]), ("ln2_b_row", [1, D]),
        ("bias_row", [1, 5120]), ("b1_pc", [128, 32]),
        ("tabs1", [SEQ, 1024]), ("ones_row", [1, 128]),
    ):
        dr[nm] = nc.dram_tensor(nm, shape, F32, kind="ExternalInput").ap()
    for nm, shape in (("mask01", [128, 128]), ("maskneg", [128, 128]),
                      ("identd", [128, 128])):
        dr[nm] = nc.dram_tensor(nm, shape, BF16, kind="ExternalInput").ap()

    X = mk.sbuf("X", [128, TCH * 1024], F32)
    XnT = mk.sbuf("XnT", [128, DCH * CST], BF16)
    ident = mk.sbuf("identsb", [128, 128], BF16)
    lnb = alloc_ln(mk)
    lnb["ident"] = ident
    moe_bufs, moe_W = alloc_moe(mk)
    moe_W["dr"] = dr
    a0_bufs, a0_W = alloc_attn0(mk)
    a0_bufs["lnsq"] = lnb["sq"]
    a0_W["ident"] = ident
    l1_bufs, l1_W = alloc_l1ffn(mk)
    b1_bufs, b1_W = alloc_blk1(mk)
    b1_bufs["lnsq"] = lnb["sq"]
    b1_W["ident"] = ident

    queue_moe_router_weights(mk, moe_W, dr)
    queue_attn0_consts(mk, a0_W, dict(dr, gnw_row=dr["gnw_row"]))
    queue_blk1_consts(mk, b1_W, dict(dr, b1_pc=dr["b1_pc"], maskneg=dr["maskneg"]))
    mk.init_consts()
    zcols = bass.AP(XnT, 0, [[XnT.shape[1], 128], [CST, DCH], [1, XOFF]])
    mk.v.memset(zcols, 0.0).then_inc(mk.sv.h, 1)
    mk.sv.bump()
    mk.y.dma_start(ident[:], dr["identd"]).then_inc(mk.sd.h, 16)
    mk.sd.bump(16)
    mk.t.wait_ge(mk.sd.h, mk.sd.n)
    mk.s.wait_ge(mk.sd.h, mk.sd.n)
    mk.v.wait_ge(mk.sd.h, mk.sd.n)

    pes = {}
    n_prev_out = None
    xin_t = x_in.ap().rearrange("b (tc p) d -> b p tc d", p=128)
    xout_t = x_out.ap().rearrange("b (tc p) d -> b p tc d", p=128)
    xsp_t = xspill.ap().rearrange("p (tc d) -> p tc d", d=1024)

    def run_ln(x_ready, gamma_beta=None, gb_rows=None, W=None, use_x_tile=True):
        xt_ = pes.pop("x_tile", None)
        with ExitStack() as lps:
            gb = None
            if gb_rows is not None:
                gb = load_gamma_beta(mk, W, gb_rows[0], gb_rows[1], lps)
                lnb["gscr"] = mk.sbuf("gscr", [128, 1024], F32, lps)
            lnb["ptA"] = mk.psum("ptA", [128, 1024], BF16, lps)
            lnb["ptB"] = mk.psum("ptB", [128, 1024], BF16, lps)
            lnb["xnt_free"] = pes.get("xnt_free")
            lnb["x_ready"] = x_ready
            if use_x_tile and xt_ is not None:
                lnb["x_tile"] = xt_
            if pes.get("psum_free") is not None:
                mk.t.wait_ge(*pes["psum_free"])
            nv_ln = emit_ln_transpose(mk, X, XnT, lnb, gamma_beta=gb)
            lnb.pop("gscr", None)
        pes["psum_free"] = (mk.sv.h, nv_ln)
        return nv_ln

    def dma_gate(n=None):
        nn = mk.sv.n if n is None else n
        mk.g.wait_ge(mk.sv.h, nn)
        mk.y.wait_ge(mk.sv.h, nn)
        pes["sbuf_free"] = (mk.sv.h, nn)

    cur_seq = [0]

    def tap(i):
        if dbg and cur_seq[0] == dbgseq:
            mk.y.wait_ge(mk.sv.h, mk.sv.n)
            mk.y.dma_start(dbg_ts[i].ap(), X[:, :]).then_inc(mk.so.h, 16)
            mk.so.bump(16)

    for sq_ in range(2):
        cur_seq[0] = sq_
        if n_prev_out is not None:
            mk.y.wait_ge(mk.so.h, n_prev_out)
        sx = mk.sem("x_in")
        sxg = mk.sem("x_in_g")
        xv = X.ap().rearrange("p (tc d) -> p tc d", d=1024)
        mk.y.dma_start(xv[:, 0:8, :], xin_t[sq_][:, 0:8, :]).then_inc(sx.h, 16)
        n_x = sx.bump(16)
        if n_prev_out is not None:
            mk.g.wait_ge(mk.so.h, n_prev_out)
        mk.g.dma_start(xv[:, 8:16, :], xin_t[sq_][:, 8:16, :]).then_inc(sxg.h, 16)
        n_xg = sxg.bump(16)
        mk.s.wait_ge(sxg.h, n_xg)
        mk.v.wait_ge(sxg.h, n_xg)
        xr = (sx.h, n_x)
        run_ln(xr, use_x_tile=False)
        _mark(mk, 'ln_moe1/%d' % sq_)
        if dbg and sq_ == dbgseq:
            mk.y.wait_ge(mk.sv.h, mk.sv.n)
            mk.y.dma_start(dbg_xnt.ap(), XnT[:, :]).then_inc(mk.so.h, 16)
            mk.so.bump(16)
        dma_gate()
        nv = emit_moe(mk, X, XnT, moe_W, moe_bufs, pes)
        _mark(mk, 'moe1/%d' % sq_)
        pes["xnt_free"] = (mk.sp, mk.sp.n)
        if dbg and sq_ == dbgseq:
            mk.y.wait_ge(mk.sv.h, mk.sv.n)
            mk.y.dma_start(dbg_cw.ap(), moe_bufs["dbg_cw"][:, :]).then_inc(mk.so.h, 16)
            mk.so.bump(16)
        tap(0)
        run_ln((mk.sv.h, nv))
        _mark(mk, 'ln_attn0/%d' % sq_)
        dma_gate(nv)
        nv = emit_attn0(mk, X, XnT, a0_W, a0_bufs, pes)
        _mark(mk, 'attn0/%d' % sq_)
        pes["xnt_free"] = (mk.sp, mk.sp.n)
        tap(1)
        run_ln((mk.sv.h, nv))
        _mark(mk, 'ln_moe2/%d' % sq_)
        dma_gate(nv)
        nv = emit_moe(mk, X, XnT, moe_W, moe_bufs, pes)
        _mark(mk, 'moe2/%d' % sq_)
        pes["xnt_free"] = (mk.sp, mk.sp.n)
        tap(2)
        run_ln((mk.sv.h, nv))
        _mark(mk, 'ln_l1f1/%d' % sq_)
        dma_gate(nv)
        l1_W["dr"] = dr
        nv = emit_l1ffn(mk, X, XnT, l1_W, l1_bufs, pes)
        _mark(mk, 'l1ffn1/%d' % sq_)
        pes["xnt_free"] = (mk.sp, mk.sp.n)
        tap(3)
        sxp = mk.sem("xspill")
        xtl_sp = pes.pop("x_tile", None)
        if xtl_sp is not None:
            for tc in range(TCH):
                mk.y.wait_ge(mk.sv.h, xtl_sp[tc])
                mk.y.dma_start(xsp_t[:, tc, :],
                               X[:, tc * 1024:(tc + 1) * 1024]).then_inc(sxp.h, 16)
                sxp.bump(16)
            lnb["x_tile"] = xtl_sp
        else:
            mk.y.wait_ge(mk.sv.h, nv)
            mk.y.dma_start(xspill.ap(), X[:, :]).then_inc(sxp.h, 16)
            sxp.bump(16)
            mk.s.wait_ge(sxp.h, sxp.n)
        n_sp = sxp.n
        mk.v.wait_ge(sxp.h, n_sp)
        nv_pl = emit_ln_inplace(mk, X, lnb, x_ready=(mk.sv.h, nv))
        pes["x_tile"] = lnb.pop("out_tiles")
        _mark(mk, 'ln_inpl/%d' % sq_)
        dma_gate()
        b1_W["n_gb_users"] = pes.get("gb_users")
        b1_W["dr"] = dr
        run_ln((mk.sv.h, nv_pl), gb_rows=(dr["ln1_g_row"], dr["ln1_b_row"]), W=b1_W)
        _mark(mk, 'ln_gb1/%d' % sq_)
        dma_gate()
        emit_blk1(mk, X, XnT, b1_W, b1_bufs, pes, lnb)
        _mark(mk, 'blk1/%d' % sq_)
        dma_gate()
        nv = emit_add_from_dram(mk, X, xsp_t, pes)
        pes["xnt_free"] = (mk.sp, mk.sp.n)
        pes["gb_users"] = mk.sv.n
        _mark(mk, 'addx/%d' % sq_)
        tap(4)
        run_ln((mk.sv.h, nv))
        _mark(mk, 'ln_l1f2/%d' % sq_)
        dma_gate(nv)
        nv = emit_l1ffn(mk, X, XnT, l1_W, l1_bufs, pes)
        _mark(mk, 'l1ffn2/%d' % sq_)
        pes["xnt_free"] = (mk.sp, mk.sp.n)
        tap(5)
        mk.y.wait_ge(mk.sv.h, nv)
        mk.y.dma_start(xout_t[sq_],
                       X.ap().rearrange("p (tc d) -> p tc d", d=1024)).then_inc(mk.so.h, 16)
        n_prev_out = mk.so.bump(16)
    mk.y.wait_ge(mk.so.h, n_prev_out)
    return mk.finish()


_PROG_CACHE = {}


def _get_prog(name, builder):
    if name not in _PROG_CACHE:
        _PROG_CACHE[name] = builder()
    return _PROG_CACHE[name]


def _l0attn_tables():
    inv = 1.0 / (10000.0 ** (np.arange(0, 128, 2, dtype=np.float64) / 128.0))
    ang = np.arange(SEQ, dtype=np.float64)[:, None] * inv[None, :]
    h = np.arange(8)
    s_h = np.log2(1.0 - 2.0 ** (-5.0 - h))
    tt = np.arange(SEQ, dtype=np.float64)
    aq = 2.0 ** (tt[:, None] * s_h[None, :]) / np.sqrt(128.0)
    bk = 2.0 ** (-tt[:, None] * s_h[None, :])
    cos = np.cos(ang)[:, None, :]
    sin = np.sin(ang)[:, None, :]
    mk_tab = lambda trig, sc: np.ascontiguousarray(
        (trig * sc[:, :, None]).reshape(SEQ, 512).astype(np.float32))
    return dict(cos_aq=mk_tab(cos, aq), sin_aq=mk_tab(sin, aq),
                cos_bk=mk_tab(cos, bk), sin_bk=mk_tab(sin, bk))


def _blk1_tables():
    inv = 1.0 / (10000.0 ** (np.arange(0, 64, 2, dtype=np.float64) / 64.0))
    ang = np.arange(SEQ, dtype=np.float64)[:, None] * inv[None, :]
    cos, sin = np.cos(ang), np.sin(ang)
    sc = 1.0 / 8.0
    return np.concatenate([np.tile(cos * sc, (1, 8)), np.tile(sin * sc, (1, 8)),
                           np.tile(cos, (1, 8)), np.tile(sin, (1, 8))],
                          axis=1).astype(np.float32)


def _build_wmap(inputs):
    const = host_constants()
    asnp = lambda k: np.ascontiguousarray(np.asarray(inputs[k], np.float32))

    return dict(
        wg_t=np.ascontiguousarray(asnp("L0_ffn_Wg").T),
        w1=asnp("L0_ffn_W1"), w2=asnp("L0_ffn_W2"),
        ws1=asnp("L0_ffn_Ws1"), ws2=asnp("L0_ffn_Ws2"),
        wq=asnp("L0_attn_Wq"), wk0=asnp("L0_attn_Wk"),
        wv=asnp("L0_attn_Wv"), wg0=asnp("L0_attn_Wg"), wo=asnp("L0_attn_Wo"),
        gnw_row=row(asnp("L0_attn_g_norm_w")),
        mask01=const["mask01"],
        wr_t=np.ascontiguousarray(asnp("L1_ffn_W_r").T),
        wk_t=np.ascontiguousarray(asnp("L1_ffn_W_k").T),
        wv_t=np.ascontiguousarray(asnp("L1_ffn_W_v").T),
        mu_r_pc=percol(asnp("L1_ffn_mu_r")),
        mu_k_pc=percol(asnp("L1_ffn_mu_k")),
        wqkv=asnp("L1_attn_Wqkv"), wo1=asnp("L1_attn_Wo"),
        w1l=asnp("L1_attn_W1"), w2l=asnp("L1_attn_W2"),
        ln1_g_row=row(asnp("L1_attn_ln1_g")), ln1_b_row=row(asnp("L1_attn_ln1_b")),
        ln2_g_row=row(asnp("L1_attn_ln2_g")), ln2_b_row=row(asnp("L1_attn_ln2_b")),
        bias_row=np.ascontiguousarray(np.concatenate(
            [asnp("L1_attn_bqkv"), asnp("L1_attn_bo"), asnp("L1_attn_b2")]
        ).reshape(1, -1)),
        b1_pc=percol(asnp("L1_attn_b1")),
        maskneg=const["maskneg"], tabs1=_blk1_tables(),
        ones_row=np.ones((1, 128), dtype=np.float32),
        identd=const["ident"],
        **_l0attn_tables(),
    )


def kernel(**inputs):
    x = np.ascontiguousarray(np.asarray(inputs["L0_attn_hidden_states"], np.float32))
    B = x.shape[0]
    per = B // NCORES
    wmap = _build_wmap(inputs)
    nc = _get_prog("fused", build_fused_prog)
    in_maps = []
    for c in range(NCORES):
        m = dict(wmap)
        m["x_in"] = np.ascontiguousarray(x[c * per:(c + 1) * per])
        in_maps.append(m)
    res = run_bass_kernel_spmd(nc, in_maps, core_ids=list(range(NCORES)))
    return np.concatenate([r["x_out"] for r in res.results], axis=0)
